# version 04
# baseline (speedup 1.0000x reference)
import math
from contextlib import ExitStack

import numpy as np
import concourse.bass as bass
import concourse.mybir as mybir
from concourse.bass import IndirectOffsetOnAxis
from concourse.bass_utils import run_bass_kernel_spmd

F32 = mybir.dt.float32
BF16 = mybir.dt.bfloat16
I32 = mybir.dt.int32
U32 = mybir.dt.uint32
AF = mybir.ActivationFunctionType
ALU = mybir.AluOpType
AX = mybir.AxisListType

NCORES = 8
B, T, D = 2, 8192, 1024
NTOK = B * T
TPC = NTOK // NCORES
TILES_PC = TPC // 128
LNEPS = 1e-5

ENGINES = ("sync", "tensor", "vector", "scalar", "gpsimd")



class Sched:

    def __init__(self, p):
        self.p = p
        self.ops = {e: [] for e in ENGINES}
        self.events = {}

    def add(self, engine, fn, deps=(), event=None, dma=False):
        idx = len(self.ops[engine])
        deps = [d for d in deps if d is not None]
        for d in deps:
            assert d in self.events, f"dep {d} not yet defined (op {engine}:{idx})"
        self.ops[engine].append(dict(fn=fn, deps=deps, event=event, dma=dma))
        if event is not None:
            assert event not in self.events, f"dup event {event}"
            self.events[event] = (engine, idx)
        return event

    def emit(self, block):
        p = self.p
        csem = {e: p.sem(f"c_{e}") for e in ENGINES}
        dsem = {e: p.sem(f"d_{e}") for e in ENGINES}
        referenced = set()
        for e in ENGINES:
            for op in self.ops[e]:
                referenced.update(op["deps"])
        val_after = {}
        for e in ENGINES:
            c = 0
            dc = 0
            for op in self.ops[e]:
                if op["dma"]:
                    dc += 16
                    op["inc"] = ("d", dc)
                elif op["event"] is not None and op["event"] in referenced:
                    c += 1
                    op["inc"] = ("c", c)
                else:
                    op["inc"] = None
                if op["event"] is not None:
                    val_after[op["event"]] = (e, "d", dc) if op["dma"] else (e, "c", c)
        self._val_after = val_after
        self._csem, self._dsem = csem, dsem

        def make_section(eng_name):
            def body(eng):
                for op in self.ops[eng_name]:
                    waits = {}
                    for dname in op["deps"]:
                        pe, kind, v = val_after[dname]
                        key = (pe, kind)
                        waits[key] = max(waits.get(key, 0), v)
                    for (pe, kind), v in sorted(waits.items()):
                        eng.wait_ge(csem[pe] if kind == "c" else dsem[pe], v)
                    ins = op["fn"](eng)
                    if op["inc"] is not None:
                        kind, v = op["inc"]
                        ins.then_inc(csem[eng_name] if kind == "c" else dsem[eng_name],
                                     16 if kind == "d" else 1)
            return body

        for name in ENGINES:
            if self.ops[name]:
                getattr(block, name)(make_section(name))

    def finish(self, *final_events):
        def fn(g):
            return g.engine_nop()
        self.add("gpsimd", fn, deps=list(final_events), event="_done_" + str(len(self.events)))

    def phase_fence(self, tag, extra=()):
        deps = list(extra)
        for e in ENGINES:
            lastd = lastc = None
            for i in range(len(self.ops[e]) - 1, -1, -1):
                op = self.ops[e][i]
                if op["dma"] and lastd is None:
                    lastd = i
                if (not op["dma"]) and lastc is None:
                    lastc = i
                if lastd is not None and lastc is not None:
                    break
            for i in (lastd, lastc):
                if i is None:
                    continue
                op = self.ops[e][i]
                if op["event"] is None:
                    op["event"] = f"_f_{tag}_{e}_{i}"
                    self.events[op["event"]] = (e, i)
                deps.append(op["event"])
        self.add("gpsimd", lambda g: g.engine_nop(), deps=deps, event=f"fence_{tag}")
        return f"fence_{tag}"


class PSched:

    def __init__(self, s, pfx, barrier_ev=None):
        self.s = s
        self.pfx = pfx
        self.barrier = barrier_ev
        self._seen = set()
        self.events = _PrefixedEvents(s, pfx)

    def add(self, engine, fn, deps=(), event=None, dma=False):
        deps = [self.pfx + d for d in deps if d is not None]
        if self.barrier and engine not in self._seen:
            self._seen.add(engine)
            deps.append(self.barrier)
        return self.s.add(engine, fn, deps, (self.pfx + event) if event else None, dma)

    def finish(self, *evs):
        pass

    def emit(self, block):
        pass

    def fence(self, tag):
        return self.s.phase_fence(tag)


class _PrefixedEvents:
    def __init__(self, s, pfx):
        self.s = s
        self.pfx = pfx

    def __contains__(self, name):
        return (self.pfx + name) in self.s.events


class PProg:

    def __init__(self, p, pfx):
        self.p = p
        self.pfx = pfx
        self.nc = p.nc
        self.es = ExitStack()

    def sb(self, name, shape, dtype):
        return self.es.enter_context(self.nc.sbuf_tensor(self.pfx + name, shape, dtype))

    def ps(self, name, shape, dtype):
        return self.es.enter_context(self.nc.psum_tensor(self.pfx + name, shape, dtype))

    def close(self):
        self.es.close()


class Prog:
    def __init__(self):
        self.nc = bass.Bass("TRN2", target_bir_lowering=False, debug=False)
        self.es = ExitStack()

    def sb(self, name, shape, dtype):
        return self.es.enter_context(self.nc.sbuf_tensor(name, shape, dtype))

    def ps(self, name, shape, dtype):
        return self.es.enter_context(self.nc.psum_tensor(name, shape, dtype))

    def sem(self, name):
        return self.es.enter_context(self.nc.semaphore(name))


def emit_setup_ident(s, identf, ident, identF=None):
    s.add("gpsimd", lambda g: g.iota(identf[:], pattern=[[1, 128]], base=0,
                                     channel_multiplier=-1,
                                     allow_small_or_imprecise_dtypes=True),
          event="identf")
    s.add("vector", lambda v: v.tensor_single_scalar(ident[:], identf[:], 0.0, ALU.is_equal),
          deps=["identf"], event="ident")
    if identF is not None:
        s.add("vector", lambda v: v.tensor_single_scalar(identF[:], identf[:], 0.0, ALU.is_equal),
              deps=["identf"], event="identF")


def emit_rows_to_cols(s, tag, row_sb, r, k, out_sb, identF, pbank, dep_load):
    for kc in range(k):
        s.add("tensor", lambda t, kc=kc: t.transpose(
            pbank[:, 0:r], row_sb[:r, kc * 128:(kc + 1) * 128], identF[:r, :r]),
            deps=[dep_load, "identF"] + ([f"_c_{tag}_{kc-1}"] if kc >= 1 else []),
            event=f"_ct_{tag}_{kc}")
        s.add("vector", lambda v, kc=kc: v.tensor_copy(out_sb[:, kc, 0:r], pbank[:, 0:r]),
            deps=[f"_ct_{tag}_{kc}"],
            event=(f"_c_{tag}_{kc}" if kc < k - 1 else f"cols_{tag}"))


def emit_layernorm(s, tag, src_ap, out_ap, sc, deps, store_fn=None, store_dep_event=None):
    s.add("vector", lambda v: v.tensor_reduce(sc["m"][:], src_ap, axis=AX.X, op=ALU.add),
          deps=deps, event=f"_lnsum_{tag}")
    s.add("vector", lambda v: v.tensor_mul(sc["lnsq"][:], src_ap, src_ap),
          deps=deps, event=f"_lnsq_{tag}")
    s.add("vector", lambda v: v.tensor_reduce(sc["s2"][:], sc["lnsq"][:], axis=AX.X, op=ALU.add),
          deps=[f"_lnsq_{tag}"], event=f"_lns2_{tag}")
    s.add("vector", lambda v: v.tensor_scalar(sc["mu"][:], sc["m"][:], 1.0 / D, None, ALU.mult),
          deps=[f"_lnsum_{tag}"], event=f"_lnmu_{tag}")
    s.add("vector", lambda v: v.tensor_mul(sc["tmp"][:], sc["mu"][:], sc["mu"][:]),
          deps=[f"_lnmu_{tag}"], event=f"_lnmu2_{tag}")
    s.add("vector", lambda v: v.tensor_scalar(sc["var"][:], sc["s2"][:], 1.0 / D, None, ALU.mult),
          deps=[f"_lns2_{tag}"], event=f"_lnvar0_{tag}")
    s.add("vector", lambda v: v.tensor_scalar(sc["var"][:], sc["var"][:], sc["tmp"][:], LNEPS,
                                              ALU.subtract, ALU.add),
          deps=[f"_lnvar0_{tag}", f"_lnmu2_{tag}"], event=f"_lnvar_{tag}")
    s.add("scalar", lambda e: e.activation(sc["rstd"][:], sc["var"][:], AF.Ln),
          deps=[f"_lnvar_{tag}"], event=f"_lnln_{tag}")
    s.add("scalar", lambda e: e.activation(sc["rstd"][:], sc["rstd"][:], AF.Exp, scale=-0.5),
          deps=[f"_lnln_{tag}"], event=f"_lnrstd_{tag}")
    s.add("vector", lambda v: v.tensor_scalar(out_ap, src_ap, sc["mu"][:], sc["rstd"][:],
                                              ALU.subtract, ALU.mult),
          deps=[f"_lnrstd_{tag}"] + ([store_dep_event] if store_dep_event else []),
          event=f"ln_{tag}")
    if store_fn is not None:
        s.add("scalar", store_fn, deps=[f"ln_{tag}"], dma=True, event=f"st_{tag}")


def make_ln_scratch(p, prefix=""):
    return dict(
        lnsq=p.sb(prefix + "lnsq", [128, D], F32),
        m=p.sb(prefix + "lnm", [128, 1], F32),
        s2=p.sb(prefix + "lns2", [128, 1], F32),
        mu=p.sb(prefix + "lnmu", [128, 1], F32),
        var=p.sb(prefix + "lnvar", [128, 1], F32),
        rstd=p.sb(prefix + "lnrstd", [128, 1], F32),
        tmp=p.sb(prefix + "lntmp", [128, 1], F32),
    )



def build_phase_d(ctx=None):
    if ctx is None:
        p = Prog()
        nc = p.nc
        x3 = nc.dram_tensor("x3s", [TPC, D], F32, kind="ExternalInput").ap()
        gateW = nc.dram_tensor("gate_W", [D, 8], F32, kind="ExternalInput").ap()
        W1 = nc.dram_tensor("W1", [8, D, 512], F32, kind="ExternalInput").ap()
        b1 = nc.dram_tensor("b1", [8, 512], F32, kind="ExternalInput").ap()
        W2 = nc.dram_tensor("W2", [8, 512, D], F32, kind="ExternalInput").ap()
        b2 = nc.dram_tensor("b2", [8, D], F32, kind="ExternalInput").ap()
        Ws1 = nc.dram_tensor("Ws1", [D, 512], F32, kind="ExternalInput").ap()
        bs1 = nc.dram_tensor("bs1", [512], F32, kind="ExternalInput").ap()
        Ws2 = nc.dram_tensor("Ws2", [512, D], F32, kind="ExternalInput").ap()
        bs2 = nc.dram_tensor("bs2", [D], F32, kind="ExternalInput").ap()
        rsc = nc.dram_tensor("route_scale", [1, 1], F32, kind="ExternalInput").ap()
        x4 = nc.dram_tensor("x4s", [TPC, D], F32, kind="ExternalOutput").ap()
        s = Sched(p)
        block = p.es.enter_context(nc.Block())
    else:
        p, s = ctx["p"], ctx["s"]
        nc = p.nc
        x3 = ctx["x3"]; gateW = ctx["gate_W"]; W1 = ctx["W1"]; b1 = ctx["b1"]
        W2 = ctx["W2"]; b2 = ctx["b2"]; Ws1 = ctx["Ws1"]; bs1 = ctx["bs1"]
        Ws2 = ctx["Ws2"]; bs2 = ctx["bs2"]; rsc = ctx["route_scale"]; x4 = ctx["x4"]
    NG = TPC // 512

    identf = p.sb("identf", [128, 128], F32)
    ident = p.sb("ident", [128, 128], BF16)
    identF = p.sb("identF", [128, 128], F32)
    onesrow = p.sb("onesrow", [1, 128], BF16)
    gw_sb = p.sb("gw_sb", [128, 8, 8], BF16)
    b1row = p.sb("b1row", [8, 512], F32)
    bs1row = p.sb("bs1row", [1, 512], F32)
    b1_sb = p.sb("b1_sb", [128, 4, 8], F32)
    bs1_sb = p.sb("bs1_sb", [128, 4, 1], F32)
    b2_sb = p.sb("b2_sb", [8, D], BF16)
    bs2_sb = p.sb("bs2_sb", [1, D], BF16)
    rs_sb = p.sb("rs_sb", [1, 1], BF16)
    rs_bc = p.sb("rs_bc", [128, 1], F32)
    ws1_sb = p.sb("ws1_sb", [128, 8, 4, 128], BF16)
    ws2_sb = p.sb("ws2_sb", [128, 4, 2, 512], BF16)
    w1_sb = [p.sb(f"w1_sb{i}", [128, 8, 4, 128], BF16) for i in range(2)]
    w2_sb = [p.sb(f"w2_sb{i}", [128, 4, 2, 512], BF16) for i in range(2)]

    xT = p.sb("xT", [128, 8, TPC], BF16)
    x3t = [p.sb(f"x3t{i}", [128, D], F32) for i in range(2)]
    xb16 = [p.sb(f"xb16{i}", [128, D], BF16) for i in range(2)]

    gt = p.sb("gt", [128, TILES_PC, 8], F32)
    gTt = p.sb("gTt", [8, TILES_PC, 128], BF16)
    gsc = p.sb("gsc", [128, 8], F32)
    gsc2 = p.sb("gsc2", [128, 8], F32)
    gm8 = p.sb("gm8", [128, 8], F32)
    gmi = p.sb("gmi", [128, 8], U32)
    gmif = p.sb("gmif", [128, 8], F32)
    giota = p.sb("giota", [128, 8], F32)
    gw1 = p.sb("gw1", [128, 1], F32)
    gw2 = p.sb("gw2", [128, 1], F32)
    gs12 = p.sb("gs12", [128, 1], F32)
    grcp = p.sb("grcp", [128, 1], F32)
    goh = p.sb("goh", [128, 8], F32)
    goh2 = p.sb("goh2", [128, 8], F32)
    gfb = p.sb("gfb", [128, 8], BF16)

    h1T = p.sb("h1T", [128, 4, 512], BF16)
    hsT = p.sb("hsT", [128, 4, 512], BF16)
    acc = [p.sb(f"acc{i}", [128, 4, D], F32) for i in range(2)]
    x3r = p.sb("x3r", [128, 4, D], F32)
    radd = p.sb("radd", [128, D], F32)
    rgrp = p.sb("rgrp", [128, 4, D], F32)
    outt = p.sb("outt", [128, D], F32)
    lnsc = make_ln_scratch(p)

    ptr = p.ps("ptr", [128, 1024], BF16)
    pg = p.ps("pg", [128, 512], F32)
    ph = [p.ps(f"ph{i}", [128, 512], F32) for i in range(2)]
    py = [p.ps(f"py{i}", [128, 512], F32) for i in range(2)]
    psh = [p.ps(f"psh{i}", [128, 512], F32) for i in range(2)]

    emit_setup_ident(s, identf, ident, identF)
    s.add("vector", lambda v: v.memset(onesrow[:], 1.0), event="ones")
    s.add("gpsimd", lambda g: g.iota(giota[:], pattern=[[1, 8]], base=0, channel_multiplier=0,
                                     allow_small_or_imprecise_dtypes=True), event="giota")
    s.add("sync", lambda e: e.dma_start(out=b1row[:], in_=b1[:, :]), dma=True, event="ld_b1r")
    s.add("sync", lambda e: e.dma_start(out=bs1row[:], in_=bs1[None, :]), dma=True, event="ld_bs1r")
    emit_rows_to_cols(s, "b1", b1row, 8, 4, b1_sb, identF, pg, "ld_b1r")
    emit_rows_to_cols(s, "bs1", bs1row, 1, 4, bs1_sb, identF, pg, "ld_bs1r")
    s.add("gpsimd", lambda g: g.dma_start(out=rs_sb[:], in_=rsc[:, :]), dma=True, event="ld_rs")
    s.add("gpsimd", lambda g: g.dma_start(out=gw_sb[:], in_=gateW.rearrange("(dc p) e -> p dc e", p=128)),
          dma=True, event="ld_gw")
    s.add("gpsimd", lambda g: g.dma_start(out=b2_sb[:], in_=b2[:, :]), dma=True, event="ld_b2")
    s.add("gpsimd", lambda g: g.dma_start(out=bs2_sb[:], in_=bs2[None, :]), dma=True, event="ld_bs2")
    s.add("gpsimd", lambda g: g.dma_start(out=ws1_sb[:], in_=Ws1.rearrange("(dc p) (hc q) -> p dc hc q", p=128, q=128)),
          dma=True, event="ld_ws1")
    s.add("gpsimd", lambda g: g.dma_start(out=ws2_sb[:], in_=Ws2.rearrange("(hc p) (dh q) -> p hc dh q", p=128, q=512)),
          dma=True, event="ld_ws2")

    s.add("tensor", lambda t: t.matmul(pg[:, 0:1], onesrow[:, :], rs_sb[:, :], start=True, stop=True),
          deps=["ones", "ld_rs", "cols_b1", "cols_bs1"], event="rsbc_mm")
    s.add("vector", lambda v: v.tensor_copy(rs_bc[:], pg[:, 0:1]), deps=["rsbc_mm"], event="rsbc")

    for t in range(TILES_PC):
        bi = t % 2
        s.add("scalar", lambda e, t=t, bi=bi: e.dma_start(out=x3t[bi][:], in_=x3[t * 128:(t + 1) * 128, :]),
              deps=[f"xcast{t-2}"] if t >= 2 else [], dma=True, event=f"x3l{t}")
        s.add("vector", lambda v, bi=bi: v.tensor_copy(xb16[bi][:], x3t[bi][:]),
              deps=[f"x3l{t}"] + ([f"tp{t-1}"] if t >= 1 else []), event=f"xcast{t}")
        first_deps = [f"xcast{t}", "ident"] + ([f"xTe{t-1}", f"gTe{t-1}"] if t >= 1 else [])
        for dc in range(8):
            s.add("tensor", lambda tt, bi=bi, dc=dc: tt.transpose(
                ptr[:, dc * 128:(dc + 1) * 128], xb16[bi][:, dc * 128:(dc + 1) * 128], ident[:]),
                deps=first_deps if dc == 0 else [], event=f"tp{t}" if dc == 7 else None)
        s.add("vector", lambda v, t=t: v.tensor_copy(
            xT[:, :, t * 128:(t + 1) * 128],
            ptr[:, :].rearrange("p (dc q) -> p dc q", q=128)),
            deps=[f"tp{t}"], event=f"xTe{t}")
        gdeps = [f"xTe{t}", "ld_gw", "rsbc"] + ([f"sig{t-1}"] if t >= 1 else [])
        for dc in range(8):
            s.add("tensor", lambda tt, t=t, dc=dc: tt.matmul(
                pg[:, 0:8], xT[:, dc, t * 128:(t + 1) * 128], gw_sb[:, dc, :],
                start=(dc == 0), stop=(dc == 7)),
                deps=gdeps if dc == 0 else [], event=f"gmm{t}" if dc == 7 else None)
        s.add("scalar", lambda e: e.activation(gsc[:], pg[:, 0:8], AF.Sigmoid),
              deps=[f"gmm{t}"] + ([f"grs{t-1}"] if t >= 1 else []), event=f"sig{t}")
        s.add("vector", lambda v: v.tensor_scalar(gsc2[:], gsc[:], rs_bc[:], None, ALU.mult),
              deps=[f"sig{t}", "giota"] + ([f"gdone{t-1}"] if t >= 1 else []), event=f"grs{t}")
        s.add("vector", lambda v: v.max(out=gm8[:], in_=gsc2[:]), deps=[f"grs{t}"], event=f"gm8_{t}")
        s.add("vector", lambda v: v.max_index(out=gmi[:], in_max=gm8[:], in_values=gsc2[:]),
              deps=[f"gm8_{t}"], event=f"gmi_{t}")
        s.add("vector", lambda v: v.tensor_copy(gmif[:], gmi[:]), deps=[f"gmi_{t}"], event=f"gmif_{t}")
        s.add("vector", lambda v: v.tensor_add(gs12[:], gm8[:, 0:1], gm8[:, 1:2]),
              deps=[f"gm8_{t}"], event=f"gs12_{t}")
        s.add("vector", lambda v: v.reciprocal(grcp[:], gs12[:]), deps=[f"gs12_{t}"], event=f"grcp_{t}")
        s.add("vector", lambda v: v.tensor_mul(gw1[:], gm8[:, 0:1], grcp[:]), deps=[f"grcp_{t}"], event=f"gw1_{t}")
        s.add("vector", lambda v: v.tensor_mul(gw2[:], gm8[:, 1:2], grcp[:]), deps=[f"grcp_{t}"], event=f"gw2_{t}")
        s.add("vector", lambda v: v.tensor_scalar(goh[:], giota[:], gmif[:, 0:1], None, ALU.is_equal),
              deps=[f"gmif_{t}"], event=f"goh_{t}")
        s.add("vector", lambda v: v.tensor_scalar(goh2[:], giota[:], gmif[:, 1:2], None, ALU.is_equal),
              deps=[f"gmif_{t}"], event=f"goh2_{t}")
        s.add("vector", lambda v, t=t: v.tensor_scalar(gt[:, t, :], goh[:], gw1[:], None, ALU.mult),
              deps=[f"goh_{t}", f"gw1_{t}"], event=f"gtp1_{t}")
        s.add("vector", lambda v, t=t: v.scalar_tensor_tensor(gt[:, t, :], goh2[:], gw2[:], gt[:, t, :],
                                                              ALU.mult, ALU.add),
              deps=[f"gtp1_{t}", f"goh2_{t}", f"gw2_{t}"], event=f"gfull_{t}")
        s.add("vector", lambda v, t=t: v.tensor_copy(gfb[:], gt[:, t, :]),
              deps=[f"gfull_{t}"], event=f"gdone{t}")
        s.add("tensor", lambda tt: tt.transpose(ptr[0:8, 0:128], gfb[:, :], ident[:]),
              deps=[f"gdone{t}", f"xTe{t}"], event=f"gtp{t}")
        s.add("vector", lambda v, t=t: v.tensor_copy(gTt[:8, t, :], ptr[0:8, 0:128]),
              deps=[f"gtp{t}"], event=f"gTe{t}")

    phk = 0
    last_ph_ev = [None, None]
    last_py_ev = [None, None]
    last_psh_ev = [None, None]
    last_w_ev = {}

    def h_chunks(tag, gi, lhs_sb, bias_sb_fn, out_sbT, extra_deps):
        nonlocal phk
        evs = []
        for hc in range(4):
            pk = phk % 2
            phk += 1
            deps = list(extra_deps) + [f"xTe{gi*4+3}"]
            if last_ph_ev[pk] is not None:
                deps.append(last_ph_ev[pk])
            for dc in range(8):
                s.add("tensor", lambda tt, gi=gi, hc=hc, dc=dc, pk=pk, lhs_sb=lhs_sb: tt.matmul(
                    ph[pk][:, :], lhs_sb[:, dc, hc, :], xT[:, dc, gi * 512:(gi + 1) * 512],
                    start=(dc == 0), stop=(dc == 7)),
                    deps=deps if dc == 0 else [],
                    event=f"hmm_{tag}_{hc}" if dc == 7 else None)
            s.add("scalar", lambda e, hc=hc, pk=pk: e.activation(
                out_sbT[:, hc, :], ph[pk][:, :], AF.Gelu, bias=bias_sb_fn(hc)),
                deps=[f"hmm_{tag}_{hc}"], event=f"hev_{tag}_{hc}")
            last_ph_ev[pk] = f"hev_{tag}_{hc}"
            evs.append(f"hev_{tag}_{hc}")
        return evs

    st_events = []
    for gi in range(NG):
        zdeps = []
        if gi > 0:
            zdeps = [f"radd_{gi-1}_{ti}_{dh}" for ti in range(4) for dh in range(2)]
        s.add("vector", lambda v: v.memset(acc[0][:], 0.0), deps=zdeps, event=f"acc0_{gi}")
        accev = {(ti, dh): f"acc0_{gi}" for ti in range(4) for dh in range(2)}

        shevs = h_chunks(f"sh{gi}", gi, ws1_sb, lambda hc: bs1_sb[:, hc, 0:1], hsT,
                         ["ld_ws1", "cols_bs1"])

        for e in range(8):
            k = gi * 8 + e
            buf = k % 2
            wdeps = [last_w_ev[k - 2]] if (k - 2) in last_w_ev else []
            s.add("gpsimd", lambda g, e=e, buf=buf: g.dma_start(
                out=w1_sb[buf][:], in_=W1[e].rearrange("(dc p) (hc q) -> p dc hc q", p=128, q=128)),
                deps=wdeps, dma=True, event=f"wl1_{k}")
            s.add("gpsimd", lambda g, e=e, buf=buf: g.dma_start(
                out=w2_sb[buf][:], in_=W2[e].rearrange("(hc p) (dh q) -> p hc dh q", p=128, q=512)),
                dma=True, event=f"wl2_{k}")
            hevs = h_chunks(f"e{k}", gi, w1_sb[buf],
                            lambda hc, e=e: b1_sb[:, hc, e:e + 1], h1T,
                            [f"wl1_{k}", "cols_b1"])
            for ti in range(4):
                tglob = gi * 4 + ti
                for dh in range(2):
                    deps = list(hevs) + [f"wl2_{k}"]
                    if last_py_ev[dh] is not None:
                        deps.append(last_py_ev[dh])
                    for hc in range(4):
                        s.add("tensor", lambda tt, ti=ti, dh=dh, hc=hc, buf=buf: tt.matmul(
                            py[dh][:, :], h1T[:, hc, ti * 128:(ti + 1) * 128],
                            w2_sb[buf][:, hc, dh, :],
                            start=(hc == 0), stop=(hc == 3)),
                            deps=deps if hc == 0 else [],
                            event=f"ymm_{k}_{ti}_{dh}" if hc == 3 else None)
                    nxt, cur = (e + 1) % 2, e % 2
                    s.add("vector", lambda v, ti=ti, dh=dh, tglob=tglob, e=e, nxt=nxt, cur=cur: v.scalar_tensor_tensor(
                        acc[nxt][:, ti, dh * 512:(dh + 1) * 512], py[dh][:, :],
                        gt[:, tglob, e:e + 1], acc[cur][:, ti, dh * 512:(dh + 1) * 512],
                        ALU.mult, ALU.add),
                        deps=[f"ymm_{k}_{ti}_{dh}", accev[(ti, dh)]],
                        event=f"yev_{k}_{ti}_{dh}")
                    accev[(ti, dh)] = f"yev_{k}_{ti}_{dh}"
                    last_py_ev[dh] = f"yev_{k}_{ti}_{dh}"
            last_w_ev[k] = f"ymm_{k}_{3}_{1}"

        accfin = acc[8 % 2]
        for ti in range(4):
            rdeps = [f"x3l{TILES_PC-1}"] if gi == 0 else []
            s.add("scalar", lambda e, gi=gi, ti=ti: e.dma_start(
                out=x3r[:, ti, :], in_=x3[(gi * 4 + ti) * 128:(gi * 4 + ti + 1) * 128, :]),
                deps=rdeps, dma=True, event=f"x3r_{gi}_{ti}")

        for ti in range(4):
            tglob = gi * 4 + ti
            for dh in range(2):
                deps = list(shevs) + [f"gTe{tglob}", "ld_b2", "ld_bs2", "ld_ws2", "ones"]
                if last_psh_ev[dh] is not None:
                    deps.append(last_psh_ev[dh])
                for hc in range(4):
                    s.add("tensor", lambda tt, ti=ti, dh=dh, hc=hc: tt.matmul(
                        psh[dh][:, :], hsT[:, hc, ti * 128:(ti + 1) * 128],
                        ws2_sb[:, hc, dh, :], start=(hc == 0), stop=False),
                        deps=deps if hc == 0 else [], event=None)
                s.add("tensor", lambda tt, tglob=tglob, dh=dh: tt.matmul(
                    psh[dh][:, :], gTt[:8, tglob, :], b2_sb[:8, dh * 512:(dh + 1) * 512],
                    start=False, stop=False), event=None)
                s.add("tensor", lambda tt, dh=dh: tt.matmul(
                    psh[dh][:, :], onesrow[:, :], bs2_sb[:, dh * 512:(dh + 1) * 512],
                    start=False, stop=True), event=f"fmm_{gi}_{ti}_{dh}")
                s.add("vector", lambda v, ti=ti, dh=dh, accfin=accfin: v.scalar_tensor_tensor(
                    radd[:, dh * 512:(dh + 1) * 512], psh[dh][:, :], 1.0,
                    accfin[:, ti, dh * 512:(dh + 1) * 512], ALU.mult, ALU.add),
                    deps=[f"fmm_{gi}_{ti}_{dh}", accev[(ti, dh)]] +
                         ([f"rg_{gi}_{ti}_{dh}"] if False else []) +
                         ([f"radd_prev_guard"] if False else []),
                    event=f"ra_{gi}_{ti}_{dh}")
                s.add("vector", lambda v, gi=gi, ti=ti, dh=dh: v.tensor_add(
                    rgrp[:, ti, dh * 512:(dh + 1) * 512], radd[:, dh * 512:(dh + 1) * 512],
                    x3r[:, ti, dh * 512:(dh + 1) * 512]),
                    deps=[f"ra_{gi}_{ti}_{dh}", f"x3r_{gi}_{ti}"],
                    event=f"radd_{gi}_{ti}_{dh}")
                last_psh_ev[dh] = f"radd_{gi}_{ti}_{dh}"

        for ti in range(4):
            tglob = gi * 4 + ti
            tag = f"d{tglob}"
            emit_layernorm(
                s, tag, rgrp[:, ti, :], outt[:],
                lnsc,
                deps=[f"radd_{gi}_{ti}_0", f"radd_{gi}_{ti}_1"] +
                     ([f"st_d{tglob-1}"] if tglob >= 1 else []),
                store_fn=lambda e, tglob=tglob: e.dma_start(
                    out=x4[tglob * 128:(tglob + 1) * 128, :], in_=outt[:]),
                store_dep_event=f"st_d{tglob-1}" if tglob >= 1 else None,
            )
            st_events.append(f"st_d{tglob}")

    if ctx is None:
        s.finish(*st_events)
        s.emit(block)
        return p
    return st_events




BUCKET = 768
CAPACITY = 4096
RMSEPS = 1e-6
BIGIDX = float(1 << 23)
SBK = 384


def build_phase_b(debug=False, nexp=8, ctx=None):
    if ctx is None:
        p = Prog()
        nc = p.nc
        x1 = nc.dram_tensor("x1s", [TPC, D], F32, kind="ExternalInput").ap()
        Wr = nc.dram_tensor("Wr", [D, 8], F32, kind="ExternalInput").ap()
        normw = nc.dram_tensor("norm_w", [8, D], F32, kind="ExternalInput").ap()
        Wg = nc.dram_tensor("Wg", [8, D, 2048], F32, kind="ExternalInput").ap()
        Wu = nc.dram_tensor("Wu", [8, D, 2048], F32, kind="ExternalInput").ap()
        Wd = nc.dram_tensor("Wd", [8, 2048, D], F32, kind="ExternalInput").ap()
        fnorm = nc.dram_tensor("f_norm", [D], F32, kind="ExternalInput").ap()
        fWg = nc.dram_tensor("fWg", [D, 512], F32, kind="ExternalInput").ap()
        fWu = nc.dram_tensor("fWu", [D, 512], F32, kind="ExternalInput").ap()
        fWd = nc.dram_tensor("fWd", [512, D], F32, kind="ExternalInput").ap()
        x2 = nc.dram_tensor("x2s", [TPC, D], F32, kind="ExternalOutput").ap()
        pmsk = nc.dram_tensor("prevmask", [1, 8], F32, kind="ExternalInput").ap()
        _pfx = ""
    else:
        p, s = ctx["p"], ctx["s"]
        nc = p.nc
        x1 = ctx["x1"]; Wr = ctx["Wr"]; normw = ctx["norm_w"]; Wg = ctx["Wg"]
        Wu = ctx["Wu"]; Wd = ctx["Wd"]; fnorm = ctx["f_norm"]; fWg = ctx["fWg"]
        fWu = ctx["fWu"]; fWd = ctx["fWd"]; x2 = ctx["x2"]; pmsk = ctx["prevmask"]
        _pfx = ctx["pfx"]

    if debug:
        dbg_dst = nc.dram_tensor("dbg_dst", [2, 128, TILES_PC], I32, kind="ExternalOutput").ap()
        dbg_wk = nc.dram_tensor("dbg_wk", [128, TILES_PC, 2], F32, kind="ExternalOutput").ap()
        dbg_cnt = nc.dram_tensor("dbg_cnt", [8, TILES_PC], F32, kind="ExternalOutput").ap()
        dbg_bo = nc.dram_tensor("dbg_bo", [8, TILES_PC], F32, kind="ExternalOutput").ap()
        dbg_bkt = nc.dram_tensor("dbg_bkt", [8 * BUCKET, D], F32, kind="ExternalOutput").ap()
        dbg_ybkt = nc.dram_tensor("dbg_ybkt", [8 * BUCKET, D], F32, kind="ExternalOutput").ap()
        dbg_ovf = nc.dram_tensor("dbg_ovf", [128, TILES_PC], F32, kind="ExternalOutput").ap()
        dbg_hT = nc.dram_tensor("dbg_hT", [128, 8, BUCKET], F32, kind="ExternalOutput").ap()
        dbg_hTs = nc.dram_tensor("dbg_hTs", [128, 8, BUCKET], F32, kind="ExternalOutput").ap()
        dbg_rms = nc.dram_tensor("dbg_rms", [128, BUCKET], F32, kind="ExternalOutput").ap()
    buckets = nc.dram_tensor(_pfx + "buckets", [8 * BUCKET, D], BF16)
    ybuckets = nc.dram_tensor(_pfx + "ybuckets", [8 * BUCKET, D], BF16)
    agi = nc.dram_tensor(_pfx + "agi", [8, TILES_PC], F32)
    ago = nc.dram_tensor(_pfx + "ago", [64, TILES_PC], F32)

    if ctx is None:
        s = Sched(p)
        block = p.es.enter_context(nc.Block())

    identf = p.sb("identf", [128, 128], F32)
    ident = p.sb("ident", [128, 128], BF16)
    identF = p.sb("identF", [128, 128], F32)
    tribf = p.sb("tribf", [128, 128], BF16)
    onesrow = p.sb("onesrow", [1, 128], BF16)
    onesrowF = p.sb("onesrowF", [1, 128], F32)
    onescol = p.sb("onescol", [128, 1], BF16)
    giota = p.sb("giota", [128, 8], F32)

    wr_sb = p.sb("wr_sb", [128, 8, 8], BF16)
    nwrow = p.sb("nwrow", [8, D], F32)
    nw_sb = p.sb("nw_sb", [128, 8, 8], F32)
    fnrow = p.sb("fnrow", [1, D], F32)
    fnw_col = p.sb("fnw_col", [128, 8, 1], F32)
    fwg_sb = p.sb("fwg_sb", [128, 8, 4, 128], BF16)
    fwu_sb = p.sb("fwu_sb", [128, 8, 4, 128], BF16)
    fwd_sb = p.sb("fwd_sb", [128, 4, 8, 128], BF16)

    x1t = [p.sb(f"x1t{i}", [128, D], F32) for i in range(2)]
    x1s3 = p.sb("x1s3", [128, D], F32)
    x1s6 = x1s3
    xT = p.sb("xT", [128, 8, TPC], BF16)
    xb16 = p.sb("xb16", [128, D], BF16)

    rprob = p.sb("rprob", [128, 8], F32)
    rexp = p.sb("rexp", [128, 8], F32)
    rscr = p.sb("rscr", [128, 8], F32)
    rm = p.sb("rm", [128, 1], F32)
    rsum = p.sb("rsum", [128, 1], F32)
    rrcp = p.sb("rrcp", [128, 1], F32)
    rm8 = p.sb("rm8", [128, 8], F32)
    rmi = p.sb("rmi", [128, 8], U32)
    rmif = p.sb("rmif", [128, 8], F32)
    roh = [p.sb(f"roh{k}", [128, 8], F32) for k in range(2)]
    g01b = p.sb("g01b", [128, 8], BF16)
    cex = p.sb("cex", [128, 8], F32)
    cnt_own = p.sb("cnt_own", [8, TILES_PC], F32)
    cnt_glob = p.sb("cnt_glob", [8, NCORES * TILES_PC], F32)
    base_l = p.sb("base_l", [8, TILES_PC], F32)
    base_o = p.sb("base_o", [8, TILES_PC], F32)
    tot_all = p.sb("tot_all", [8, 8], F32)
    pm_sb = p.sb("pm_sb", [1, 8], BF16)
    pm_bc = p.sb("pm_bc", [8, 8], F32)
    cbase = p.sb("cbase", [8, 1], F32)
    totsc = p.sb("totsc", [8, 8], F32)
    zrow8 = p.sb("zrow8", [8, NCORES * TILES_PC], F32)
    brow = p.sb("brow", [1, 8], F32)
    bbc = p.sb("bbc", [128, 8], F32)
    brow2 = p.sb("brow2", [1, 8], F32)
    bbc2 = p.sb("bbc2", [128, 8], F32)

    e_ka = p.sb("e_ka", [128, TILES_PC, 2], F32)
    w_ka = p.sb("w_ka", [128, TILES_PC, 2], F32)
    c_ka = p.sb("c_ka", [128, TILES_PC, 2], F32)
    wk = p.sb("wk", [128, TILES_PC, 2], F32)
    ovf = p.sb("ovf", [128, TILES_PC], F32)
    ovfu = p.sb("ovfu", [128, TILES_PC], U32)
    dsti = [p.sb(f"dsti{k}", [128, TILES_PC], I32) for k in range(2)]
    t1 = p.sb("t1", [128, 1], F32)
    t2 = p.sb("t2", [128, 1], F32)
    t3 = p.sb("t3", [128, 1], F32)
    t4 = p.sb("t4", [128, 1], F32)
    t5 = p.sb("t5", [128, 1], F32)
    ohs = p.sb("ohs", [128, 8], F32)

    hT = p.sb("hT", [128, 8, BUCKET], BF16)
    hTs = p.sb("hTs", [128, 8, BUCKET], BF16)
    sqs = p.sb("sqs", [128, BUCKET], BF16)
    rmsrow = p.sb("rmsrow", [1, BUCKET], F32)
    rms_bc = p.sb("rms_bc", [128, BUCKET], F32)
    wgc = [p.sb(f"wgc{i}", [128, 8, 128], BF16) for i in range(2)]
    wuc = [p.sb(f"wuc{i}", [128, 8, 128], BF16) for i in range(2)]
    wdc = [p.sb("wdc0", [128, 16, 128], BF16)] * 2
    gg = p.sb("gg", [128, SBK], BF16)
    uu = p.sb("uu", [128, SBK], BF16)
    guT = p.sb("guT", [128, 16, SBK], BF16)
    yT = p.sb("yT", [128, 8, BUCKET], BF16)
    ytm = xb16

    fhT = p.sb("fhT", [128, 8, 512], BF16)
    fsqs = p.sb("fsqs", [128, 512], BF16)
    frms_row = p.sb("frms_row", [1, 512], F32)
    frms_bc = rms_bc[:, 0:512]
    fgg = p.sb("fgg", [128, 512], BF16)
    fuu = p.sb("fuu", [128, 512], BF16)
    fgu = p.sb("fgu", [128, 4, 512], BF16)
    foutT = p.sb("foutT", [128, 8, 512], BF16)
    fout_tm = p.sb("fout_tm", [128, D], BF16)

    gy = [p.sb(f"gy{k}", [128, D], BF16) for k in range(2)]
    comb = p.sb("comb", [128, D], F32)
    comb2 = p.sb("comb2", [128, D], F32)
    selo = p.sb("selo", [128, D], F32)
    rfin = p.sb("rfin", [128, D], F32)
    outt = comb2
    lnsc = dict(
        lnsq=comb,
        m=p.sb("lnm", [128, 1], F32), s2=p.sb("lns2", [128, 1], F32),
        mu=p.sb("lnmu", [128, 1], F32), var=p.sb("lnvar", [128, 1], F32),
        rstd=p.sb("lnrstd", [128, 1], F32), tmp=p.sb("lntmp", [128, 1], F32))

    ptr = p.ps("ptr", [128, 1024], BF16)
    pa = p.ps("pa", [128, 512], F32)
    pc_ = [p.ps(f"pc{i}", [128, 512], F32) for i in range(2)]
    pd_ = [p.ps(f"pd{i}", [128, 512], F32) for i in range(2)]
    pe_ = [p.ps(f"pe{i}", [128, 512], F32) for i in range(2)]

    breg_h = {}

    def _mk_breg(g):
        breg_h["r"] = g.alloc_register("bnd")
        return g.reg_mov(breg_h["r"], 8 * BUCKET - 1)

    s.add("gpsimd", _mk_breg, event="breg")
    emit_setup_ident(s, identf, ident, identF)
    s.add("vector", lambda v: v.tensor_single_scalar(tribf[:], identf[:], 0.0, ALU.is_gt),
          deps=["identf"], event="tri")
    s.add("vector", lambda v: v.memset(onesrow[:], 1.0), event="ones")
    s.add("vector", lambda v: v.memset(onesrowF[:], 1.0), event="onesF")
    s.add("vector", lambda v: v.memset(onescol[:], 1.0), event="onesc")
    s.add("vector", lambda v: v.memset(zrow8[:], 0.0), event="zrow")
    s.add("vector", lambda v: v.memset(gy[0][:], 0.0), event="gyz0")
    s.add("vector", lambda v: v.memset(gy[1][:], 0.0), event="gyz1")
    s.add("gpsimd", lambda g: g.iota(giota[:], pattern=[[1, 8]], base=0, channel_multiplier=0,
                                     allow_small_or_imprecise_dtypes=True), event="giota")
    s.add("gpsimd", lambda g: g.dma_start(out=wr_sb[:], in_=Wr.rearrange("(dc p) e -> p dc e", p=128)),
          dma=True, event="ld_wr")
    s.add("sync", lambda e: e.dma_start(out=nwrow[:], in_=normw[:, :]), dma=True, event="ld_nwr")
    emit_rows_to_cols(s, "nw", nwrow, 8, 8, nw_sb, identF, pa, "ld_nwr")
    s.add("sync", lambda e: e.dma_start(out=fnrow[:], in_=fnorm[None, :]), dma=True, event="ld_fnr")
    emit_rows_to_cols(s, "fnw", fnrow, 1, 8, fnw_col, identF, pa, "ld_fnr")
    s.add("gpsimd", lambda g: g.dma_start(out=fwg_sb[:], in_=fWg.rearrange("(dc p) (fc q) -> p dc fc q", p=128, q=128)),
          dma=True, event="ld_fwg")
    s.add("gpsimd", lambda g: g.dma_start(out=fwu_sb[:], in_=fWu.rearrange("(dc p) (fc q) -> p dc fc q", p=128, q=128)),
          dma=True, event="ld_fwu")
    s.add("gpsimd", lambda g: g.dma_start(out=fwd_sb[:], in_=fWd.rearrange("(fc p) (dc q) -> p fc dc q", p=128, q=128)),
          dma=True, event="ld_fwd")

    for t in range(TILES_PC):
        s.add("scalar", lambda e, t=t: e.dma_start(out=x1t[t % 2][:], in_=x1[t * 128:(t + 1) * 128, :]),
              deps=[f"xcast{t-2}"] if t >= 2 else [], dma=True, event=f"x1l{t}")
        s.add("vector", lambda v, t=t: v.tensor_copy(xb16[:], x1t[t % 2][:]),
              deps=[f"x1l{t}"] + ([f"tp{t-1}"] if t else []), event=f"xcast{t}")
        fdeps = [f"xcast{t}", "ident"] + ([f"xTe{t-1}"] if t else [])
        for dc in range(8):
            s.add("tensor", lambda tt, dc=dc: tt.transpose(
                ptr[:, dc * 128:(dc + 1) * 128], xb16[:, dc * 128:(dc + 1) * 128], ident[:]),
                deps=fdeps if dc == 0 else [], event=f"tp{t}" if dc == 7 else None)
        s.add("vector", lambda v, t=t: v.tensor_copy(
            xT[:, :, t * 128:(t + 1) * 128], ptr[:, :].rearrange("p (dc q) -> p dc q", q=128)),
            deps=[f"tp{t}"], event=f"xTe{t}")
        rdeps = [f"xTe{t}", "ld_wr"] + ([f"rexp{t-1}"] if t else ["cols_nw", "cols_fnw"])
        for dc in range(8):
            s.add("tensor", lambda tt, t=t, dc=dc: tt.matmul(
                pa[:, 0:8], xT[:, dc, t * 128:(t + 1) * 128], wr_sb[:, dc, :],
                start=(dc == 0), stop=(dc == 7)),
                deps=rdeps if dc == 0 else [], event=f"rmm{t}" if dc == 7 else None)
        sdeps = [f"rmm{t}"] + ([f"rdone{t-1}"] if t else [])
        s.add("vector", lambda v: v.tensor_reduce(rm[:], pa[:, 0:8], axis=AX.X, op=ALU.max),
              deps=sdeps, event=f"rmax{t}")
        s.add("vector", lambda v: v.tensor_scalar(rm[:], rm[:], -1.0, None, ALU.mult),
              deps=[f"rmax{t}"], event=f"rneg{t}")
        s.add("scalar", lambda e: e.activation(rexp[:], pa[:, 0:8], AF.Exp, bias=rm[:]),
              deps=[f"rneg{t}"], event=f"rexp{t}")
        s.add("vector", lambda v: v.tensor_reduce(rsum[:], rexp[:], axis=AX.X, op=ALU.add),
              deps=[f"rexp{t}"], event=f"rsum{t}")
        s.add("vector", lambda v: v.reciprocal(rrcp[:], rsum[:]), deps=[f"rsum{t}"], event=f"rrcp{t}")
        s.add("vector", lambda v: v.tensor_scalar(rprob[:], rexp[:], rrcp[:], None, ALU.mult),
              deps=[f"rrcp{t}"], event=f"rprob{t}")
        s.add("vector", lambda v: v.max(out=rm8[:], in_=rprob[:]), deps=[f"rprob{t}"], event=f"rm8{t}")
        s.add("vector", lambda v: v.max_index(out=rmi[:], in_max=rm8[:], in_values=rprob[:]),
              deps=[f"rm8{t}"], event=f"rmi{t}")
        s.add("vector", lambda v: v.tensor_copy(rmif[:], rmi[:]), deps=[f"rmi{t}"], event=f"rmif{t}")
        for k in range(2):
            s.add("vector", lambda v, k=k: v.tensor_scalar(roh[k][:], giota[:], rmif[:, k:k + 1], None, ALU.is_equal),
                  deps=[f"rmif{t}", "giota"], event=f"roh{k}_{t}")
        s.add("vector", lambda v: v.tensor_add(cex[:], roh[0][:], roh[1][:]),
              deps=[f"roh0_{t}", f"roh1_{t}"], event=f"g01_{t}")
        s.add("vector", lambda v: v.tensor_copy(g01b[:], cex[:]), deps=[f"g01_{t}"], event=f"g01b_{t}")
        s.add("tensor", lambda tt: tt.matmul(pd_[0][:, 0:8], tribf[:, :], g01b[:, :], start=True, stop=True),
              deps=[f"g01b_{t}", "tri"] + ([f"cexe{t-1}"] if t else []), event=f"cexm{t}")
        s.add("vector", lambda v: v.tensor_copy(cex[:], pd_[0][:, 0:8]), deps=[f"cexm{t}"], event=f"cexe{t}")
        s.add("tensor", lambda tt: tt.matmul(pd_[1][:8, 0:1], g01b[:, :], onescol[:, :], start=True, stop=True),
              deps=[f"g01b_{t}", "onesc"] + ([f"cnte{t-1}"] if t else []), event=f"cntm{t}")
        s.add("vector", lambda v, t=t: v.tensor_copy(cnt_own[:8, t:t + 1], pd_[1][:8, 0:1]),
              deps=[f"cntm{t}"], event=f"cnte{t}")
        s.add("vector", lambda v, t=t: v.tensor_copy(w_ka[:, t, :], rm8[:, 0:2]),
              deps=[f"rm8{t}"], event=f"wst{t}")
        s.add("vector", lambda v, t=t: v.tensor_copy(e_ka[:, t, :], rmif[:, 0:2]),
              deps=[f"rmif{t}"], event=f"est{t}")
        for k in range(2):
            s.add("vector", lambda v, k=k: v.tensor_mul(rscr[:], cex[:], roh[k][:]),
                  deps=[f"cexe{t}", f"roh{k}_{t}"] + ([f"cst{k}_{t-1}"] if t else []), event=f"cm{k}_{t}")
            s.add("vector", lambda v, k=k, t=t: v.tensor_reduce(c_ka[:, t, k:k + 1], rscr[:], axis=AX.X, op=ALU.add),
                  deps=[f"cm{k}_{t}"], event=f"cst{k}_{t}")
        s.add("vector", lambda v: v.engine_nop(),
              deps=[f"cst0_{t}", f"cst1_{t}", f"wst{t}", f"est{t}"], event=f"rdone{t}")

    s.add("gpsimd", lambda g: g.dma_start(out=agi.ap()[:, :], in_=cnt_own[:8, :]),
          deps=[f"cnte{TILES_PC-1}"], dma=True, event="cnt_out")
    s.add("gpsimd", lambda g: g.collective_compute(
        "AllGather", ALU.bypass, replica_groups=[list(range(NCORES))],
        ins=[agi.ap().opt()], outs=[ago.ap().opt()]),
        deps=["cnt_out"], event="cnt_ag")
    s.add("gpsimd", lambda g: g.dma_start(out=cnt_glob[:8, :].rearrange("e (c t) -> e c t", c=8), in_=ago.ap().rearrange("(c e) t -> e c t", e=8)),
          deps=["cnt_ag"], dma=True, event="cnt_in")
    s.add("gpsimd", lambda g: g.dma_start(out=pm_sb[:], in_=pmsk[:, :]), dma=True, event="ld_pm")
    s.add("vector", lambda v: v.tensor_reduce(tot_all[:8, :], cnt_glob[:8, :].rearrange("e (c t) -> e c t", c=8),
                                              axis=AX.X, op=ALU.add),
          deps=["cnt_in"], event="tot_all")
    s.add("tensor", lambda t: t.matmul(pa[:8, 32:40], onesrow[:, 0:8], pm_sb[:, :], start=True, stop=True),
          deps=["ld_pm", "ones", f"bbcanswer" if False else "tot_all"], event="pmb_mm")
    s.add("vector", lambda v: v.tensor_copy(pm_bc[:8, :], pa[:8, 32:40]), deps=["pmb_mm"], event="pm_bc")
    s.add("vector", lambda v: v.tensor_mul(totsc[:8, :], tot_all[:8, :], pm_bc[:8, :]),
          deps=["pm_bc"], event="totsc")
    s.add("vector", lambda v: v.tensor_reduce(cbase[:8, :], totsc[:8, :], axis=AX.X, op=ALU.add),
          deps=["totsc"], event="cbase")
    s.add("vector", lambda v: v.tensor_tensor_scan(base_l[:8, :], cnt_own[:8, :], zrow8[:8, 0:TILES_PC],
                                                   0.0, ALU.add, ALU.add),
          deps=[f"cnte{TILES_PC-1}", "zrow"], event="scan_l")
    s.add("vector", lambda v: v.tensor_sub(base_l[:8, :], base_l[:8, :], cnt_own[:8, :]),
          deps=["scan_l"], event="base_l")
    s.add("vector", lambda v: v.tensor_scalar(base_o[:8, :], base_l[:8, :], cbase[:8, :], None, ALU.add),
          deps=["base_l", "cbase"], event="base_o")

    scat_evs = []
    for t in range(TILES_PC):
        s.add("tensor", lambda tt, t=t: tt.transpose(pa[0:1, 0:8], base_o[:8, t:t + 1], identF[:8, :8]),
              deps=["base_o", "identF"] + ([f"bbce{t-1}"] if t else []), event=f"btp{t}")
        s.add("vector", lambda v: v.tensor_copy(brow[:], pa[0:1, 0:8]), deps=[f"btp{t}"], event=f"brw{t}")
        s.add("tensor", lambda tt: tt.matmul(pa[:, 16:24], onesrowF[:, :], brow[:, :], start=True, stop=True),
              deps=[f"brw{t}", "onesF"], event=f"bmm{t}")
        s.add("vector", lambda v: v.tensor_copy(bbc[:], pa[:, 16:24]), deps=[f"bmm{t}"], event=f"bbce{t}")
        s.add("tensor", lambda tt, t=t: tt.transpose(pa[0:1, 8:16], base_l[:8, t:t + 1], identF[:8, :8]),
              deps=["base_l", f"bbce{t}"], event=f"btp2{t}")
        s.add("vector", lambda v: v.tensor_copy(brow2[:], pa[0:1, 8:16]), deps=[f"btp2{t}"], event=f"brw2{t}")
        s.add("tensor", lambda tt: tt.matmul(pa[:, 24:32], onesrowF[:, :], brow2[:, :], start=True, stop=True),
              deps=[f"brw2{t}", "onesF"], event=f"bmm2{t}")
        s.add("vector", lambda v: v.tensor_copy(bbc2[:], pa[:, 24:32]), deps=[f"bmm2{t}"], event=f"bbce2{t}")
        kevs = []
        for k in range(2):
            kb = f"{t}_{k}"
            s.add("vector", lambda v, t=t, k=k: v.tensor_scalar(ohs[:], giota[:], e_ka[:, t, k:k + 1], None, ALU.is_equal),
                  deps=[f"bbce2{t}"] + ([f"dsts_{t}_{k-1}"] if k else []), event=f"soh{kb}")
            s.add("vector", lambda v: v.tensor_mul(rscr[:], ohs[:], bbc[:]), deps=[f"soh{kb}"], event=f"sg0{kb}")
            s.add("vector", lambda v: v.tensor_reduce(t1[:], rscr[:], axis=AX.X, op=ALU.add),
                  deps=[f"sg0{kb}"], event=f"sg{kb}")
            s.add("vector", lambda v: v.tensor_mul(rscr[:], ohs[:], bbc2[:]), deps=[f"sg{kb}"], event=f"sl0{kb}")
            s.add("vector", lambda v: v.tensor_reduce(t2[:], rscr[:], axis=AX.X, op=ALU.add),
                  deps=[f"sl0{kb}"], event=f"sl{kb}")
            s.add("vector", lambda v, t=t, k=k: v.tensor_add(t3[:], t1[:], c_ka[:, t, k:k + 1]),
                  deps=[f"sg{kb}"], event=f"pg{kb}")
            s.add("vector", lambda v: v.tensor_scalar(t4[:], t3[:], float(CAPACITY - 1), None, ALU.is_le),
                  deps=[f"pg{kb}"], event=f"kp{kb}")
            s.add("vector", lambda v, t=t, k=k: v.tensor_add(t3[:], t2[:], c_ka[:, t, k:k + 1]),
                  deps=[f"sl{kb}", f"kp{kb}"], event=f"pl{kb}")
            s.add("vector", lambda v: v.tensor_scalar(t5[:], t3[:], float(BUCKET - 1), None, ALU.is_le),
                  deps=[f"pl{kb}"], event=f"ib{kb}")
            s.add("vector", lambda v: v.tensor_mul(t5[:], t5[:], t4[:]), deps=[f"ib{kb}"], event=f"kk{kb}")
            s.add("vector", lambda v, t=t, k=k: v.tensor_mul(wk[:, t, k:k + 1], w_ka[:, t, k:k + 1], t5[:]),
                  deps=[f"kk{kb}"], event=f"wkf{kb}")
            s.add("vector", lambda v, t=t, k=k: v.tensor_scalar(rscr[:, 0:1], e_ka[:, t, k:k + 1], float(BUCKET), None, ALU.mult),
                  deps=[f"kk{kb}"], event=f"d0{kb}")
            s.add("vector", lambda v: v.tensor_add(t3[:], t3[:], rscr[:, 0:1]), deps=[f"d0{kb}"], event=f"d1{kb}")
            s.add("vector", lambda v: v.tensor_scalar(t5[:], t5[:], 1.0, -BIGIDX, ALU.subtract, ALU.mult),
                  deps=[f"d1{kb}", f"wkf{kb}"], event=f"d2{kb}")
            s.add("vector", lambda v: v.tensor_add(t3[:], t3[:], t5[:]), deps=[f"d2{kb}"], event=f"d3{kb}")
            s.add("vector", lambda v, t=t, k=k: v.tensor_copy(dsti[k][:, t:t + 1], t3[:]),
                  deps=[f"d3{kb}"], event=f"dsts_{t}_{k}")
            kevs.append(f"dsts_{t}_{k}")
        for k in range(2):
            s.add("vector", lambda v, t=t, k=k: v.tensor_scalar(ohs[:], giota[:], e_ka[:, t, k:k + 1], None, ALU.is_equal),
                  deps=kevs if k == 0 else [f"ov0_{t}"], event=f"xoh{t}_{k}")
            s.add("vector", lambda v: v.tensor_mul(rscr[:], ohs[:], bbc[:]), deps=[f"xoh{t}_{k}"], event=f"xg0{t}_{k}")
            s.add("vector", lambda v: v.tensor_reduce(t1[:], rscr[:], axis=AX.X, op=ALU.add),
                  deps=[f"xg0{t}_{k}"], event=f"xg{t}_{k}")
            s.add("vector", lambda v, t=t, k=k: v.tensor_add(t3[:], t1[:], c_ka[:, t, k:k + 1]),
                  deps=[f"xg{t}_{k}"], event=f"xp{t}_{k}")
            if k == 0:
                s.add("vector", lambda v: v.tensor_scalar(t4[:], t3[:], float(CAPACITY - 1), None, ALU.is_le),
                      deps=[f"xp{t}_{k}"], event=f"ov0_{t}")
            else:
                s.add("vector", lambda v: v.tensor_scalar(t5[:], t3[:], float(CAPACITY - 1), None, ALU.is_le),
                      deps=[f"xp{t}_{k}"], event=f"ov1_{t}")
        s.add("vector", lambda v: v.tensor_mul(t4[:], t4[:], t5[:]), deps=[f"ov1_{t}", f"ov0_{t}"], event=f"ovm_{t}")
        s.add("vector", lambda v, t=t: v.tensor_scalar(ovf[:, t:t + 1], t4[:], 1.0, -1.0, ALU.subtract, ALU.mult),
              deps=[f"ovm_{t}"], event=f"ovff_{t}")
        s.add("vector", lambda v, t=t: v.tensor_copy(ovfu[:, t:t + 1], ovf[:, t:t + 1]),
              deps=[f"ovff_{t}"], event=f"ovf_{t}")
        s.add("scalar", lambda e, t=t: e.dma_start(out=x1s3[:], in_=x1[t * 128:(t + 1) * 128, :]),
              deps=[f"xsc{t-1}"] if t else [f"xcast{TILES_PC-1}"], dma=True, event=f"x1l3_{t}")
        s.add("vector", lambda v, t=t: v.tensor_copy(xb16[:], x1s3[:]),
              deps=[f"x1l3_{t}", f"xcast{TILES_PC-1}", f"dsts_{t}_1"] + ([f"scat_{t-1}_1"] if t else []),
              event=f"xsc{t}")
        for k in range(2):
            s.add("gpsimd", lambda g, t=t, k=k: g.indirect_dma_start(
                buckets.ap()[:, :], IndirectOffsetOnAxis(dsti[k][:, t:t + 1], 0), xb16[:],
                None, bounds_check=breg_h["r"], oob_is_err=False),
                deps=[f"xsc{t}", f"dsts_{t}_{k}", "breg"], dma=True, event=f"scat_{t}_{k}")
            scat_evs.append(f"scat_{t}_{k}")

    yst_evs = []
    prev_guT = None
    for e in range(nexp):
        eb = f"e{e}"
        hl_evs = []
        for dc in range(8):
            deps = scat_evs if (e == 0 and dc == 0) else (
                [f"hTs_done_e{e-1}", f"yTe{e-1}_1_7"] if dc == 0 else [])
            s.add("sync", lambda ee, e=e, dc=dc: ee.dma_start(
                out=hT[:, dc, :], in_=buckets.ap()[e * BUCKET:(e + 1) * BUCKET, dc * 128:(dc + 1) * 128],
                transpose=True),
                deps=deps, dma=True, event=f"hl{eb}_{dc}")
            hl_evs.append(f"hl{eb}_{dc}")
        for dc in range(8):
            s.add("vector", lambda v, dc=dc: v.tensor_mul(sqs[:], hT[:, dc, :], hT[:, dc, :]),
                  deps=[f"hl{eb}_{dc}"] + ([f"sq{eb}_{dc-1}e"] if dc else ([f"sqm{f'e{e-1}'}_7_1"] if e else [])),
                  event=f"sq{eb}_{dc}")
            for blk in range(2):
                pbank = pa if blk == 0 else pc_[0]
                s.add("tensor", lambda tt, dc=dc, blk=blk, pbank=pbank: tt.matmul(
                    pbank[0:1, 0:SBK], onescol[:, :], sqs[:, blk * SBK:(blk + 1) * SBK],
                    start=(dc == 0), stop=(dc == 7)),
                    deps=([f"sq{eb}_{dc}", "onesc"] +
                          ([f"bbce{TILES_PC-1}", f"bbce2{TILES_PC-1}"] if (e == 0 and dc == 0) else []) +
                          ([f"rmsrow{f'e{e-1}'}_{blk}"] if (e > 0 and dc == 0) else [])),
                    event=f"sqm{eb}_{dc}_{blk}" if dc == 7 else None)
            s.add("vector", lambda v: v.engine_nop(), deps=[f"sq{eb}_{dc}"], event=f"sq{eb}_{dc}e")
        for blk in range(2):
            pbank = pa if blk == 0 else pc_[0]
            s.add("vector", lambda v, blk=blk, pbank=pbank: v.tensor_scalar(
                rmsrow[0:1, blk * SBK:(blk + 1) * SBK], pbank[0:1, 0:SBK], 1.0 / D, RMSEPS, ALU.mult, ALU.add),
                deps=[f"sqm{eb}_{7}_{blk}"], event=f"rmsrow{eb}_{blk}")
        s.add("scalar", lambda ee: ee.activation(rmsrow[:], rmsrow[:], AF.Ln),
              deps=[f"rmsrow{eb}_0", f"rmsrow{eb}_1"], event=f"rln{eb}")
        s.add("scalar", lambda ee: ee.activation(rmsrow[:], rmsrow[:], AF.Exp, scale=-0.5),
              deps=[f"rln{eb}"], event=f"rexp{eb}")
        for blk in range(2):
            s.add("tensor", lambda tt, blk=blk: tt.matmul(
                pd_[blk][:, 0:SBK], onesrowF[:, :], rmsrow[0:1, blk * SBK:(blk + 1) * SBK],
                start=True, stop=True),
                deps=([f"rexp{eb}", "onesF"] +
                      ([f"cexe{TILES_PC-1}", f"cnte{TILES_PC-1}"] if e == 0 else
                       [f"rbc{f'e{e-1}'}_{blk}"])),
                event=f"rbm{eb}_{blk}")
            s.add("vector", lambda v, blk=blk: v.tensor_copy(rms_bc[:, blk * SBK:(blk + 1) * SBK], pd_[blk][:, 0:SBK]),
                  deps=[f"rbm{eb}_{blk}"], event=f"rbc{eb}_{blk}")
        for dc in range(8):
            s.add("vector", lambda v, dc=dc, e=e: v.scalar_tensor_tensor(
                hTs[:, dc, :], hT[:, dc, :], nw_sb[:, dc, e:e + 1], rms_bc[:, :], ALU.mult, ALU.mult),
                deps=[f"rbc{eb}_0", f"rbc{eb}_1", "cols_nw"] +
                     ([f"gmm{e-1}_1_15", f"umm{e-1}_1_15"] if (e and dc == 0) else []),
                event=f"hTs{eb}_{dc}")
        s.add("vector", lambda v: v.engine_nop(), deps=[f"hTs{eb}_{dc}" for dc in range(8)],
              event=f"hTs_done_e{e}")
        for half in range(2):
            hof = half * SBK
            for fc in range(16):
                kk = (e * 2 + half) * 16 + fc
                buf = kk % 2
                wdeps = [f"gue{e}_{half}_{fc-2}"] if fc >= 2 else (
                    [f"gue{e}_{half-1}_{14+fc}"] if half else (
                        [f"gue{e-1}_1_{14+fc}"] if e else []))
                s.add("gpsimd", lambda g, e=e, fc=fc, buf=buf: g.dma_start(
                    out=wgc[buf][:], in_=Wg[e, :, fc * 128:(fc + 1) * 128].rearrange("(dc p) q -> p dc q", p=128)),
                    deps=wdeps, dma=True, event=f"wgl{e}_{half}_{fc}")
                s.add("gpsimd", lambda g, e=e, fc=fc, buf=buf: g.dma_start(
                    out=wuc[buf][:], in_=Wu[e, :, fc * 128:(fc + 1) * 128].rearrange("(dc p) q -> p dc q", p=128)),
                    dma=True, event=f"wul{e}_{half}_{fc}")
                gdeps = [f"wgl{e}_{half}_{fc}", f"hTs_done_e{e}"] + (
                    [f"gge{e}_{half}_{fc-1}"] if fc else (
                        [f"gge{e}_{half-1}_15"] if half else (
                            [f"rmsrow{eb}_1"] + ([f"gge{e-1}_1_15"] if e else []))))
                for dc in range(8):
                    s.add("tensor", lambda tt, dc=dc, fc=fc, buf=buf, hof=hof: tt.matmul(
                        pc_[0][:, 0:SBK], wgc[buf][:, dc, :], hTs[:, dc, hof:hof + SBK],
                        start=(dc == 0), stop=(dc == 7)),
                        deps=gdeps if dc == 0 else [], event=f"gmm{e}_{half}_{fc}" if dc == 7 else None)
                s.add("scalar", lambda ee: ee.activation(gg[:], pc_[0][:, 0:SBK], AF.Gelu),
                      deps=[f"gmm{e}_{half}_{fc}"] + ([f"gue{e}_{half}_{fc-1}"] if fc else []),
                      event=f"gge{e}_{half}_{fc}")
                udeps = [f"wul{e}_{half}_{fc}", f"hTs_done_e{e}"] + (
                    [f"uue{e}_{half}_{fc-1}"] if fc else (
                        [f"uue{e}_{half-1}_15"] if half else [f"rbc{eb}_{1}"] + ([f"uue{e-1}_1_15"] if e else [])))
                for dc in range(8):
                    s.add("tensor", lambda tt, dc=dc, fc=fc, buf=buf, hof=hof: tt.matmul(
                        pc_[1][:, 0:SBK], wuc[buf][:, dc, :], hTs[:, dc, hof:hof + SBK],
                        start=(dc == 0), stop=(dc == 7)),
                        deps=udeps if dc == 0 else [], event=f"umm{e}_{half}_{fc}" if dc == 7 else None)
                s.add("vector", lambda v: v.tensor_copy(uu[:], pc_[1][:, 0:SBK]),
                      deps=[f"umm{e}_{half}_{fc}"] + ([f"gue{e}_{half}_{fc-1}"] if fc else []),
                      event=f"uue{e}_{half}_{fc}")
                s.add("vector", lambda v, fc=fc: v.tensor_mul(guT[:, fc, :], gg[:], uu[:]),
                      deps=[f"gge{e}_{half}_{fc}", f"uue{e}_{half}_{fc}"] +
                           ([] if fc else ([f"dmm{e}_{half-1}_7"] if half else
                                           ([f"dmm{e-1}_1_7"] if e else []))),
                      event=f"gue{e}_{half}_{fc}")
            for dc in range(8):
                kk = (e * 2 + half) * 8 + dc
                wdeps = [f"dmm{e}_{half}_{dc-1}"] if dc >= 1 else (
                    [f"dmm{e}_{half-1}_7"] if half else ([f"dmm{e-1}_1_7"] if e else []))
                s.add("gpsimd", lambda g, e=e, dc=dc: g.dma_start(
                    out=wdc[0][:], in_=Wd[e, :, dc * 128:(dc + 1) * 128].rearrange("(fc p) q -> p fc q", p=128)),
                    deps=wdeps, dma=True, event=f"wdl{e}_{half}_{dc}")
                ddeps = [f"wdl{e}_{half}_{dc}"] + [f"gue{e}_{half}_{fc}" for fc in range(16)] + (
                    [f"yTe{e}_{half}_{dc-1}"] if dc else (
                        [f"yTe{e}_{half-1}_7"] if half else (
                            [f"ytm_done_e{e-1}"] if e else [])))
                for fc in range(16):
                    s.add("tensor", lambda tt, fc=fc, hof=hof: tt.matmul(
                        pe_[0][:, 0:SBK], wdc[0][:, fc, :], guT[:, fc, :],
                        start=(fc == 0), stop=(fc == 15)),
                        deps=ddeps if fc == 0 else [], event=f"dmm{e}_{half}_{dc}" if fc == 15 else None)
                s.add("vector", lambda v, dc=dc, hof=hof: v.scalar_tensor_tensor(
                    yT[:, dc, hof:hof + SBK], pe_[0][:, 0:SBK], 1.0,
                    hT[:, dc, hof:hof + SBK], ALU.mult, ALU.add),
                    deps=[f"dmm{e}_{half}_{dc}"], event=f"yTe{e}_{half}_{dc}")
        for dc in range(8):
            s.add("vector", lambda v: v.engine_nop(),
                  deps=[f"yTe{e}_0_{dc}", f"yTe{e}_1_{dc}"], event=f"dme2_{e}_{dc}")
        for sb in range(BUCKET // 128):
            tdeps = [f"dme2_{e}_{dc}" for dc in range(8)] + ([f"ytme{e}_{sb-1}"] if sb else []) + \
                    ([f"xTe{TILES_PC-1}"] if e == 0 and sb == 0 else [])
            for dc in range(8):
                s.add("tensor", lambda tt, dc=dc, sb=sb: tt.transpose(
                    ptr[:, dc * 128:(dc + 1) * 128], yT[:, dc, sb * 128:(sb + 1) * 128], ident[:]),
                    deps=tdeps if dc == 0 else [], event=f"ytp{e}_{sb}" if dc == 7 else None)
            s.add("vector", lambda v: v.tensor_copy(ytm[:], ptr[:, :]),
                  deps=[f"ytp{e}_{sb}"] + ([f"yst{e}_{sb-1}"] if sb else
                       ([f"yst{e-1}_{BUCKET // 128 - 1}"] if e else [])),
                  event=f"ytme{e}_{sb}")
            s.add("scalar", lambda ee, e=e, sb=sb: ee.dma_start(
                out=ybuckets.ap()[e * BUCKET + sb * 128:e * BUCKET + (sb + 1) * 128, :], in_=ytm[:]),
                deps=[f"ytme{e}_{sb}"], dma=True, event=f"yst{e}_{sb}")
            yst_evs.append(f"yst{e}_{sb}")
        s.add("vector", lambda v: v.engine_nop(), deps=[f"ytme{e}_{sb}" for sb in range(BUCKET // 128)],
              event=f"ytm_done_e{e}")

    st_evs = []
    for blk in range(TPC // 512):
        fb = f"f{blk}"
        for dc in range(8):
            s.add("vector", lambda v, dc=dc, blk=blk: v.tensor_mul(
                fsqs[:], xT[:, dc, blk * 512:(blk + 1) * 512], xT[:, dc, blk * 512:(blk + 1) * 512]),
                deps=[f"xTe{TILES_PC-1}"] + ([f"fsqm{fb}_{dc-1}"] if dc else
                     ([f"rmsrowe{nexp-1}_0", f"frbcf{blk-1}"] if blk else [f"rmsrowe{nexp-1}_0"])),
                event=f"fsq{fb}_{dc}")
            s.add("tensor", lambda tt, dc=dc: tt.matmul(
                pa[0:1, 0:512], onescol[:, :], fsqs[:, :],
                start=(dc == 0), stop=(dc == 7)),
                deps=[f"fsq{fb}_{dc}", "onesc"] +
                     ([f"bbce{TILES_PC-1}"] if False else []),
                event=f"fsqm{fb}_{dc}")
        s.add("vector", lambda v: v.tensor_scalar(frms_row[:], pa[0:1, 0:512], 1.0 / D, RMSEPS, ALU.mult, ALU.add),
              deps=[f"fsqm{fb}_7"], event=f"frr{fb}")
        s.add("scalar", lambda ee: ee.activation(frms_row[:], frms_row[:], AF.Ln),
              deps=[f"frr{fb}"], event=f"frln{fb}")
        s.add("scalar", lambda ee: ee.activation(frms_row[:], frms_row[:], AF.Exp, scale=-0.5),
              deps=[f"frln{fb}"], event=f"frexp{fb}")
        s.add("tensor", lambda tt: tt.matmul(pa[:, 0:512], onesrowF[:, :], frms_row[:, :], start=True, stop=True),
              deps=[f"frexp{fb}", "onesF"], event=f"frbm{fb}")
        s.add("vector", lambda v: v.tensor_copy(frms_bc, pa[:, 0:512]),
              deps=[f"frbm{fb}", f"hTs_done_e{nexp-1}"], event=f"frbc{fb}")
        for dc in range(8):
            s.add("vector", lambda v, dc=dc, blk=blk: v.scalar_tensor_tensor(
                fhT[:, dc, :], xT[:, dc, blk * 512:(blk + 1) * 512],
                fnw_col[:, dc, 0:1], frms_bc, ALU.mult, ALU.mult),
                deps=[f"frbc{fb}", "cols_fnw"] + ([f"fdm{f'f{blk-1}'}_7"] if blk else []),
                event=f"fh{fb}_{dc}")
        fh_evs = [f"fh{fb}_{dc}" for dc in range(8)]
        for fc in range(4):
            gdeps = fh_evs + ["ld_fwg"] + ([f"fgge{fb}_{fc-1}"] if fc else
                     ([f"gge{nexp-1}_1_15"] if blk == 0 else [f"fgge{f'f{blk-1}'}_3"]))
            for dc in range(8):
                s.add("tensor", lambda tt, dc=dc, fc=fc, blk=blk: tt.matmul(
                    pc_[0][:, 0:512], fwg_sb[:, dc, fc, :], fhT[:, dc, :],
                    start=(dc == 0), stop=(dc == 7)),
                    deps=gdeps if dc == 0 else [], event=f"fgm{fb}_{fc}" if dc == 7 else None)
            s.add("scalar", lambda ee: ee.activation(fgg[:], pc_[0][:, 0:512], AF.Gelu),
                  deps=[f"fgm{fb}_{fc}"] + ([f"fgu{fb}_{fc-1}"] if fc else []), event=f"fgge{fb}_{fc}")
            udeps = fh_evs + ["ld_fwu"] + ([f"fuue{fb}_{fc-1}"] if fc else
                     ([f"uue{nexp-1}_1_15"] if blk == 0 else [f"fuue{f'f{blk-1}'}_3"]))
            for dc in range(8):
                s.add("tensor", lambda tt, dc=dc, fc=fc, blk=blk: tt.matmul(
                    pc_[1][:, 0:512], fwu_sb[:, dc, fc, :], fhT[:, dc, :],
                    start=(dc == 0), stop=(dc == 7)),
                    deps=udeps if dc == 0 else [], event=f"fum{fb}_{fc}" if dc == 7 else None)
            s.add("vector", lambda v: v.tensor_copy(fuu[:], pc_[1][:, 0:512]),
                  deps=[f"fum{fb}_{fc}"] + ([f"fgu{fb}_{fc-1}"] if fc else []), event=f"fuue{fb}_{fc}")
            s.add("vector", lambda v, fc=fc: v.tensor_mul(fgu[:, fc, :], fgg[:], fuu[:]),
                  deps=[f"fgge{fb}_{fc}", f"fuue{fb}_{fc}"], event=f"fgu{fb}_{fc}")
        for dc in range(8):
            ddeps = [f"fgu{fb}_{fc}" for fc in range(4)] + ["ld_fwd"] + \
                    ([f"fde{fb}_{dc-1}"] if dc else
                     ([f"dmm{nexp-1}_1_7"] if blk == 0 else [f"fde{f'f{blk-1}'}_7"]))
            for fc in range(4):
                s.add("tensor", lambda tt, dc=dc, fc=fc: tt.matmul(
                    pd_[0][:, 0:512], fwd_sb[:, fc, dc, :], fgu[:, fc, :],
                    start=(fc == 0), stop=(fc == 3)),
                    deps=ddeps if fc == 0 else [], event=f"fdm{fb}_{dc}" if fc == 3 else None)
            s.add("vector", lambda v, dc=dc, blk=blk: v.scalar_tensor_tensor(
                foutT[:, dc, :], pd_[0][:, 0:512], 1.0,
                xT[:, dc, blk * 512:(blk + 1) * 512], ALU.mult, ALU.add),
                deps=[f"fdm{fb}_{dc}"] + ([f"ftp{blk*4-1}"] if (blk and dc == 0) else []),
                event=f"fde{fb}_{dc}")
        fall_blk = [f"fde{fb}_{dc}" for dc in range(8)]

        for ti in range(4):
            t = blk * 4 + ti
            tb = f"c{t}"
            for k in range(2):
                s.add("gpsimd", lambda g, t=t, k=k: g.indirect_dma_start(
                    gy[k][:], None, ybuckets.ap()[:, :], IndirectOffsetOnAxis(dsti[k][:, t:t + 1], 0),
                    bounds_check=breg_h["r"], oob_is_err=False),
                    deps=(yst_evs if t == 0 else []) + [f"gyz{k}"] + ([f"cb{t-1}"] if t else []),
                    dma=True, event=f"gy{tb}_{k}")
            s.add("scalar", lambda e, t=t: e.dma_start(out=x1s6[:], in_=x1[t * 128:(t + 1) * 128, :]),
                  deps=[f"rfin{t-1}"] if t else [f"xsc{TILES_PC-1}"], dma=True, event=f"x1l6_{t}")
            s.add("vector", lambda v, t=t: v.scalar_tensor_tensor(
                comb[:], gy[0][:], wk[:, t, 0:1], x1s6[:], ALU.mult, ALU.add),
                deps=[f"gy{tb}_0", f"wkf{t}_0", f"x1l6_{t}"] + ([f"st_b{t-1}"] if t else []), event=f"cmb0{tb}")
            s.add("vector", lambda v, t=t: v.scalar_tensor_tensor(
                comb2[:], gy[1][:], wk[:, t, 1:2], comb[:], ALU.mult, ALU.add),
                deps=[f"gy{tb}_1", f"cmb0{tb}", f"wkf{t}_1"], event=f"cb{t}")
            for dc in range(8):
                s.add("tensor", lambda tt, ti=ti, dc=dc: tt.transpose(
                    ptr[:, dc * 128:(dc + 1) * 128], foutT[:, dc, ti * 128:(ti + 1) * 128], ident[:]),
                    deps=(fall_blk + ([f"ytm_done_e{nexp-1}"] if t == 0 else []) + ([f"ftme{t-1}"] if t else [])) if dc == 0 else [],
                    event=f"ftp{t}" if dc == 7 else None)
            s.add("vector", lambda v: v.tensor_copy(fout_tm[:], ptr[:, :]), deps=[f"ftp{t}"], event=f"ftme{t}")
            s.add("vector", lambda v, t=t: v.select(
                selo[:], ovfu[:, t:t + 1].to_broadcast([128, D]), fout_tm[:], comb2[:]),
                deps=[f"ftme{t}", f"cb{t}", f"ovf_{t}"], event=f"sel{t}")
            s.add("vector", lambda v, t=t: v.tensor_add(rfin[:], selo[:], x1s6[:]),
                  deps=[f"sel{t}"], event=f"rfin{t}")
            emit_layernorm(s, f"b{t}", rfin[:], outt[:], lnsc,
                           deps=[f"rfin{t}"] + ([f"st_b{t-1}"] if t else []),
                           store_fn=lambda e, t=t: e.dma_start(out=x2[t * 128:(t + 1) * 128, :], in_=outt[:]),
                           store_dep_event=f"st_b{t-1}" if t else None)
            st_evs.append(f"st_b{t}")

    if debug:
        dbg_scr = p.sb("dbg_scr", [128, D], F32)
        lastev = []
        for k in range(2):
            s.add("gpsimd", lambda g, k=k: g.dma_start(out=dbg_dst[k], in_=dsti[k][:, :]),
                  deps=st_evs, dma=True, event=f"dbgd{k}")
        s.add("gpsimd", lambda g: g.dma_start(out=dbg_wk[:, :, :], in_=wk[:, :, :]),
              deps=st_evs, dma=True, event="dbgwk")
        s.add("gpsimd", lambda g: g.dma_start(out=dbg_cnt[:, :], in_=cnt_own[:8, :]),
              deps=st_evs, dma=True, event="dbgcnt")
        s.add("gpsimd", lambda g: g.dma_start(out=dbg_bo[:, :], in_=base_o[:8, :]),
              deps=st_evs, dma=True, event="dbgbo")
        s.add("gpsimd", lambda g: g.dma_start(out=dbg_ovf[:, :], in_=ovf[:, :]),
              deps=st_evs, dma=True, event="dbgovf")
        for dc in range(8):
            s.add("vector", lambda v, dc=dc: v.tensor_copy(comb[:, 0:BUCKET], hT[:, dc, :]),
                  deps=st_evs + ([f"dbghs{dc-1}"] if dc else []), event=f"dbghv{dc}")
            s.add("gpsimd", lambda g, dc=dc: g.dma_start(out=dbg_hT[:, dc, :], in_=comb[:, 0:BUCKET]),
                  deps=[f"dbghv{dc}"], dma=True, event=f"dbgh{dc}")
            s.add("vector", lambda v, dc=dc: v.tensor_copy(comb[:, 0:BUCKET], hTs[:, dc, :]),
                  deps=[f"dbgh{dc}"], event=f"dbghsv{dc}")
            s.add("gpsimd", lambda g, dc=dc: g.dma_start(out=dbg_hTs[:, dc, :], in_=comb[:, 0:BUCKET]),
                  deps=[f"dbghsv{dc}"], dma=True, event=f"dbghs{dc}")
        s.add("vector", lambda v: v.tensor_copy(comb[:, 0:BUCKET], rms_bc[:, :]),
              deps=st_evs + ["dbghs7"], event="dbgrv")
        s.add("gpsimd", lambda g: g.dma_start(out=dbg_rms[:, :], in_=comb[:, 0:BUCKET]),
              deps=["dbgrv"], dma=True, event="dbgrm")
        for r in range(0, 8 * BUCKET, 128):
            ev1 = f"dbgb{r}"
            s.add("scalar", lambda e, r=r: e.dma_start(out=dbg_scr[:].bitcast(BF16)[:, 0:D], in_=buckets.ap()[r:r+128, :]),
                  deps=st_evs + ([f"dbgbc{r-128}"] if r else []), dma=True, event=f"dbgbl{r}")
            s.add("vector", lambda v: v.tensor_copy(outt[:], dbg_scr[:].bitcast(BF16)[:, 0:D]),
                  deps=[f"dbgbl{r}"], event=f"dbgbv{r}")
            s.add("scalar", lambda e, r=r: e.dma_start(out=dbg_bkt[r:r+128, :], in_=outt[:]),
                  deps=[f"dbgbv{r}"], dma=True, event=f"dbgbc{r}")
            s.add("scalar", lambda e, r=r: e.dma_start(out=dbg_scr[:].bitcast(BF16)[:, 0:D], in_=ybuckets.ap()[r:r+128, :]),
                  deps=[f"dbgbc{r}"], dma=True, event=f"dbgyl{r}")
            s.add("vector", lambda v: v.tensor_copy(outt[:], dbg_scr[:].bitcast(BF16)[:, 0:D]),
                  deps=[f"dbgyl{r}"], event=f"dbgyv{r}")
            s.add("scalar", lambda e, r=r: e.dma_start(out=dbg_ybkt[r:r+128, :], in_=outt[:]),
                  deps=[f"dbgyv{r}"], dma=True, event=f"dbgyc{r}")
            lastev.append(f"dbgyc{r}")
        s.finish(*(st_evs + lastev + ["dbgd0", "dbgd1", "dbgwk", "dbgcnt", "dbgbo", "dbgovf", "dbgrm"]))
    elif ctx is None:
        s.finish(*st_evs)
    if ctx is None:
        s.emit(block)
        return p
    return st_evs


def _phase_b_in_maps(inputs):
    x1full = np.ascontiguousarray(inputs["x1"], dtype=np.float32)
    maps = []
    for c in range(NCORES):
        m = dict(
            x1s=x1full[c * TPC:(c + 1) * TPC],
            Wr=inputs["L0_ffn_Wr"], norm_w=inputs["L0_ffn_norm_w"],
            Wg=inputs["L0_ffn_Wg"], Wu=inputs["L0_ffn_Wu"], Wd=inputs["L0_ffn_Wd"],
            f_norm=inputs["L0_ffn_f_norm"], fWg=inputs["L0_ffn_fWg"],
            fWu=inputs["L0_ffn_fWu"], fWd=inputs["L0_ffn_fWd"],
            prevmask=(np.arange(8) < c).astype(np.float32).reshape(1, 8),
        )
        maps.append({k: np.ascontiguousarray(v, np.float32) for k, v in m.items()})
    return maps


def run_phase_b(inputs, debug=False, nexp=8):
    p = build_phase_b(debug=debug, nexp=nexp)
    maps = _phase_b_in_maps(inputs)
    res = run_bass_kernel_spmd(p.nc, maps, core_ids=list(range(NCORES)))
    out = np.concatenate([res.results[c]["x2s"] for c in range(NCORES)], axis=0)
    if debug:
        return out, res.results
    return out




def build_phase_c(ctx=None):
    if ctx is None:
        p = Prog()
        nc = p.nc
        x2b = nc.dram_tensor("x2batch", [T, D], F32, kind="ExternalInput").ap()
        x2o = nc.dram_tensor("x2own", [TPC, D], F32, kind="ExternalInput").ap()
        x3 = nc.dram_tensor("x3s", [TPC, D], F32, kind="ExternalOutput").ap()
        _pfx = ""
    else:
        p, s = ctx["p"], ctx["s"]
        nc = p.nc
        x2b = ctx["x2b"]; x2o = ctx["x2o"]; x3 = ctx["x3"]
        _pfx = ctx["pfx"]

    xck_d = nc.dram_tensor(_pfx + "xck_d", [T, 16 * 128], BF16)
    vpad_d = nc.dram_tensor(_pfx + "vpad_d", [T, 16 * 65], BF16)
    negBT_d = nc.dram_tensor(_pfx + "negBT_d", [16, TPC], F32)
    zrows_d = nc.dram_tensor(_pfx + "zrows_d", [16, TPC], F32)

    if ctx is None:
        s = Sched(p)
        block = p.es.enter_context(nc.Block())

    NBT = T // 128
    NQT = 4
    NHG = 4
    NSMC = T // 512
    HPG = 4

    identf = p.sb("identf", [128, 128], F32)
    ident = p.sb("ident", [128, 128], BF16)
    identF = p.sb("identF", [128, 128], F32)
    onesrowF = p.sb("onesrowF", [1, 128], F32)
    onescolB = p.sb("onescolB", [128, 1], BF16)

    x2t = [p.sb(f"x2t{i}", [128, D], F32) for i in range(2)]
    xb16 = [p.sb(f"xb16{i}", [128, D], BF16) for i in range(2)]
    vint = p.sb("vint", [128, 16, 65], BF16)
    xint = p.sb("xint", [128, 16, 128], BF16)
    meanrow = p.sb("meanrow", [1, D], F32)
    mean_bc = p.sb("mean_bc", [128, D], F32)
    xcf = p.sb("xcf", [128, D], F32)
    xcb = p.sb("xcb", [128, D], BF16)
    sqb = p.sb("sqb", [128, D], BF16)
    n2t = p.sb("n2t", [128, 16], F32)
    nmax = p.sb("nmax", [128, 16], F32)
    n2own = p.sb("n2own", [128, TILES_PC, 16], F32)
    m2r = p.sb("m2r", [16, 1], F32)
    lnm2bc = p.sb("lnm2bc", [128, 16], F32)
    lnn2 = p.sb("lnn2", [128, 16], F32)
    bneg = p.sb("bneg", [128, 16], F32)
    negBT = p.sb("negBT", [16, TPC], F32)
    xcT = p.sb("xcT", [128, 8, TPC], BF16)

    qrhs = p.sb("qrhs", [65, HPG, 512], BF16)
    kbuf = [p.sb(f"kbuf{i}", [128, HPG, 512], BF16) for i in range(3)]
    vbuf = [p.sb(f"vbuf{i}", [128, 4, 16 * 65], BF16) for i in range(3)]
    Ebuf = [p.sb(f"Ebuf{i}", [128, 512], BF16) for i in range(4)]
    cstage = p.sb("cstage", [64, 512], BF16)
    ctxe = p.sb("ctxe", [65, 512], F32)
    zrows = p.sb("zrows", [16, TPC], F32)
    zinv = p.sb("zinv", [128, 16], F32)
    attn_tok = p.sb("attn_tok", [128, 4, D], F32)
    scl = p.sb("scl", [128, D], F32)
    rfin = p.sb("rfin", [128, D], F32)
    outt = p.sb("outt", [128, D], F32)
    lnsc = make_ln_scratch(p, "c")

    ptr = p.ps("ptr", [128, 1024], BF16)
    psc = [p.ps(f"psc{i}", [128, 512], F32) for i in range(3)]
    pctx = [p.ps(f"pctx{i}", [128, 512], F32) for i in range(4)]

    emit_setup_ident(s, identf, ident, identF)
    s.add("vector", lambda v: v.memset(onesrowF[:], 1.0), event="onesF")
    s.add("vector", lambda v: v.memset(onescolB[:], 1.0), event="onesB")
    s.add("vector", lambda v: v.memset(vint[:, :, 64:65], 1.0), event="vones")
    s.add("vector", lambda v: v.memset(xint[:, :, 64:128], 0.0), event="xz")
    s.add("vector", lambda v: v.memset(xint[:, :, 64:65], 1.0), deps=["xz"], event="xones")
    s.add("vector", lambda v: v.engine_nop(), event="kones0")
    s.add("vector", lambda v: v.engine_nop(), event="kones1")

    for bt in range(NBT):
        bi = bt % 2
        s.add("scalar", lambda e, bt=bt, bi=bi: e.dma_start(out=x2t[bi][:], in_=x2b[bt * 128:(bt + 1) * 128, :]),
              deps=[f"vpe{bt-2}"] if bt >= 2 else [], dma=True, event=f"p1l{bt}")
        s.add("vector", lambda v, bi=bi: v.tensor_copy(xb16[bi][:], x2t[bi][:]),
              deps=[f"p1l{bt}"] + ([f"msum{bt-1}_1"] if bt else []), event=f"p1c{bt}")
        for h2 in range(2):
            s.add("tensor", lambda t, bi=bi, h2=h2, bt=bt: t.matmul(
                psc[h2][0:1, 0:512], onescolB[:, :], xb16[bi][:, h2 * 512:(h2 + 1) * 512],
                start=(bt == 0), stop=(bt == NBT - 1)),
                deps=([f"p1c{bt}", "onesB"] if bt == 0 else [f"p1c{bt}"]) if h2 == 0 else [],
                event=f"msum{bt}_{h2}")
        s.add("vector", lambda v, bi=bi: v.tensor_copy(
            vint[:, :, 0:64], xb16[bi][:, :].rearrange("p (h j) -> p h j", j=64)),
            deps=[f"p1c{bt}", "vones"] + ([f"vps{bt-1}"] if bt else []), event=f"vpi{bt}")
        s.add("gpsimd", lambda g, bt=bt: g.dma_start(
            out=vpad_d.ap()[bt * 128:(bt + 1) * 128, :], in_=vint[:, :, :].rearrange("p h c -> p (h c)")),
            deps=[f"vpi{bt}"], dma=True, event=f"vps{bt}")
        s.add("vector", lambda v: v.engine_nop(), deps=[f"vpi{bt}"], event=f"vpe{bt}")
    s.add("vector", lambda v: v.tensor_scalar(meanrow[0:1, 0:512], psc[0][0:1, 0:512], 1.0 / T, None, ALU.mult),
          deps=[f"msum{NBT-1}_0", f"msum{NBT-1}_1"], event="mr0")
    s.add("vector", lambda v: v.tensor_scalar(meanrow[0:1, 512:1024], psc[1][0:1, 0:512], 1.0 / T, None, ALU.mult),
          deps=["mr0"], event="mr1")
    for h2 in range(2):
        s.add("tensor", lambda t, h2=h2: t.matmul(psc[h2][:, 0:512], onesrowF[:, :],
                                                  meanrow[0:1, h2 * 512:(h2 + 1) * 512], start=True, stop=True),
              deps=["mr1", "onesF"], event=f"mbm{h2}")
        s.add("vector", lambda v, h2=h2: v.tensor_copy(mean_bc[:, h2 * 512:(h2 + 1) * 512], psc[h2][:, 0:512]),
              deps=[f"mbm{h2}"], event=f"mbc{h2}")

    s.add("vector", lambda v: v.memset(nmax[:], 0.0), event="nm0")
    for bt in range(NBT):
        bi = bt % 2
        s.add("scalar", lambda e, bt=bt, bi=bi: e.dma_start(out=x2t[bi][:], in_=x2b[bt * 128:(bt + 1) * 128, :]),
              deps=[f"nrm{bt-2}"] if bt >= 2 else ["mbc0", "mbc1"], dma=True, event=f"p2l{bt}")
        s.add("vector", lambda v, bi=bi: v.tensor_sub(xcf[:], x2t[bi][:], mean_bc[:]),
              deps=[f"p2l{bt}"] + ([f"xcks{bt-1}"] if bt else ["mbc0", "mbc1"]), event=f"xc{bt}")
        s.add("vector", lambda v: v.tensor_copy(xcb[:], xcf[:]), deps=[f"xc{bt}"], event=f"xcb{bt}")
        s.add("vector", lambda v, bt=bt: v.tensor_copy(
            xint[:, :, 0:64], xcb[:, :].rearrange("p (h j) -> p h j", j=64)),
            deps=[f"xcb{bt}", "xones"] + ([f"xcks{bt-1}"] if bt else []), event=f"xci{bt}")
        s.add("gpsimd", lambda g, bt=bt: g.dma_start(
            out=xck_d.ap()[bt * 128:(bt + 1) * 128, :], in_=xint[:, :, :].rearrange("p h c -> p (h c)")),
            deps=[f"xci{bt}"], dma=True, event=f"xcks{bt}")
        s.add("vector", lambda v: v.tensor_mul(sqb[:], xcb[:], xcb[:]), deps=[f"xcb{bt}"], event=f"sq{bt}")
        s.add("vector", lambda v: v.tensor_reduce(n2t[:], sqb[:].rearrange("p (h j) -> p h j", j=64),
                                                  axis=AX.X, op=ALU.add),
              deps=[f"sq{bt}"], event=f"n2{bt}")
        s.add("vector", lambda v: v.tensor_max(nmax[:], nmax[:], n2t[:]),
              deps=[f"n2{bt}", "nm0"] + ([f"nrm{bt-1}"] if bt else []), event=f"nrm{bt}")
    s.add("tensor", lambda t: t.transpose(psc[0][0:16, 0:128], nmax[:, :], identF[:]),
          deps=[f"nrm{NBT-1}", "identF", "mbc0"], event="nmt")
    s.add("vector", lambda v: v.tensor_reduce(m2r[:16, :], psc[0][0:16, 0:128], axis=AX.X, op=ALU.max),
          deps=["nmt"], event="m2r")
    s.add("scalar", lambda e: e.activation(m2r[:16, :], m2r[:16, :], AF.Ln), deps=["m2r"], event="m2ln")
    s.add("tensor", lambda t: t.transpose(psc[0][0:1, 0:16], m2r[:16, 0:1], identF[:16, :16]),
          deps=["m2ln", "m2r"], event="m2t")
    s.add("vector", lambda v: v.tensor_copy(meanrow[0:1, 0:16], psc[0][0:1, 0:16]), deps=["m2t"], event="m2row")
    s.add("tensor", lambda t: t.matmul(psc[0][:, 0:16], onesrowF[:, :], meanrow[0:1, 0:16], start=True, stop=True),
          deps=["m2row", "onesF"], event="m2bm")
    s.add("vector", lambda v: v.tensor_copy(lnm2bc[:], psc[0][:, 0:16]), deps=["m2bm"], event="m2bc")

    for t0 in range(TILES_PC):
        bi = t0 % 2
        s.add("scalar", lambda e, t0=t0, bi=bi: e.dma_start(out=x2t[bi][:], in_=x2o[t0 * 128:(t0 + 1) * 128, :]),
              deps=[f"otp{t0-2}"] if t0 >= 2 else [f"p2l{NBT-1}"], dma=True, event=f"ol{t0}")
        s.add("vector", lambda v, bi=bi: v.tensor_sub(xcf[:], x2t[bi][:], mean_bc[:]),
              deps=[f"ol{t0}", f"nrm{NBT-1}"] + ([f"oxcb{t0-1}"] if t0 else [f"xcks{NBT-1}"]), event=f"oxc{t0}")
        s.add("vector", lambda v: v.tensor_copy(xcb[:], xcf[:]), deps=[f"oxc{t0}"], event=f"oxcb{t0}")
        for dc in range(8):
            s.add("tensor", lambda tt, dc=dc: tt.transpose(
                ptr[:, dc * 128:(dc + 1) * 128], xcb[:, dc * 128:(dc + 1) * 128], ident[:]),
                deps=([f"oxcb{t0}", "ident"] + ([f"oxT{t0-1}"] if t0 else [])) if dc == 0 else [],
                event=f"otp{t0}" if dc == 7 else None)
        s.add("vector", lambda v, t0=t0: v.tensor_copy(
            xcT[:, :, t0 * 128:(t0 + 1) * 128], ptr[:, :].rearrange("p (dc q) -> p dc q", q=128)),
            deps=[f"otp{t0}"], event=f"oxT{t0}")
        s.add("vector", lambda v: v.tensor_mul(sqb[:], xcb[:], xcb[:]), deps=[f"oxcb{t0}"], event=f"osq{t0}")
        s.add("vector", lambda v, t0=t0: v.tensor_reduce(n2own[:, t0, :], sqb[:].rearrange("p (h j) -> p h j", j=64),
                                                         axis=AX.X, op=ALU.add),
              deps=[f"osq{t0}"], event=f"on2{t0}")
    for t0 in range(TILES_PC):
        s.add("scalar", lambda e, t0=t0: e.activation(lnn2[:], n2own[:, t0, :], AF.Ln),
              deps=[f"on2{t0}", "m2bc"] + ([f"bT{t0-1}"] if t0 else []), event=f"bln{t0}")
        s.add("vector", lambda v: v.tensor_add(lnn2[:], lnn2[:], lnm2bc[:]), deps=[f"bln{t0}"], event=f"badd{t0}")
        s.add("scalar", lambda e: e.activation(bneg[:], lnn2[:], AF.Exp, scale=0.5), deps=[f"badd{t0}"], event=f"bexp{t0}")
        s.add("vector", lambda v: v.tensor_scalar(bneg[:], bneg[:], -1.0, None, ALU.mult),
              deps=[f"bexp{t0}"], event=f"bnegv{t0}")
        s.add("tensor", lambda tt, t0=t0: tt.transpose(psc[0][0:16, 0:128], bneg[:, :], identF[:]),
              deps=[f"bnegv{t0}"] + ([f"bT{t0-1}"] if t0 else ["m2bc"]), event=f"bt{t0}")
        s.add("vector", lambda v, t0=t0: v.tensor_copy(negBT[:16, t0 * 128:(t0 + 1) * 128], psc[0][0:16, 0:128]),
              deps=[f"bt{t0}"], event=f"bTv{t0}")
        s.add("gpsimd", lambda g, t0=t0: g.dma_start(
            out=negBT_d.ap()[:, t0 * 128:(t0 + 1) * 128], in_=negBT[:16, t0 * 128:(t0 + 1) * 128]),
            deps=[f"bTv{t0}"], dma=True, event=f"bT{t0}")

    ev_store = []
    last_psc = [f"bT{TILES_PC-1}", "mbc1", None]
    last_ptr = f"oxT{TILES_PC-1}"
    last_ctx_evac = [None] * HPG
    last_E_reader = [None] * 4
    last_qrhs = None
    prev_qb_c = None
    last_attn_reader = None
    for qt in range(NQT):
        zs_evs = []
        for hg in range(NHG):
            heads = [hg * HPG + i for i in range(HPG)]
            qb = f"q{qt}h{hg}"
            qdeps = [f"oxT{TILES_PC-1}", f"bT{TILES_PC-1}"]
            if last_qrhs:
                qdeps.append(last_qrhs)
            for i, h in enumerate(heads):
                s.add("vector", lambda v, h=h, i=i, qt=qt: v.tensor_copy(
                    qrhs[0:64, i, :],
                    xcT[(h % 2) * 64:(h % 2) * 64 + 64, h // 2, qt * 512:(qt + 1) * 512]),
                    deps=qdeps if i == 0 else [], event=f"qr{qb}_{i}")
            s.add("gpsimd", lambda g, qt=qt, h0=heads[0]: g.dma_start(
                out=qrhs[64:65, :, :], in_=negBT_d.ap()[h0:h0 + HPG, qt * 512:(qt + 1) * 512]),
                deps=[f"qr{qb}_{i}" for i in range(HPG)], dma=True, event=f"qrb{qb}")
            ev_q = f"qrb{qb}"
            for smc in range(NSMC):
                bi = smc % 3
                kdeps = [ev_q, f"xcks{NBT-1}", "kones0", "kones1"]
                if smc >= 3:
                    kdeps.append(f"iter{qb}_{smc-3}")
                elif prev_qb_c is not None:
                    kdeps.append(f"iter{prev_qb_c}_{NSMC-3+smc}")
                for i, h in enumerate(heads):
                    s.add("sync", lambda e, h=h, i=i, smc=smc, bi=bi: e.dma_start(
                        out=kbuf[bi][:, i, :], in_=xck_d.ap()[smc * 512:(smc + 1) * 512, h * 128:(h + 1) * 128],
                        transpose=True),
                        deps=kdeps if i == 0 else [], dma=True, event=f"kl{qb}_{smc}_{i}")
                s.add("sync", lambda e, smc=smc, bi=bi: e.dma_start(
                    out=vbuf[bi][:, :, :],
                    in_=vpad_d.ap()[smc * 512:(smc + 1) * 512, :].rearrange("(a p) c -> p a c", p=128)),
                    deps=[f"vps{NBT-1}"] + kdeps[-1:] if smc >= 3 or prev_qb_c is not None else [f"vps{NBT-1}"],
                    dma=True, event=f"vl{qb}_{smc}")
                iter_evs = []
                for sc in range(4):
                    for i, h in enumerate(heads):
                        it = smc * 4 * HPG + sc * HPG + i
                        pp = it % 3
                        pq = it % 4
                        mdeps = [f"kl{qb}_{smc}_{HPG-1}", f"vl{qb}_{smc}"]
                        if last_psc[pp]:
                            mdeps.append(last_psc[pp])
                        s.add("tensor", lambda tt, i=i, sc=sc, bi=bi, pp=pp: tt.matmul(
                            psc[pp][:, :], kbuf[bi][0:65, i, sc * 128:(sc + 1) * 128], qrhs[:, i, :],
                            start=True, stop=True),
                            deps=mdeps, event=f"smm{qb}_{smc}_{sc}_{i}")
                        edeps = [f"smm{qb}_{smc}_{sc}_{i}"]
                        if last_E_reader[pq]:
                            edeps.append(last_E_reader[pq])
                        s.add("scalar", lambda e, pp=pp, pq=pq: e.activation(Ebuf[pq][:], psc[pp][:, :], AF.Exp),
                              deps=edeps, event=f"exp{qb}_{smc}_{sc}_{i}")
                        last_psc[pp] = f"exp{qb}_{smc}_{sc}_{i}"
                        first = (smc == 0 and sc == 0)
                        last = (smc == NSMC - 1 and sc == 3)
                        cdeps = [f"exp{qb}_{smc}_{sc}_{i}"]
                        if first and last_ctx_evac[i]:
                            cdeps.append(last_ctx_evac[i])
                        s.add("tensor", lambda tt, i=i, h=h, sc=sc, bi=bi, pq=pq, first=first, last=last: tt.matmul(
                            pctx[i][0:65, :], vbuf[bi][:, sc, h * 65:(h + 1) * 65], Ebuf[pq][:, :],
                            start=first, stop=last),
                            deps=cdeps, event=f"cmm{qb}_{smc}_{sc}_{i}")
                        last_E_reader[pq] = f"cmm{qb}_{smc}_{sc}_{i}"
                        iter_evs.append(f"cmm{qb}_{smc}_{sc}_{i}")
                s.add("vector", lambda v: v.engine_nop(), deps=[iter_evs[-1], f"exp{qb}_{smc}_{3}_{HPG-1}"],
                      event=f"iter{qb}_{smc}")
            for i, h in enumerate(heads):
                edeps = [f"cmm{qb}_{NSMC-1}_{3}_{i}"] + ([f"zs{qb}_{i-1}"] if i else [])
                s.add("vector", lambda v, i=i: v.tensor_copy(ctxe[:, :], pctx[i][0:65, :]),
                      deps=edeps, event=f"ce0_{qb}_{i}")
                last_ctx_evac[i] = f"ce0_{qb}_{i}"
                s.add("gpsimd", lambda g, h=h, qt=qt: g.dma_start(
                    out=zrows_d.ap()[h:h + 1, qt * 512:(qt + 1) * 512], in_=ctxe[64:65, :]),
                    deps=[f"ce0_{qb}_{i}"], dma=True, event=f"zs{qb}_{i}")
                s.add("vector", lambda v: v.tensor_copy(cstage[:, :], ctxe[0:64, :]),
                      deps=[f"ce0_{qb}_{i}"] + ([f"ctp_{qb}_{i-1}_3"] if i else []), event=f"cbf{qb}_{i}")
                for ti in range(4):
                    tdeps = [f"cbf{qb}_{i}"]
                    if ti == 0:
                        tdeps.append(last_ptr)
                        if last_attn_reader:
                            tdeps.append(last_attn_reader)
                    else:
                        tdeps.append(f"cte_{qb}_{i}_{ti-1}")
                    s.add("tensor", lambda tt, ti=ti: tt.transpose(
                        ptr[:, 0:64], cstage[:, ti * 128:(ti + 1) * 128], ident[:64, :64]),
                        deps=tdeps, event=f"ctp_{qb}_{i}_{ti}")
                    s.add("vector", lambda v, i=i, h=h, ti=ti: v.tensor_copy(
                        attn_tok[:, ti, h * 64:(h + 1) * 64], ptr[:, 0:64]),
                        deps=[f"ctp_{qb}_{i}_{ti}"], event=f"cte_{qb}_{i}_{ti}")
                last_ptr = f"cte_{qb}_{i}_{3}"
            zs_evs.append(f"zs{qb}_{HPG-1}")
            last_qrhs = f"smm{qb}_{NSMC-1}_{3}_{HPG-1}"
            prev_qb_c = qb
        ep_evs = []
        for ti in range(4):
            t0 = qt * 4 + ti
            tb = f"ep{t0}"
            if ti == 0:
                s.add("gpsimd", lambda g, qt=qt: g.dma_start(
                    out=zrows[:16, qt * 512:(qt + 1) * 512], in_=zrows_d.ap()[:, qt * 512:(qt + 1) * 512]),
                    deps=zs_evs, dma=True, event=f"zld{qt}")
            ztdeps = [f"zld{qt}", f"cte_q{qt}h{NHG-1}_{HPG-1}_3"] + [last_psc[0]]
            if ti:
                ztdeps.append(f"zi{qt}_{ti-1}")
            s.add("tensor", lambda tt, t0=t0: tt.transpose(
                psc[0][0:128, 0:16], zrows[:16, t0 * 128:(t0 + 1) * 128], identF[:16, :16]),
                deps=ztdeps, event=f"zt{tb}")
            s.add("vector", lambda v: v.tensor_copy(zinv[:], psc[0][0:128, 0:16]), deps=[f"zt{tb}"], event=f"zc{tb}")
            last_psc[0] = f"zc{tb}"
            s.add("vector", lambda v: v.reciprocal(zinv[:], zinv[:]), deps=[f"zc{tb}"], event=f"zi{qt}_{ti}")
            s.add("scalar", lambda e, t0=t0: e.dma_start(out=x2t[0][:], in_=x2o[t0 * 128:(t0 + 1) * 128, :]),
                  deps=[f"ol{TILES_PC-1}"] + ([f"rf{t0-1}"] if t0 else []), dma=True, event=f"el{t0}")
            s.add("vector", lambda v, ti=ti: v.tensor_mul(
                scl[:].rearrange("p (h j) -> p h j", j=64), attn_tok[:, ti, :].rearrange("p (h j) -> p h j", j=64),
                zinv[:, :, None].to_broadcast([128, 16, 64])),
                deps=[f"zi{qt}_{ti}"], event=f"sc{t0}")
            ep_evs.append(f"sc{t0}")
            s.add("vector", lambda v: v.tensor_add(rfin[:], scl[:], x2t[0][:]),
                  deps=[f"sc{t0}", f"el{t0}"], event=f"rf{t0}")
            emit_layernorm(s, f"c{t0}", rfin[:], outt[:], lnsc,
                           deps=[f"rf{t0}"] + ([f"st_c{t0-1}"] if t0 else []),
                           store_fn=lambda e, t0=t0: e.dma_start(out=x3[t0 * 128:(t0 + 1) * 128, :], in_=outt[:]),
                           store_dep_event=f"st_c{t0-1}" if t0 else None)
            ev_store.append(f"st_c{t0}")
        last_attn_reader = f"sc{qt * 4 + 3}"

    if ctx is None:
        s.finish(*ev_store)
        s.emit(block)
        return p
    return ev_store


def _phase_c_in_maps(inputs):
    x2full = np.ascontiguousarray(inputs["x2"], dtype=np.float32)
    maps = []
    for c in range(NCORES):
        b = c // 4
        m = dict(
            x2batch=x2full[b * T:(b + 1) * T],
            x2own=x2full[c * TPC:(c + 1) * TPC],
        )
        maps.append({k2: np.ascontiguousarray(v, np.float32) for k2, v in m.items()})
    return maps


def run_phase_c(inputs):
    p = build_phase_c()
    maps = _phase_c_in_maps(inputs)
    res = run_bass_kernel_spmd(p.nc, maps, core_ids=list(range(NCORES)))
    return np.concatenate([res.results[c]["x3s"] for c in range(NCORES)], axis=0)




NEGC = -1e30
NEGZ = -2e30


def build_phase_a(ctx=None):
    if ctx is None:
        p = Prog()
        nc = p.nc
        x0b = nc.dram_tensor("x0batch", [T, D], F32, kind="ExternalInput").ap()
        x0o = nc.dram_tensor("x0own", [TPC, D], F32, kind="ExternalInput").ap()
        qoff = nc.dram_tensor("qoff", [1, 1], F32, kind="ExternalInput").ap()
        Wq = nc.dram_tensor("Wq", [D, D], F32, kind="ExternalInput").ap()
        Wk = nc.dram_tensor("Wk", [D, D], F32, kind="ExternalInput").ap()
        Wv = nc.dram_tensor("Wv", [D, D], F32, kind="ExternalInput").ap()
        Wo = nc.dram_tensor("Wo", [D, D], F32, kind="ExternalInput").ap()
        bo = nc.dram_tensor("bo", [D], F32, kind="ExternalInput").ap()
        Wqi = nc.dram_tensor("Wqi", [D, 256], F32, kind="ExternalInput").ap()
        Wki = nc.dram_tensor("Wki", [D, 64], F32, kind="ExternalInput").ap()
        Ww = nc.dram_tensor("Ww", [D, 4], F32, kind="ExternalInput").ap()
        x1 = nc.dram_tensor("x1s", [TPC, D], F32, kind="ExternalOutput").ap()
        _pfx = ""
    else:
        p, s = ctx["p"], ctx["s"]
        nc = p.nc
        x0b = ctx["x0b"]; x0o = ctx["x0o"]; qoff = ctx["qoff"]
        Wq = ctx["Wq"]; Wk = ctx["Wk"]; Wv = ctx["Wv"]; Wo = ctx["Wo"]
        bo = ctx["bo"]; Wqi = ctx["Wqi"]; Wki = ctx["Wki"]; Ww = ctx["Ww"]
        x1 = ctx["x1"]
        _pfx = ctx["pfx"]

    kT_d = nc.dram_tensor(_pfx + "kT_d", [D, T], BF16)
    vpad_d = nc.dram_tensor(_pfx + "vpad_d", [T, 16 * 65], BF16)
    mT_d = nc.dram_tensor(_pfx + "mT_d", [T, TPC], BF16)
    qT_d = nc.dram_tensor(_pfx + "qT_d", [D, TPC], BF16)

    if ctx is None:
        s = Sched(p)
        block = p.es.enter_context(nc.Block())

    NQT = 4
    NHG = 4
    HPG = 4
    NSMC = T // 512
    NGRP = T // 512

    identf = p.sb("identf", [128, 128], F32)
    ident = p.sb("ident", [128, 128], BF16)
    identF = p.sb("identF", [128, 128], F32)
    onesrowF = p.sb("onesrowF", [1, 128], F32)

    wbig1 = p.sb("wbig1", [128, 8, 8, 128], BF16)
    wbig2 = p.sb("wbig2", [128, 8, 2, 512], BF16)
    wq_sb = wbig1
    wk_sb = wbig1
    wv_sb = wbig2
    wo_sb = wbig2[:, :, :, :].rearrange("p dc a q -> p dc (a q)").rearrange("p dc (jc j) -> p dc jc j", j=128)
    wqi_sb = p.sb("wqi_sb", [128, 8, 256], BF16)
    wki_sb = p.sb("wki_sb", [128, 8, 64], BF16)
    ww_sb = p.sb("ww_sb", [128, 8, 4], BF16)
    bo_row = p.sb("bo_row", [1, D], F32)
    bo_col = p.sb("bo_col", [128, 8, 1], F32)

    _x0t = p.sb("x0t0", [128, D], F32)
    x0t = [_x0t, _x0t]
    _xb16 = p.sb("xb160", [128, D], BF16)
    xb16 = [_xb16, _xb16]
    x0T = p.sb("x0T", [128, 8, 512], BF16)
    istage = p.sb("istage", [64, 4, 128], BF16)
    qstage = p.sb("qstage", [64, 4, 512], BF16)
    vint = p.sb("vint", [128, 16, 65], BF16)
    kevac = p.sb("kevac", [128, 512], BF16)
    vevac = p.sb("vevac", [128, 1024], BF16)
    kiT = p.sb("kiT", [64, T], BF16)

    wqiT = p.sb("wqiT", [128, 2, TPC], BF16)
    wrow = p.sb("wrow", [128, 4], F32)
    wexp = p.sb("wexp", [128, 4], F32)
    wm = p.sb("wm", [128, 1], F32)
    wsum = p.sb("wsum", [128, 1], F32)
    qi_tok = p.sb("qi_tok", [128, 256], F32)
    qi_scl = p.sb("qi_scl", [128, 256], BF16)
    qoffc = p.sb("qoffc", [128, 1], F32)
    rowpos = p.sb("rowpos", [128, 1], F32)
    tilepos = p.sb("tilepos", [128, 1], F32)
    iota512 = p.sb("iota512", [128, 512], F32)
    cmask = p.sb("cmask", [128, 512], F32)

    idx = p.sb("idx", [128, T], F32)
    m8 = p.sb("m8", [128, 8], F32)
    msb = p.sb("msb", [128, 512], BF16)
    mtr = p.sb("mtr", [128, 4, 128], BF16)
    scomb = [p.sb(f"scomb{i}", [128, 512], F32) for i in range(2)]

    kbuf = [p.sb(f"kbuf{i}", [64, HPG, 512], BF16) for i in range(3)]
    vbuf = [p.sb(f"vbuf{i}", [128, 4, 16 * 65], BF16) for i in range(3)]
    mbuf = [p.sb(f"mbuf{i}", [128, 4, 512], BF16) for i in range(3)]
    Ebuf = [p.sb(f"Ebuf{i}", [128, 512], BF16) for i in range(4)]
    Pbuf = [p.sb(f"Pbuf{i}", [128, 512], BF16) for i in range(4)]
    ctxe = p.sb("ctxe", [65, 512], F32)
    yT_sb = wqiT[:, :, :].rearrange("p a q -> p (a q)").rearrange("p (jc j) -> p jc j", j=512)
    attnT = x0T
    zrow = scomb[0]
    zbcs = cmask
    rfin = idx[:, 0:D]
    outt = idx[:, D:2 * D]
    lnsc = dict(
        lnsq=idx[:, 2 * D:3 * D],
        m=p.sb("alnm", [128, 1], F32), s2=p.sb("alns2", [128, 1], F32),
        mu=p.sb("alnmu", [128, 1], F32), var=p.sb("alnvar", [128, 1], F32),
        rstd=p.sb("alnrstd", [128, 1], F32), tmp=p.sb("alntmp", [128, 1], F32))
    ytok = idx[:, 3 * D:4 * D]

    ptr = p.ps("ptr", [128, 1024], BF16)
    psc = [p.ps(f"psc{i}", [128, 512], F32) for i in range(2)]
    pctx = [p.ps(f"pctx{i}", [128, 512], F32) for i in range(4)]
    pz = p.ps("pz", [128, 512], F32)

    emit_setup_ident(s, identf, ident, identF)
    s.add("vector", lambda v: v.memset(onesrowF[:], 1.0), event="onesF")
    s.add("vector", lambda v: v.memset(vint[:, :, 64:65], 1.0), event="vones")
    s.add("gpsimd", lambda g: g.iota(iota512[:], pattern=[[1, 512]], base=0, channel_multiplier=0,
                                     allow_small_or_imprecise_dtypes=True), event="io512")
    s.add("gpsimd", lambda g: g.iota(rowpos[:], pattern=[[0, 1]], base=0, channel_multiplier=1,
                                     allow_small_or_imprecise_dtypes=True), event="iorow")
    s.add("gpsimd", lambda g: g.dma_start(out=qoffc[0:1, :], in_=qoff[:, :]), dma=True, event="ld_qoff")
    s.add("gpsimd", lambda e: e.dma_start(out=wk_sb[:], in_=Wk.rearrange("(dc p) (jc j) -> p dc jc j", p=128, j=128)),
          dma=True, event="ld_wk")
    s.add("gpsimd", lambda e: e.dma_start(out=wv_sb[:], in_=Wv.rearrange("(dc p) (jh j) -> p dc jh j", p=128, j=512)),
          dma=True, event="ld_wv")
    s.add("gpsimd", lambda e: e.dma_start(out=wqi_sb[:], in_=Wqi.rearrange("(dc p) j -> p dc j", p=128)),
          dma=True, event="ld_wqi")
    s.add("gpsimd", lambda e: e.dma_start(out=wki_sb[:], in_=Wki.rearrange("(dc p) j -> p dc j", p=128)),
          dma=True, event="ld_wki")
    s.add("gpsimd", lambda e: e.dma_start(out=ww_sb[:], in_=Ww.rearrange("(dc p) j -> p dc j", p=128)),
          dma=True, event="ld_ww")
    s.add("sync", lambda e: e.dma_start(out=bo_row[:], in_=bo[None, :]), dma=True, event="ld_bor")
    emit_rows_to_cols(s, "bo", bo_row, 1, 8, bo_col, identF, pz, "ld_bor")
    s.add("tensor", lambda t: t.matmul(pz[:, 0:1], onesrowF[:, :], qoffc[0:1, 0:1], start=True, stop=True),
          deps=["ld_qoff", "onesF", "cols_bo"], event="qob_mm")
    s.add("vector", lambda v: v.tensor_copy(qoffc[:], pz[:, 0:1]), deps=["qob_mm"], event="qob")

    last_psc_pre = [None, None]
    last_pctx_pre = [None, None]
    for gp in range(NGRP):
        gb = f"g{gp}"
        shdone = []
        for ti in range(4):
            bt = gp * 4 + ti
            bi = ti % 2
            prevg = f"g{gp-1}"
            s.add("scalar", lambda e, bt=bt, bi=bi: e.dma_start(out=x0t[bi][:], in_=x0b[bt * 128:(bt + 1) * 128, :]),
                  deps=[f"pac{gb}_{ti-1}"] if ti >= 1 else ([f"pac{prevg}_3"] if gp else []),
                  dma=True, event=f"pal{gb}_{ti}")
            cdep = [f"pal{gb}_{ti}"]
            if ti:
                cdep.append(f"patp{gb}_{ti-1}")
            elif gp:
                cdep.append(f"patp{prevg}_3")
            s.add("vector", lambda v, bi=bi: v.tensor_copy(xb16[bi][:], x0t[bi][:]),
                  deps=cdep, event=f"pac{gb}_{ti}")
            tdep = [f"pac{gb}_{ti}", "ident"]
            if ti:
                tdep.append(f"pashTe{gb}_{ti-1}")
            elif gp:
                tdep.append(f"pashTe{prevg}_3")
            for dc in range(8):
                s.add("tensor", lambda tt, bi=bi, dc=dc: tt.transpose(
                    ptr[:, dc * 128:(dc + 1) * 128], xb16[bi][:, dc * 128:(dc + 1) * 128], ident[:]),
                    deps=tdep if dc == 0 else [], event=f"patp{gb}_{ti}" if dc == 7 else None)
            xdep = [f"patp{gb}_{ti}"]
            if gp and ti == 0:
                xdep += [f"kimm{prevg}", f"vmm{prevg}_3_1"]
            s.add("vector", lambda v, ti=ti: v.tensor_copy(
                x0T[:, :, ti * 128:(ti + 1) * 128], ptr[:, :].rearrange("p (dc q) -> p dc q", q=128)),
                deps=xdep, event=f"pashTe{gb}_{ti}")
            shdone.append(f"pashTe{gb}_{ti}")
        for jc in range(8):
            pk = jc % 2
            kdeps = shdone + ["ld_wk"]
            if last_psc_pre[pk]:
                kdeps.append(last_psc_pre[pk])
            for dc in range(8):
                s.add("tensor", lambda tt, jc=jc, dc=dc, pk=pk: tt.matmul(
                    psc[pk][:, :], wk_sb[:, dc, jc, :], x0T[:, dc, :],
                    start=(dc == 0), stop=(dc == 7)),
                    deps=kdeps if dc == 0 else [], event=f"kmm{gb}_{jc}" if dc == 7 else None)
            s.add("vector", lambda v, pk=pk: v.tensor_copy(kevac[:], psc[pk][:, :]),
                  deps=[f"kmm{gb}_{jc}"] + ([f"kst{gb}_{jc-1}"] if jc else ([f"kst{f'g{gp-1}'}_7"] if gp else [])),
                  event=f"kev{gb}_{jc}")
            last_psc_pre[pk] = f"kev{gb}_{jc}"
            s.add("gpsimd", lambda g, gp=gp, jc=jc: g.dma_start(
                out=kT_d.ap()[jc * 128:(jc + 1) * 128, gp * 512:(gp + 1) * 512], in_=kevac[:]),
                deps=[f"kev{gb}_{jc}"], dma=True, event=f"kst{gb}_{jc}")
        for ti in range(4):
            for jh in range(2):
                pk = jh
                vdeps = [f"pashTe{gb}_{ti}", "ld_wv"]
                if last_pctx_pre[pk]:
                    vdeps.append(last_pctx_pre[pk])
                for dc in range(8):
                    s.add("tensor", lambda tt, ti=ti, jh=jh, dc=dc, pk=pk: tt.matmul(
                        pctx[pk][:, :], x0T[:, dc, ti * 128:(ti + 1) * 128], wv_sb[:, dc, jh, :],
                        start=(dc == 0), stop=(dc == 7)),
                        deps=vdeps if dc == 0 else [], event=f"vmm{gb}_{ti}_{jh}" if dc == 7 else None)
                s.add("vector", lambda v, jh=jh, pk=pk: v.tensor_copy(
                    vevac[:, jh * 512:(jh + 1) * 512], pctx[pk][:, :]),
                    deps=[f"vmm{gb}_{ti}_{jh}"] + ([f"vii{gb}_{ti-1}"] if (ti and jh == 0) else []),
                    event=f"vee{gb}_{ti}_{jh}")
                last_pctx_pre[pk] = f"vee{gb}_{ti}_{jh}"
            s.add("vector", lambda v: v.tensor_copy(
                vint[:, :, 0:64], vevac[:, :].rearrange("p (h j) -> p h j", j=64)),
                deps=[f"vee{gb}_{ti}_0", f"vee{gb}_{ti}_1", "vones"] +
                     ([f"vst{gb}_{ti-1}"] if ti else ([f"vst{f'g{gp-1}'}_3"] if gp else [])),
                event=f"vii{gb}_{ti}")
            s.add("gpsimd", lambda g, gp=gp, ti=ti: g.dma_start(
                out=vpad_d.ap()[(gp * 4 + ti) * 128:(gp * 4 + ti + 1) * 128, :],
                in_=vint[:, :, :].rearrange("p h c -> p (h c)")),
                deps=[f"vii{gb}_{ti}"], dma=True, event=f"vst{gb}_{ti}")
        kideps = shdone + ["ld_wki"] + ([f"kiev{f'g{gp-1}'}"] if gp else ["qob"])
        for dc in range(8):
            s.add("tensor", lambda tt, dc=dc: tt.matmul(
                pz[0:64, :], wki_sb[:, dc, :], x0T[:, dc, :],
                start=(dc == 0), stop=(dc == 7)),
                deps=kideps if dc == 0 else [], event=f"kimm{gb}" if dc == 7 else None)
        s.add("vector", lambda v, gp=gp: v.tensor_copy(kiT[:, gp * 512:(gp + 1) * 512], pz[0:64, :]),
              deps=[f"kimm{gb}"], event=f"kiev{gb}")

    s.add("gpsimd", lambda e: e.dma_start(out=wq_sb[:], in_=Wq.rearrange("(dc p) (jc j) -> p dc jc j", p=128, j=128)),
          deps=[f"kmm{f'g{NGRP-1}'}_7"], dma=True, event="ld_wq")
    s.add("gpsimd", lambda e: e.dma_start(
        out=wbig2[:, :, :, :].rearrange("p dc a q -> p dc (a q)"),
        in_=Wo.rearrange("(dc p) j -> p dc j", p=128)),
        deps=[f"vmm{f'g{NGRP-1}'}_3_1"], dma=True, event="ld_wo")
    last_q_psc = [last_psc_pre[0], last_psc_pre[1]]
    for gp in range(TPC // 512):
        gb = f"og{gp}"
        shdone = []
        for ti in range(4):
            ot = gp * 4 + ti
            bi = ti % 2
            prevg = f"og{gp-1}"
            s.add("scalar", lambda e, ot=ot, bi=bi: e.dma_start(out=x0t[bi][:], in_=x0o[ot * 128:(ot + 1) * 128, :]),
                  deps=[f"oac{gb}_{ti-1}"] if ti >= 1 else ([f"oac{prevg}_3"] if gp else [f"pac{f'g{NGRP-1}'}_3"]),
                  dma=True, event=f"oal{gb}_{ti}")
            cdep = [f"oal{gb}_{ti}"]
            if ti:
                cdep.append(f"oatp{gb}_{ti-1}")
            elif gp:
                cdep.append(f"oatp{prevg}_3")
            else:
                cdep.append(f"patp{f'g{NGRP-1}'}_3")
            s.add("vector", lambda v, bi=bi: v.tensor_copy(xb16[bi][:], x0t[bi][:]),
                  deps=cdep, event=f"oac{gb}_{ti}")
            tdep = [f"oac{gb}_{ti}", "ident"]
            if ti:
                tdep.append(f"oashTe{gb}_{ti-1}")
            elif gp:
                tdep.append(f"oashTe{prevg}_3")
                tdep.append(f"qiT{(gp-1)*4+3}")
            else:
                tdep.append(f"pashTe{f'g{NGRP-1}'}_3")
            for dc in range(8):
                s.add("tensor", lambda tt, bi=bi, dc=dc: tt.transpose(
                    ptr[:, dc * 128:(dc + 1) * 128], xb16[bi][:, dc * 128:(dc + 1) * 128], ident[:]),
                    deps=tdep if dc == 0 else [], event=f"oatp{gb}_{ti}" if dc == 7 else None)
            xdep = [f"oatp{gb}_{ti}"]
            if gp and ti == 0:
                xdep += [f"qmm{prevg}_7", f"qimm{prevg}_3_1"]
            elif gp == 0 and ti == 0:
                xdep += [f"kimm{f'g{NGRP-1}'}", f"vmm{f'g{NGRP-1}'}_3_1"]
            s.add("vector", lambda v, ti=ti: v.tensor_copy(
                x0T[:, :, ti * 128:(ti + 1) * 128], ptr[:, :].rearrange("p (dc q) -> p dc q", q=128)),
                deps=xdep, event=f"oashTe{gb}_{ti}")
            shdone.append(f"oashTe{gb}_{ti}")
        for jc in range(8):
            pk = jc % 2
            qdeps = shdone + ["ld_wq"]
            if last_q_psc[pk]:
                qdeps.append(last_q_psc[pk])
            for dc in range(8):
                s.add("tensor", lambda tt, jc=jc, dc=dc, pk=pk: tt.matmul(
                    psc[pk][:, :], wq_sb[:, dc, jc, :], x0T[:, dc, :],
                    start=(dc == 0), stop=(dc == 7)),
                    deps=qdeps if dc == 0 else [], event=f"qmm{gb}_{jc}" if dc == 7 else None)
            s.add("vector", lambda v, pk=pk: v.tensor_copy(kevac[:], psc[pk][:, :]),
                deps=[f"qmm{gb}_{jc}"] + ([f"qsd{gb}_{jc-1}"] if jc else ([f"qsd{f'og{gp-1}'}_7"] if gp else [f"kst{f'g{NGRP-1}'}_7"])),
                event=f"qev{gb}_{jc}")
            s.add("gpsimd", lambda g, gp=gp, jc=jc: g.dma_start(
                out=qT_d.ap()[jc * 128:(jc + 1) * 128, gp * 512:(gp + 1) * 512], in_=kevac[:]),
                deps=[f"qev{gb}_{jc}"], dma=True, event=f"qsd{gb}_{jc}")
            last_q_psc[pk] = f"qev{gb}_{jc}"
        for ti in range(4):
            ot = gp * 4 + ti
            tb = f"oi{ot}"
            wdeps = [f"oashTe{gb}_{ti}", "ld_ww"] + ([f"wex{ot-1}"] if ot else ["qob", f"kiev{f'g{NGRP-1}'}"])
            for dc in range(8):
                s.add("tensor", lambda tt, ti=ti, dc=dc: tt.matmul(
                    pz[:, 0:4], x0T[:, dc, ti * 128:(ti + 1) * 128], ww_sb[:, dc, :],
                    start=(dc == 0), stop=(dc == 7)),
                    deps=wdeps if dc == 0 else [], event=f"wmm{tb}" if dc == 7 else None)
            s.add("vector", lambda v: v.tensor_reduce(wm[:], pz[:, 0:4], axis=AX.X, op=ALU.max),
                  deps=[f"wmm{tb}"], event=f"wmx{tb}")
            s.add("vector", lambda v: v.tensor_scalar(wm[:], wm[:], -1.0, None, ALU.mult),
                  deps=[f"wmx{tb}"], event=f"wng{tb}")
            s.add("scalar", lambda e: e.activation(wexp[:], pz[:, 0:4], AF.Exp, bias=wm[:]),
                  deps=[f"wng{tb}"], event=f"wep{tb}")
            s.add("vector", lambda v: v.tensor_reduce(wsum[:], wexp[:], axis=AX.X, op=ALU.add),
                  deps=[f"wep{tb}"], event=f"wsm{tb}")
            s.add("vector", lambda v: v.reciprocal(wsum[:], wsum[:]), deps=[f"wsm{tb}"], event=f"wrc{tb}")
            s.add("vector", lambda v: v.tensor_scalar(wrow[:], wexp[:], wsum[:], 0.125, ALU.mult, ALU.mult),
                  deps=[f"wrc{tb}"], event=f"wex{ot}")
            qideps = [f"oashTe{gb}_{ti}", "ld_wqi", f"qmm{gb}_7"] + ([f"qisc{ot-1}"] if ot else [])
            for jh in range(2):
                pk = jh
                for dc in range(8):
                    s.add("tensor", lambda tt, ti=ti, jh=jh, dc=dc, pk=pk: tt.matmul(
                        pctx[2 + pk][:, 0:128], x0T[:, dc, ti * 128:(ti + 1) * 128], wqi_sb[:, dc, jh * 128:(jh + 1) * 128],
                        start=(dc == 0), stop=(dc == 7)),
                        deps=qideps if dc == 0 else [], event=f"qimm{gb}_{ti}_{jh}" if dc == 7 else None)
                s.add("vector", lambda v, jh=jh, pk=pk: v.tensor_copy(
                    qi_tok[:, jh * 128:(jh + 1) * 128], pctx[2 + pk][:, 0:128]),
                    deps=[f"qimm{gb}_{ti}_{jh}"] + ([f"qisc{ot-1}"] if (ot and jh == 0) else []),
                    event=f"qie{tb}_{jh}")
            s.add("vector", lambda v: v.tensor_mul(
                qi_scl[:].rearrange("p (h j) -> p h j", j=64),
                qi_tok[:, :].rearrange("p (h j) -> p h j", j=64),
                wrow[:, :, None].to_broadcast([128, 4, 64])),
                deps=[f"qie{tb}_0", f"qie{tb}_1", f"wex{ot}"], event=f"qisc{ot}")
            for jh in range(2):
                s.add("tensor", lambda tt, jh=jh: tt.transpose(
                    ptr[:, jh * 128:(jh + 1) * 128], qi_scl[:, jh * 128:(jh + 1) * 128], ident[:]),
                    deps=([f"qisc{ot}", f"oatp{gb}_{3}"] + ([f"qiT{ot-1}"] if ot else [])) if jh == 0 else [],
                    event=f"qitp{ot}" if jh == 1 else None)
            s.add("vector", lambda v, ot=ot: v.tensor_copy(
                wqiT[:, :, ot * 128:(ot + 1) * 128], ptr[:, 0:256].rearrange("p (jc q) -> p jc q", q=128)),
                deps=[f"qitp{ot}"], event=f"qiT{ot}")

    last_mtr = None
    for ot in range(TILES_PC):
        tb = f"x{ot}"
        s.add("vector", lambda v, ot=ot: v.tensor_scalar(tilepos[:], rowpos[:], float(ot * 128), qoffc[:, 0:1],
                                                         ALU.add, ALU.add),
              deps=["iorow", "qob"] + ([f"mstore{ot-1}"] if ot else [f"qiT{TILES_PC-1}"]), event=f"tp{tb}")
        for hi in range(4):
            s.add("vector", lambda v, hi=hi, ot=ot: v.tensor_copy(
                istage[:, hi, :], wqiT[(hi % 2) * 64:(hi % 2) * 64 + 64, hi // 2, ot * 128:(ot + 1) * 128]),
                deps=([f"tp{tb}", f"qiT{TILES_PC-1}"] + ([f"imm{tb}_{NSMC-1}_3" if False else f"ic3{f'x{ot-1}'}_{NSMC-1}"] if ot else []))
                     if hi == 0 else [],
                event=f"ist{tb}_{hi}")
        for st in range(NSMC):
            sb2 = f"{tb}_{st}"
            pk = st % 2
            rdeps = [f"qiT{TILES_PC-1}", f"kiev{f'g{NGRP-1}'}", f"tp{tb}"]
            if st >= 1:
                rdeps.append(f"ic3{tb}_{st-1}")
            elif ot:
                rdeps.append(f"ic3{f'x{ot-1}'}_{NSMC-1}")
            else:
                rdeps += [f"vee{f'g{NGRP-1}'}_3_0", f"vee{f'g{NGRP-1}'}_3_1",
                          f"qie{f'oi{TILES_PC-1}'}_0", f"qie{f'oi{TILES_PC-1}'}_1"]
            for hi in range(4):
                s.add("tensor", lambda tt, hi=hi, ot=ot, st=st: tt.matmul(
                    pctx[hi][:, :], istage[:, hi, :],
                    kiT[:, st * 512:(st + 1) * 512], start=True, stop=True),
                    deps=(rdeps + [f"ist{tb}_{3}"]) if hi == 0 else [], event=f"imm{sb2}_{hi}")
            s.add("scalar", lambda e, pk=pk: e.activation(scomb[pk][:], pctx[0][:, :], AF.Relu),
                  deps=[f"imm{sb2}_0"] + ([f"icmb{tb}_{st-2}"] if st >= 2 else []), event=f"ic0{sb2}")
            s.add("vector", lambda v, pk=pk: v.scalar_tensor_tensor(
                scomb[pk][:], pctx[1][:, :], 0.0, scomb[pk][:], ALU.max, ALU.add),
                deps=[f"ic0{sb2}", f"imm{sb2}_1"], event=f"ic1{sb2}")
            s.add("vector", lambda v, pk=pk: v.scalar_tensor_tensor(
                scomb[pk][:], pctx[2][:, :], 0.0, scomb[pk][:], ALU.max, ALU.add),
                deps=[f"ic1{sb2}", f"imm{sb2}_2"], event=f"ic2{sb2}")
            s.add("vector", lambda v, pk=pk: v.scalar_tensor_tensor(
                scomb[pk][:], pctx[3][:, :], 0.0, scomb[pk][:], ALU.max, ALU.add),
                deps=[f"ic2{sb2}", f"imm{sb2}_3"], event=f"ic3{sb2}")
            s.add("vector", lambda v, st=st: v.tensor_scalar(cmask[:], iota512[:], float(st * 512), tilepos[:, 0:1],
                                                             ALU.add, ALU.subtract),
                  deps=["io512", f"tp{tb}", f"ic3{sb2}"], event=f"cm0{sb2}")
            s.add("vector", lambda v: v.tensor_scalar(cmask[:], cmask[:], 0.5, None, ALU.is_ge),
                  deps=[f"cm0{sb2}"], event=f"cm1{sb2}")
            s.add("vector", lambda v, ot=ot, st=st, pk=pk: v.scalar_tensor_tensor(
                idx[:, st * 512:(st + 1) * 512], cmask[:], NEGC, scomb[pk][:], ALU.mult, ALU.add),
                deps=[f"cm1{sb2}"] + ([f"ms{tb}_{st}"] if False else []) +
                     ([f"zap{tb}"] if False else ([] if ot == 0 else ([f"mstore{ot-1}"] if st == 0 else []))),
                event=f"icmb{tb}_{st}")
        prev = f"icmb{tb}_{NSMC-1}"
        for r in range(8):
            s.add("vector", lambda v: v.max(out=m8[:], in_=idx[:, :]),
                  deps=[prev] + ([f"icmb{tb}_{st2}" for st2 in range(NSMC)] if r == 0 else []),
                  event=f"mx{tb}_{r}")
            s.add("vector", lambda v: v.match_replace(out=idx[:, :], in_to_replace=m8[:], in_values=idx[:, :],
                                                      imm_value=NEGZ),
                  deps=[f"mx{tb}_{r}"], event=f"mr{tb}_{r}")
            prev = f"mr{tb}_{r}"
        for st in range(NSMC):
            sb2 = f"{tb}_{st}"
            s.add("vector", lambda v, st=st: v.tensor_scalar(cmask[:], iota512[:], float(st * 512), tilepos[:, 0:1],
                                                             ALU.add, ALU.subtract),
                  deps=[prev] + ([f"mtre{tb}_{st-1}"] if st else []), event=f"cb0{sb2}")
            s.add("vector", lambda v: v.tensor_scalar(cmask[:], cmask[:], 0.5, None, ALU.is_lt),
                  deps=[f"cb0{sb2}"], event=f"cb1{sb2}")
            s.add("vector", lambda v, st=st: v.tensor_scalar(scomb[0][:], idx[:, st * 512:(st + 1) * 512],
                                                             NEGZ, None, ALU.is_equal),
                  deps=[f"cb1{sb2}"], event=f"cb2{sb2}")
            s.add("vector", lambda v, st=st: v.tensor_mul(msb[:], scomb[0][:], cmask[:]),
                  deps=[f"cb2{sb2}"], event=f"msc{tb}_{st}")
            tdeps2 = [f"msc{tb}_{st}"]
            if st:
                tdeps2.append(f"mtre{tb}_{st-1}")
            elif ot:
                tdeps2.append(f"mtre{f'x{ot-1}'}_{NSMC-1}")
            else:
                tdeps2.append(f"qiT{TILES_PC-1}")
            for b4 in range(4):
                s.add("tensor", lambda tt, b4=b4: tt.transpose(
                    ptr[:, b4 * 128:(b4 + 1) * 128], msb[:, b4 * 128:(b4 + 1) * 128], ident[:]),
                    deps=tdeps2 if b4 == 0 else [], event=f"mtp{sb2}" if b4 == 3 else None)
            s.add("vector", lambda v: v.tensor_copy(mtr[:, :, :], ptr[:, 0:512].rearrange("p (a q) -> p a q", q=128)),
                  deps=[f"mtp{sb2}"] + ([f"mst{tb}_{st-1}"] if st else ([f"mst{f'x{ot-1}'}_{NSMC-1}"] if ot else [])),
                  event=f"mtre{tb}_{st}")
            s.add("gpsimd", lambda g, ot=ot, st=st: g.dma_start(
                out=mT_d.ap()[st * 512:(st + 1) * 512, ot * 128:(ot + 1) * 128].rearrange("(a p) q -> p a q", p=128),
                in_=mtr[:, :, :]),
                deps=[f"mtre{tb}_{st}"], dma=True, event=f"mst{tb}_{st}")
        last_mtr = f"mst{tb}_{NSMC-1}"
        s.add("vector", lambda v: v.engine_nop(), deps=[f"mtre{tb}_{NSMC-1}"], event=f"mstore{ot}")

    last_psc = [last_q_psc[0], last_q_psc[1], "qob"]
    psc3 = [psc[0], psc[1], pz]
    last_E_reader = [None] * 4
    last_P_reader = [None] * 4
    last_ctx_evac = [None] * HPG
    last_ptr = f"mtre{f'x{TILES_PC-1}'}_{NSMC-1}"
    last_attnT_reader = None
    ev_store = []
    for qt in range(NQT):
        ytiles_done = []
        for hg in range(NHG):
            heads = [hg * HPG + i for i in range(HPG)]
            qb = f"a{qt}h{hg}"
            qsdeps = [f"qsd{f'og{TPC // 512 - 1}'}_7"]
            if (qt, hg) != (0, 0):
                qsdeps.append(f"asmm{prev_qb}_{NSMC-1}_{3}_{HPG-1}")
            for i, h in enumerate(heads):
                s.add("sync", lambda e, h=h, i=i, qt=qt: e.dma_start(
                    out=qstage[:, i, :], in_=qT_d.ap()[h * 64:(h + 1) * 64, qt * 512:(qt + 1) * 512]),
                    deps=qsdeps if i == 0 else [], dma=True, event=f"qst{qb}_{i}")
            for smc in range(NSMC):
                bi = smc % 3
                kdeps = [f"kst{f'g{NGRP-1}'}_7"]
                if smc >= 3:
                    kdeps.append(f"aiter{qb}_{smc-3}")
                elif (qt, hg) != (0, 0):
                    kdeps.append(f"aiter{prev_qb}_{NSMC-3+smc}")
                for i, h in enumerate(heads):
                    s.add("sync", lambda e, h=h, i=i, smc=smc, bi=bi: e.dma_start(
                        out=kbuf[bi][:, i, :], in_=kT_d.ap()[h * 64:(h + 1) * 64, smc * 512:(smc + 1) * 512]),
                        deps=kdeps if i == 0 else [], dma=True, event=f"akl{qb}_{smc}_{i}")
                s.add("sync", lambda e, smc=smc, bi=bi: e.dma_start(
                    out=vbuf[bi][:, :, :],
                    in_=vpad_d.ap()[smc * 512:(smc + 1) * 512, :].rearrange("(a p) c -> p a c", p=128)),
                    deps=[f"vst{f'g{NGRP-1}'}_3"] + kdeps[1:2], dma=True, event=f"avl{qb}_{smc}")
                s.add("sync", lambda e, smc=smc, bi=bi, qt=qt: e.dma_start(
                    out=mbuf[bi][:, :, :],
                    in_=mT_d.ap()[smc * 512:(smc + 1) * 512, qt * 512:(qt + 1) * 512].rearrange("(a p) c -> p a c", p=128)),
                    deps=[last_mtr] + kdeps[1:2], dma=True, event=f"aml{qb}_{smc}")
                iter_last = None
                for sc in range(4):
                    for i, h in enumerate(heads):
                        it = smc * 4 * HPG + sc * HPG + i
                        pp = it % 3
                        pq = it % 4
                        mdeps = [f"akl{qb}_{smc}_{HPG-1}", f"avl{qb}_{smc}", f"aml{qb}_{smc}"]
                        if last_psc[pp]:
                            mdeps.append(last_psc[pp])
                        s.add("tensor", lambda tt, i=i, sc=sc, bi=bi, pp=pp: tt.matmul(
                            psc3[pp][:, :], kbuf[bi][:, i, sc * 128:(sc + 1) * 128],
                            qstage[:, i, :], start=True, stop=True),
                            deps=(mdeps + [f"qst{qb}_{3}"]) if True else mdeps, event=f"asmm{qb}_{smc}_{sc}_{i}")
                        edeps = [f"asmm{qb}_{smc}_{sc}_{i}"]
                        if last_E_reader[pq]:
                            edeps.append(last_E_reader[pq])
                        s.add("scalar", lambda e, pp=pp, pq=pq: e.activation(Ebuf[pq][:], psc3[pp][:, :], AF.Exp, scale=0.125),
                              deps=edeps, event=f"aexp{qb}_{smc}_{sc}_{i}")
                        last_psc[pp] = f"aexp{qb}_{smc}_{sc}_{i}"
                        pdeps = [f"aexp{qb}_{smc}_{sc}_{i}"]
                        if last_P_reader[pq]:
                            pdeps.append(last_P_reader[pq])
                        s.add("vector", lambda v, pq=pq, sc=sc, bi=bi: v.tensor_mul(
                            Pbuf[pq][:], Ebuf[pq][:], mbuf[bi][:, sc, :]),
                            deps=pdeps, event=f"amul{qb}_{smc}_{sc}_{i}")
                        last_E_reader[pq] = f"amul{qb}_{smc}_{sc}_{i}"
                        first = (smc == 0 and sc == 0)
                        last = (smc == NSMC - 1 and sc == 3)
                        cdeps = [f"amul{qb}_{smc}_{sc}_{i}"]
                        if first and last_ctx_evac[i]:
                            cdeps.append(last_ctx_evac[i])
                        s.add("tensor", lambda tt, i=i, h=h, sc=sc, bi=bi, pq=pq, first=first, last=last: tt.matmul(
                            pctx[i][0:65, :], vbuf[bi][:, sc, h * 65:(h + 1) * 65], Pbuf[pq][:, :],
                            start=first, stop=last),
                            deps=cdeps, event=f"acmm{qb}_{smc}_{sc}_{i}")
                        last_P_reader[pq] = f"acmm{qb}_{smc}_{sc}_{i}"
                        iter_last = (f"acmm{qb}_{smc}_{sc}_{i}", f"aexp{qb}_{smc}_{sc}_{i}")
                s.add("vector", lambda v: v.engine_nop(), deps=list(iter_last), event=f"aiter{qb}_{smc}")
            prev_qb = qb
            for i, h in enumerate(heads):
                edeps = [f"acmm{qb}_{NSMC-1}_{3}_{i}"] + ([f"anrm{qb}_{i-1}"] if i else [])
                s.add("vector", lambda v, i=i: v.tensor_copy(ctxe[:, :], pctx[i][0:65, :]),
                      deps=edeps, event=f"ace{qb}_{i}")
                last_ctx_evac[i] = f"ace{qb}_{i}"
                s.add("vector", lambda v: v.reciprocal(zrow[0:1, 0:512], ctxe[64:65, :]),
                      deps=[f"ace{qb}_{i}"], event=f"azr{qb}_{i}")
                s.add("tensor", lambda tt: tt.matmul(psc[0][0:64, :], onesrowF[0:1, 0:64], zrow[0:1, 0:512],
                                                     start=True, stop=True),
                      deps=[f"azr{qb}_{i}", "onesF", last_psc[0]] + ([f"aznrm{qb}_{i-1}"] if i else
                           ([f"aznrm{prev2_qb}_{HPG-1}"] if (qt, hg) != (0, 0) else ["qob_mm"])),
                      event=f"azb{qb}_{i}")
                s.add("vector", lambda v: v.tensor_copy(zbcs[0:64, :], psc[0][0:64, :]),
                      deps=[f"azb{qb}_{i}"], event=f"azbe{qb}_{i}")
                last_psc[0] = f"azbe{qb}_{i}"
                s.add("vector", lambda v, i=i, h=h: v.tensor_mul(
                    attnT[(h % 2) * 64:(h % 2) * 64 + 64, h // 2, :], ctxe[0:64, :], zbcs[0:64, :]),
                    deps=[f"azbe{qb}_{i}"] + ([last_attnT_reader] if (last_attnT_reader and hg == 0 and i == 0) else []),
                    event=f"anrm{qb}_{i}")
                s.add("vector", lambda v: v.engine_nop(), deps=[f"anrm{qb}_{i}"], event=f"aznrm{qb}_{i}")
            prev2_qb = qb
        attn_done = [f"anrm{f'a{qt}h{hg}'}_{i}" for hg in range(NHG) for i in range(HPG)]
        for jc in range(8):
            pk = jc % 2
            odeps = attn_done + ["ld_wo"]
            if last_psc[pk]:
                odeps.append(last_psc[pk])
            for dc in range(8):
                s.add("tensor", lambda tt, jc=jc, dc=dc, pk=pk: tt.matmul(
                    psc[pk][:, :], wo_sb[:, dc, jc, :], attnT[:, dc, :],
                    start=(dc == 0), stop=(dc == 7)),
                    deps=odeps if dc == 0 else [], event=f"ymm{qt}_{jc}" if dc == 7 else None)
            s.add("vector", lambda v, jc=jc, pk=pk: v.tensor_scalar(
                yT_sb[:, jc, :], psc[pk][:, :], bo_col[:, jc, 0:1], None, ALU.add),
                deps=[f"ymm{qt}_{jc}", "cols_bo"], event=f"yev{qt}_{jc}")
            last_psc[pk] = f"yev{qt}_{jc}"
        for ti in range(4):
            t0 = qt * 4 + ti
            for dc in range(8):
                tdeps = [f"yev{qt}_{dc}"] if dc == 0 else []
                if dc == 0:
                    tdeps = [f"yev{qt}_{jc}" for jc in range(8)] + [last_ptr] + ([f"ytoke{t0-1}"] if ti else [])
                s.add("tensor", lambda tt, ti=ti, dc=dc: tt.transpose(
                    ptr[:, dc * 128:(dc + 1) * 128], yT_sb[:, dc, ti * 128:(ti + 1) * 128], ident[:]),
                    deps=tdeps, event=f"ytp{t0}" if dc == 7 else None)
            s.add("vector", lambda v: v.tensor_copy(ytok[:], ptr[:, :]),
                  deps=[f"ytp{t0}"] + ([f"_lns2_a{t0-1}"] if t0 else []), event=f"ytoke{t0}")
            last_ptr = f"ytoke{t0}"
            s.add("scalar", lambda e, t0=t0: e.dma_start(out=x0t[0][:], in_=x0o[t0 * 128:(t0 + 1) * 128, :]),
                  deps=[f"arf{t0-1}"] if t0 else [f"oal{f'og{TPC // 512 - 1}'}_3"], dma=True, event=f"arl{t0}")
            s.add("vector", lambda v: v.tensor_add(rfin, ytok[:], x0t[0][:]),
                  deps=[f"ytoke{t0}", f"arl{t0}"] +
                       ([f"msc{f'x{TILES_PC-1}'}_{NSMC-1}"] if t0 == 0 else []),
                  event=f"arf{t0}")
            emit_layernorm(s, f"a{t0}", rfin, outt, lnsc,
                           deps=[f"arf{t0}"] + ([f"st_a{t0-1}"] if t0 else []),
                           store_fn=lambda e, t0=t0: e.dma_start(out=x1[t0 * 128:(t0 + 1) * 128, :], in_=outt),
                           store_dep_event=f"st_a{t0-1}" if t0 else None)
            ev_store.append(f"st_a{t0}")
        last_attnT_reader = f"ymm{qt}_7"

    if ctx is None:
        s.finish(*ev_store)
        s.emit(block)
        return p
    return ev_store


def _phase_a_in_maps(inputs):
    x0full = np.ascontiguousarray(np.asarray(inputs["L0_attn_x"], np.float32).reshape(NTOK, D))
    maps = []
    for c in range(NCORES):
        b = c // 4
        m = dict(
            x0batch=x0full[b * T:(b + 1) * T],
            x0own=x0full[c * TPC:(c + 1) * TPC],
            qoff=np.full((1, 1), float((c % 4) * TPC), np.float32),
            Wq=inputs["L0_attn_Wq"], Wk=inputs["L0_attn_Wk"], Wv=inputs["L0_attn_Wv"],
            Wo=inputs["L0_attn_Wo"], bo=inputs["L0_attn_bo"], Wqi=inputs["L0_attn_Wqi"],
            Wki=inputs["L0_attn_Wki"], Ww=inputs["L0_attn_Ww"],
        )
        maps.append({k2: np.ascontiguousarray(v, np.float32) for k2, v in m.items()})
    return maps


def run_phase_a(inputs):
    p = build_phase_a()
    maps = _phase_a_in_maps(inputs)
    res = run_bass_kernel_spmd(p.nc, maps, core_ids=list(range(NCORES)))
    return np.concatenate([res.results[c]["x1s"] for c in range(NCORES)], axis=0)



def _phase_d_in_maps(inputs):
    x3full = np.ascontiguousarray(inputs["x3"], dtype=np.float32)
    maps = []
    for c in range(NCORES):
        m = dict(
            x3s=x3full[c * TPC:(c + 1) * TPC],
            gate_W=inputs["L1_ffn_gate_W"],
            W1=inputs["L1_ffn_W1"],
            b1=inputs["L1_ffn_b1"],
            W2=inputs["L1_ffn_W2"],
            b2=inputs["L1_ffn_b2"],
            Ws1=inputs["L1_ffn_Ws1"],
            bs1=inputs["L1_ffn_bs1"],
            Ws2=inputs["L1_ffn_Ws2"],
            bs2=inputs["L1_ffn_bs2"],
            route_scale=np.asarray(inputs["L1_ffn_route_scale"], np.float32).reshape(1, 1),
        )
        maps.append({k: np.ascontiguousarray(v, np.float32) for k, v in m.items()})
    return maps


def run_phase_d(inputs, n_cores=NCORES):
    p = build_phase_d()
    maps = _phase_d_in_maps(inputs)[:n_cores]
    res = run_bass_kernel_spmd(p.nc, maps, core_ids=list(range(n_cores)))
    out = np.concatenate([res.results[c]["x4s"] for c in range(n_cores)], axis=0)
    return out


def build_fused():
    p = Prog()
    nc = p.nc

    x0b = nc.dram_tensor("x0batch", [T, D], F32, kind="ExternalInput").ap()
    x0o = nc.dram_tensor("x0own", [TPC, D], F32, kind="ExternalInput").ap()
    qoff = nc.dram_tensor("qoff", [1, 1], F32, kind="ExternalInput").ap()
    prevmask = nc.dram_tensor("prevmask", [1, 8], F32, kind="ExternalInput").ap()
    WA = {}
    for nm, shp in [("Wq", [D, D]), ("Wk", [D, D]), ("Wv", [D, D]), ("Wo", [D, D]),
                    ("bo", [D]), ("Wqi", [D, 256]), ("Wki", [D, 64]), ("Ww", [D, 4])]:
        WA[nm] = nc.dram_tensor(nm, shp, F32, kind="ExternalInput").ap()
    WB = {}
    for nm, shp in [("Wr", [D, 8]), ("norm_w", [8, D]), ("Wg", [8, D, 2048]),
                    ("Wu", [8, D, 2048]), ("Wd", [8, 2048, D]), ("f_norm", [D]),
                    ("fWg", [D, 512]), ("fWu", [D, 512]), ("fWd", [512, D])]:
        WB[nm] = nc.dram_tensor(nm, shp, F32, kind="ExternalInput").ap()
    WD = {}
    for nm, shp in [("gate_W", [D, 8]), ("W1", [8, D, 512]), ("b1", [8, 512]),
                    ("W2", [8, 512, D]), ("b2", [8, D]), ("Ws1", [D, 512]),
                    ("bs1", [512]), ("Ws2", [512, D]), ("bs2", [D]),
                    ("route_scale", [1, 1])]:
        WD[nm] = nc.dram_tensor(nm, shp, F32, kind="ExternalInput").ap()
    x4 = nc.dram_tensor("x4s", [TPC, D], F32, kind="ExternalOutput").ap()

    x1_d = nc.dram_tensor("x1_d", [TPC, D], F32)
    x2_d = nc.dram_tensor("x2_d", [TPC, D], F32)
    x2b_d = nc.dram_tensor("x2b_d", [T, D], F32)
    x3_d = nc.dram_tensor("x3_d", [TPC, D], F32)

    s = Sched(p)
    block = p.es.enter_context(nc.Block())

    pa = PProg(p, "A_")
    sa = PSched(s, "A_")
    build_phase_a(ctx=dict(p=pa, s=sa, pfx="A_", x0b=x0b, x0o=x0o, qoff=qoff,
                           x1=x1_d.ap(), **WA))
    pa.close()
    fA = s.phase_fence("A")

    pb = PProg(p, "B_")
    sb = PSched(s, "B_", barrier_ev=fA)
    build_phase_b(ctx=dict(p=pb, s=sb, pfx="B_", x1=x1_d.ap(), x2=x2_d.ap(),
                           prevmask=prevmask, **WB))
    pb.close()
    fB = s.phase_fence("B")

    s.add("gpsimd", lambda g: g.collective_compute(
        "AllGather", ALU.bypass, replica_groups=[[0, 1, 2, 3], [4, 5, 6, 7]],
        ins=[x2_d.ap().opt()], outs=[x2b_d.ap().opt()]),
        deps=[fB], event="agx2")

    pc = PProg(p, "C_")
    sc = PSched(s, "C_", barrier_ev="agx2")
    build_phase_c(ctx=dict(p=pc, s=sc, pfx="C_", x2b=x2b_d.ap(), x2o=x2_d.ap(),
                           x3=x3_d.ap()))
    pc.close()
    fC = s.phase_fence("C")

    pd = PProg(p, "D_")
    sd = PSched(s, "D_", barrier_ev=fC)
    build_phase_d(ctx=dict(p=pd, s=sd, pfx="D_", x3=x3_d.ap(), x4=x4, **WD))
    pd.close()
    s.phase_fence("final")
    s.emit(block)
    return p


_FUSED_CACHE = {}


def _fused_in_maps(inputs):
    x0full = np.ascontiguousarray(np.asarray(inputs["L0_attn_x"], np.float32).reshape(NTOK, D))
    maps = []
    for c in range(NCORES):
        b = c // 4
        m = dict(
            x0batch=x0full[b * T:(b + 1) * T],
            x0own=x0full[c * TPC:(c + 1) * TPC],
            qoff=np.full((1, 1), float((c % 4) * TPC), np.float32),
            prevmask=(np.arange(8) < c).astype(np.float32).reshape(1, 8),
            Wq=inputs["L0_attn_Wq"], Wk=inputs["L0_attn_Wk"], Wv=inputs["L0_attn_Wv"],
            Wo=inputs["L0_attn_Wo"], bo=inputs["L0_attn_bo"], Wqi=inputs["L0_attn_Wqi"],
            Wki=inputs["L0_attn_Wki"], Ww=inputs["L0_attn_Ww"],
            Wr=inputs["L0_ffn_Wr"], norm_w=inputs["L0_ffn_norm_w"],
            Wg=inputs["L0_ffn_Wg"], Wu=inputs["L0_ffn_Wu"], Wd=inputs["L0_ffn_Wd"],
            f_norm=inputs["L0_ffn_f_norm"], fWg=inputs["L0_ffn_fWg"],
            fWu=inputs["L0_ffn_fWu"], fWd=inputs["L0_ffn_fWd"],
            gate_W=inputs["L1_ffn_gate_W"], W1=inputs["L1_ffn_W1"], b1=inputs["L1_ffn_b1"],
            W2=inputs["L1_ffn_W2"], b2=inputs["L1_ffn_b2"], Ws1=inputs["L1_ffn_Ws1"],
            bs1=inputs["L1_ffn_bs1"], Ws2=inputs["L1_ffn_Ws2"], bs2=inputs["L1_ffn_bs2"],
            route_scale=np.asarray(inputs["L1_ffn_route_scale"], np.float32).reshape(1, 1),
        )
        maps.append({k2: np.ascontiguousarray(v, np.float32) for k2, v in m.items()})
    return maps


def kernel(**inputs):
    p = build_fused()
    maps = _fused_in_maps(inputs)
    res = run_bass_kernel_spmd(p.nc, maps, core_ids=list(range(NCORES)))
    out = np.concatenate([res.results[c]["x4s"] for c in range(NCORES)], axis=0)
    return out.reshape(B, T, D)


def kernel_unfused(**inputs):
    x1 = run_phase_a(inputs)
    x2 = run_phase_b(dict(x1=x1, **inputs))
    x3 = run_phase_c(dict(x2=x2))
    x4 = run_phase_d(dict(x3=x3, **inputs))
    return x4.reshape(B, T, D)


# version 05 (final)
# speedup vs baseline: 1.5614x; 1.5614x over previous
import math
from contextlib import ExitStack

import numpy as np
import concourse.bass as bass
import concourse.mybir as mybir
from concourse.bass import IndirectOffsetOnAxis
from concourse.bass_utils import run_bass_kernel_spmd

F32 = mybir.dt.float32
BF16 = mybir.dt.bfloat16
I32 = mybir.dt.int32
U32 = mybir.dt.uint32
AF = mybir.ActivationFunctionType
ALU = mybir.AluOpType
AX = mybir.AxisListType

NCORES = 8
B, T, D = 2, 8192, 1024
NTOK = B * T
TPC = NTOK // NCORES
TILES_PC = TPC // 128
LNEPS = 1e-5

ENGINES = ("sync", "tensor", "vector", "scalar", "gpsimd")



class Sched:

    def __init__(self, p):
        self.p = p
        self.ops = {e: [] for e in ENGINES}
        self.events = {}

    def add(self, engine, fn, deps=(), event=None, dma=False):
        idx = len(self.ops[engine])
        deps = [d for d in deps if d is not None]
        for d in deps:
            assert d in self.events, f"dep {d} not yet defined (op {engine}:{idx})"
        self.ops[engine].append(dict(fn=fn, deps=deps, event=event, dma=dma))
        if event is not None:
            assert event not in self.events, f"dup event {event}"
            self.events[event] = (engine, idx)
        return event

    def emit(self, block):
        p = self.p
        csem = {e: p.sem(f"c_{e}") for e in ENGINES}
        dsem = {e: p.sem(f"d_{e}") for e in ENGINES}
        referenced = set()
        for e in ENGINES:
            for op in self.ops[e]:
                referenced.update(op["deps"])
        val_after = {}
        for e in ENGINES:
            c = 0
            dc = 0
            for op in self.ops[e]:
                if op["dma"]:
                    dc += 16
                    op["inc"] = ("d", dc)
                elif op["event"] is not None and op["event"] in referenced:
                    c += 1
                    op["inc"] = ("c", c)
                else:
                    op["inc"] = None
                if op["event"] is not None:
                    val_after[op["event"]] = (e, "d", dc) if op["dma"] else (e, "c", c)
        self._val_after = val_after
        self._csem, self._dsem = csem, dsem

        def make_section(eng_name):
            def body(eng):
                for op in self.ops[eng_name]:
                    waits = {}
                    for dname in op["deps"]:
                        pe, kind, v = val_after[dname]
                        key = (pe, kind)
                        waits[key] = max(waits.get(key, 0), v)
                    for (pe, kind), v in sorted(waits.items()):
                        eng.wait_ge(csem[pe] if kind == "c" else dsem[pe], v)
                    ins = op["fn"](eng)
                    if op["inc"] is not None:
                        kind, v = op["inc"]
                        ins.then_inc(csem[eng_name] if kind == "c" else dsem[eng_name],
                                     16 if kind == "d" else 1)
            return body

        for name in ENGINES:
            if self.ops[name]:
                getattr(block, name)(make_section(name))

    def finish(self, *final_events):
        def fn(g):
            return g.engine_nop()
        self.add("gpsimd", fn, deps=list(final_events), event="_done_" + str(len(self.events)))

    def phase_fence(self, tag, extra=()):
        deps = list(extra)
        for e in ENGINES:
            lastd = lastc = None
            for i in range(len(self.ops[e]) - 1, -1, -1):
                op = self.ops[e][i]
                if op["dma"] and lastd is None:
                    lastd = i
                if (not op["dma"]) and lastc is None:
                    lastc = i
                if lastd is not None and lastc is not None:
                    break
            for i in (lastd, lastc):
                if i is None:
                    continue
                op = self.ops[e][i]
                if op["event"] is None:
                    op["event"] = f"_f_{tag}_{e}_{i}"
                    self.events[op["event"]] = (e, i)
                deps.append(op["event"])
        self.add("gpsimd", lambda g: g.engine_nop(), deps=deps, event=f"fence_{tag}")
        return f"fence_{tag}"


class PSched:

    def __init__(self, s, pfx, barrier_ev=None):
        self.s = s
        self.pfx = pfx
        self.barrier = barrier_ev
        self._seen = set()
        self.events = _PrefixedEvents(s, pfx)

    def add(self, engine, fn, deps=(), event=None, dma=False):
        deps = [self.pfx + d for d in deps if d is not None]
        if self.barrier and engine not in self._seen:
            self._seen.add(engine)
            deps.append(self.barrier)
        return self.s.add(engine, fn, deps, (self.pfx + event) if event else None, dma)

    def finish(self, *evs):
        pass

    def emit(self, block):
        pass

    def fence(self, tag):
        return self.s.phase_fence(tag)


class _PrefixedEvents:
    def __init__(self, s, pfx):
        self.s = s
        self.pfx = pfx

    def __contains__(self, name):
        return (self.pfx + name) in self.s.events


class PProg:

    def __init__(self, p, pfx):
        self.p = p
        self.pfx = pfx
        self.nc = p.nc
        self.es = ExitStack()

    def sb(self, name, shape, dtype):
        return self.es.enter_context(self.nc.sbuf_tensor(self.pfx + name, shape, dtype))

    def ps(self, name, shape, dtype):
        return self.es.enter_context(self.nc.psum_tensor(self.pfx + name, shape, dtype))

    def close(self):
        self.es.close()


class Prog:
    def __init__(self):
        self.nc = bass.Bass("TRN2", target_bir_lowering=False, debug=False)
        self.es = ExitStack()

    def sb(self, name, shape, dtype):
        return self.es.enter_context(self.nc.sbuf_tensor(name, shape, dtype))

    def ps(self, name, shape, dtype):
        return self.es.enter_context(self.nc.psum_tensor(name, shape, dtype))

    def sem(self, name):
        return self.es.enter_context(self.nc.semaphore(name))


def emit_setup_ident(s, identf, ident, identF=None):
    s.add("gpsimd", lambda g: g.iota(identf[:], pattern=[[1, 128]], base=0,
                                     channel_multiplier=-1,
                                     allow_small_or_imprecise_dtypes=True),
          event="identf")
    s.add("vector", lambda v: v.tensor_single_scalar(ident[:], identf[:], 0.0, ALU.is_equal),
          deps=["identf"], event="ident")
    if identF is not None:
        s.add("vector", lambda v: v.tensor_single_scalar(identF[:], identf[:], 0.0, ALU.is_equal),
              deps=["identf"], event="identF")


def emit_rows_to_cols(s, tag, row_sb, r, k, out_sb, identF, pbank, dep_load):
    for kc in range(k):
        s.add("tensor", lambda t, kc=kc: t.transpose(
            pbank[:, 0:r], row_sb[:r, kc * 128:(kc + 1) * 128], identF[:r, :r]),
            deps=[dep_load, "identF"] + ([f"_c_{tag}_{kc-1}"] if kc >= 1 else []),
            event=f"_ct_{tag}_{kc}")
        s.add("vector", lambda v, kc=kc: v.tensor_copy(out_sb[:, kc, 0:r], pbank[:, 0:r]),
            deps=[f"_ct_{tag}_{kc}"],
            event=(f"_c_{tag}_{kc}" if kc < k - 1 else f"cols_{tag}"))


def emit_layernorm(s, tag, src_ap, out_ap, sc, deps, store_fn=None, store_dep_event=None):
    s.add("vector", lambda v: v.tensor_reduce(sc["m"][:], src_ap, axis=AX.X, op=ALU.add),
          deps=deps, event=f"_lnsum_{tag}")
    s.add("vector", lambda v: v.tensor_mul(sc["lnsq"][:], src_ap, src_ap),
          deps=deps, event=f"_lnsq_{tag}")
    s.add("vector", lambda v: v.tensor_reduce(sc["s2"][:], sc["lnsq"][:], axis=AX.X, op=ALU.add),
          deps=[f"_lnsq_{tag}"], event=f"_lns2_{tag}")
    s.add("vector", lambda v: v.tensor_scalar(sc["mu"][:], sc["m"][:], 1.0 / D, None, ALU.mult),
          deps=[f"_lnsum_{tag}"], event=f"_lnmu_{tag}")
    s.add("vector", lambda v: v.tensor_mul(sc["tmp"][:], sc["mu"][:], sc["mu"][:]),
          deps=[f"_lnmu_{tag}"], event=f"_lnmu2_{tag}")
    s.add("vector", lambda v: v.tensor_scalar(sc["var"][:], sc["s2"][:], 1.0 / D, None, ALU.mult),
          deps=[f"_lns2_{tag}"], event=f"_lnvar0_{tag}")
    s.add("vector", lambda v: v.tensor_scalar(sc["var"][:], sc["var"][:], sc["tmp"][:], LNEPS,
                                              ALU.subtract, ALU.add),
          deps=[f"_lnvar0_{tag}", f"_lnmu2_{tag}"], event=f"_lnvar_{tag}")
    s.add("scalar", lambda e: e.activation(sc["rstd"][:], sc["var"][:], AF.Ln),
          deps=[f"_lnvar_{tag}"], event=f"_lnln_{tag}")
    s.add("scalar", lambda e: e.activation(sc["rstd"][:], sc["rstd"][:], AF.Exp, scale=-0.5),
          deps=[f"_lnln_{tag}"], event=f"_lnrstd_{tag}")
    s.add("vector", lambda v: v.tensor_scalar(out_ap, src_ap, sc["mu"][:], sc["rstd"][:],
                                              ALU.subtract, ALU.mult),
          deps=[f"_lnrstd_{tag}"] + ([store_dep_event] if store_dep_event else []),
          event=f"ln_{tag}")
    if store_fn is not None:
        s.add("scalar", store_fn, deps=[f"ln_{tag}"], dma=True, event=f"st_{tag}")


def make_ln_scratch(p, prefix=""):
    return dict(
        lnsq=p.sb(prefix + "lnsq", [128, D], F32),
        m=p.sb(prefix + "lnm", [128, 1], F32),
        s2=p.sb(prefix + "lns2", [128, 1], F32),
        mu=p.sb(prefix + "lnmu", [128, 1], F32),
        var=p.sb(prefix + "lnvar", [128, 1], F32),
        rstd=p.sb(prefix + "lnrstd", [128, 1], F32),
        tmp=p.sb(prefix + "lntmp", [128, 1], F32),
    )



def build_phase_d(ctx=None):
    if ctx is None:
        p = Prog()
        nc = p.nc
        x3 = nc.dram_tensor("x3s", [TPC, D], F32, kind="ExternalInput").ap()
        gateW = nc.dram_tensor("gate_W", [D, 8], F32, kind="ExternalInput").ap()
        W1 = nc.dram_tensor("W1", [8, D, 512], F32, kind="ExternalInput").ap()
        b1 = nc.dram_tensor("b1", [8, 512], F32, kind="ExternalInput").ap()
        W2 = nc.dram_tensor("W2", [8, 512, D], F32, kind="ExternalInput").ap()
        b2 = nc.dram_tensor("b2", [8, D], F32, kind="ExternalInput").ap()
        Ws1 = nc.dram_tensor("Ws1", [D, 512], F32, kind="ExternalInput").ap()
        bs1 = nc.dram_tensor("bs1", [512], F32, kind="ExternalInput").ap()
        Ws2 = nc.dram_tensor("Ws2", [512, D], F32, kind="ExternalInput").ap()
        bs2 = nc.dram_tensor("bs2", [D], F32, kind="ExternalInput").ap()
        rsc = nc.dram_tensor("route_scale", [1, 1], F32, kind="ExternalInput").ap()
        x4 = nc.dram_tensor("x4s", [TPC, D], F32, kind="ExternalOutput").ap()
        s = Sched(p)
        block = p.es.enter_context(nc.Block())
    else:
        p, s = ctx["p"], ctx["s"]
        nc = p.nc
        x3 = ctx["x3"]; gateW = ctx["gate_W"]; W1 = ctx["W1"]; b1 = ctx["b1"]
        W2 = ctx["W2"]; b2 = ctx["b2"]; Ws1 = ctx["Ws1"]; bs1 = ctx["bs1"]
        Ws2 = ctx["Ws2"]; bs2 = ctx["bs2"]; rsc = ctx["route_scale"]; x4 = ctx["x4"]
    NG = TPC // 512

    identf = p.sb("identf", [128, 128], F32)
    ident = p.sb("ident", [128, 128], BF16)
    identF = p.sb("identF", [128, 128], F32)
    onesrow = p.sb("onesrow", [1, 128], BF16)
    gw_sb = p.sb("gw_sb", [128, 8, 8], BF16)
    b1row = p.sb("b1row", [8, 512], F32)
    bs1row = p.sb("bs1row", [1, 512], F32)
    b1_sb = p.sb("b1_sb", [128, 4, 8], F32)
    bs1_sb = p.sb("bs1_sb", [128, 4, 1], F32)
    b2_sb = p.sb("b2_sb", [8, D], BF16)
    bs2_sb = p.sb("bs2_sb", [1, D], BF16)
    rs_sb = p.sb("rs_sb", [1, 1], BF16)
    rs_bc = p.sb("rs_bc", [128, 1], F32)
    ws1_sb = p.sb("ws1_sb", [128, 8, 4, 128], BF16)
    ws2_sb = p.sb("ws2_sb", [128, 4, 2, 512], BF16)
    w1_sb = [p.sb(f"w1_sb{i}", [128, 8, 4, 128], BF16) for i in range(2)]
    w2_sb = [p.sb(f"w2_sb{i}", [128, 4, 2, 512], BF16) for i in range(2)]

    xT = p.sb("xT", [128, 8, TPC], BF16)
    x3t = [p.sb(f"x3t{i}", [128, D], F32) for i in range(2)]
    xb16 = [p.sb(f"xb16{i}", [128, D], BF16) for i in range(2)]

    gt = p.sb("gt", [128, TILES_PC, 8], F32)
    gTt = p.sb("gTt", [8, TILES_PC, 128], BF16)
    gsc = p.sb("gsc", [128, 8], F32)
    gsc2 = p.sb("gsc2", [128, 8], F32)
    gm8 = p.sb("gm8", [128, 8], F32)
    gmi = p.sb("gmi", [128, 8], U32)
    gmif = p.sb("gmif", [128, 8], F32)
    giota = p.sb("giota", [128, 8], F32)
    gw1 = p.sb("gw1", [128, 1], F32)
    gw2 = p.sb("gw2", [128, 1], F32)
    gs12 = p.sb("gs12", [128, 1], F32)
    grcp = p.sb("grcp", [128, 1], F32)
    goh = p.sb("goh", [128, 8], F32)
    goh2 = p.sb("goh2", [128, 8], F32)
    gfb = p.sb("gfb", [128, 8], BF16)

    h1T = p.sb("h1T", [128, 4, 512], BF16)
    hsT = p.sb("hsT", [128, 4, 512], BF16)
    acc = [p.sb(f"acc{i}", [128, 4, D], F32) for i in range(2)]
    x3r = p.sb("x3r", [128, 4, D], F32)
    radd = p.sb("radd", [128, D], F32)
    rgrp = p.sb("rgrp", [128, 4, D], F32)
    outt = p.sb("outt", [128, D], F32)
    lnsc = make_ln_scratch(p)

    ptr = p.ps("ptr", [128, 1024], BF16)
    pg = p.ps("pg", [128, 512], F32)
    ph = [p.ps(f"ph{i}", [128, 512], F32) for i in range(2)]
    py = [p.ps(f"py{i}", [128, 512], F32) for i in range(2)]
    psh = [p.ps(f"psh{i}", [128, 512], F32) for i in range(2)]

    emit_setup_ident(s, identf, ident, identF)
    s.add("vector", lambda v: v.memset(onesrow[:], 1.0), event="ones")
    s.add("gpsimd", lambda g: g.iota(giota[:], pattern=[[1, 8]], base=0, channel_multiplier=0,
                                     allow_small_or_imprecise_dtypes=True), event="giota")
    s.add("sync", lambda e: e.dma_start(out=b1row[:], in_=b1[:, :]), dma=True, event="ld_b1r")
    s.add("sync", lambda e: e.dma_start(out=bs1row[:], in_=bs1[None, :]), dma=True, event="ld_bs1r")
    emit_rows_to_cols(s, "b1", b1row, 8, 4, b1_sb, identF, pg, "ld_b1r")
    emit_rows_to_cols(s, "bs1", bs1row, 1, 4, bs1_sb, identF, pg, "ld_bs1r")
    s.add("gpsimd", lambda g: g.dma_start(out=rs_sb[:], in_=rsc[:, :]), dma=True, event="ld_rs")
    s.add("gpsimd", lambda g: g.dma_start(out=gw_sb[:], in_=gateW.rearrange("(dc p) e -> p dc e", p=128)),
          dma=True, event="ld_gw")
    s.add("gpsimd", lambda g: g.dma_start(out=b2_sb[:], in_=b2[:, :]), dma=True, event="ld_b2")
    s.add("gpsimd", lambda g: g.dma_start(out=bs2_sb[:], in_=bs2[None, :]), dma=True, event="ld_bs2")
    s.add("gpsimd", lambda g: g.dma_start(out=ws1_sb[:], in_=Ws1.rearrange("(dc p) (hc q) -> p dc hc q", p=128, q=128)),
          dma=True, event="ld_ws1")
    s.add("gpsimd", lambda g: g.dma_start(out=ws2_sb[:], in_=Ws2.rearrange("(hc p) (dh q) -> p hc dh q", p=128, q=512)),
          dma=True, event="ld_ws2")

    s.add("tensor", lambda t: t.matmul(pg[:, 0:1], onesrow[:, :], rs_sb[:, :], start=True, stop=True),
          deps=["ones", "ld_rs", "cols_b1", "cols_bs1"], event="rsbc_mm")
    s.add("vector", lambda v: v.tensor_copy(rs_bc[:], pg[:, 0:1]), deps=["rsbc_mm"], event="rsbc")

    for t in range(TILES_PC):
        bi = t % 2
        s.add("scalar", lambda e, t=t, bi=bi: e.dma_start(out=x3t[bi][:], in_=x3[t * 128:(t + 1) * 128, :]),
              deps=[f"xcast{t-2}"] if t >= 2 else [], dma=True, event=f"x3l{t}")
        s.add("vector", lambda v, bi=bi: v.tensor_copy(xb16[bi][:], x3t[bi][:]),
              deps=[f"x3l{t}"] + ([f"tp{t-1}"] if t >= 1 else []), event=f"xcast{t}")
        first_deps = [f"xcast{t}", "ident"] + ([f"xTe{t-1}", f"gTe{t-1}"] if t >= 1 else [])
        for dc in range(8):
            s.add("tensor", lambda tt, bi=bi, dc=dc: tt.transpose(
                ptr[:, dc * 128:(dc + 1) * 128], xb16[bi][:, dc * 128:(dc + 1) * 128], ident[:]),
                deps=first_deps if dc == 0 else [], event=f"tp{t}" if dc == 7 else None)
        s.add("vector", lambda v, t=t: v.tensor_copy(
            xT[:, :, t * 128:(t + 1) * 128],
            ptr[:, :].rearrange("p (dc q) -> p dc q", q=128)),
            deps=[f"tp{t}"], event=f"xTe{t}")
        gdeps = [f"xTe{t}", "ld_gw", "rsbc"] + ([f"sig{t-1}"] if t >= 1 else [])
        for dc in range(8):
            s.add("tensor", lambda tt, t=t, dc=dc: tt.matmul(
                pg[:, 0:8], xT[:, dc, t * 128:(t + 1) * 128], gw_sb[:, dc, :],
                start=(dc == 0), stop=(dc == 7)),
                deps=gdeps if dc == 0 else [], event=f"gmm{t}" if dc == 7 else None)
        s.add("scalar", lambda e: e.activation(gsc[:], pg[:, 0:8], AF.Sigmoid),
              deps=[f"gmm{t}"] + ([f"grs{t-1}"] if t >= 1 else []), event=f"sig{t}")
        s.add("vector", lambda v: v.tensor_scalar(gsc2[:], gsc[:], rs_bc[:], None, ALU.mult),
              deps=[f"sig{t}", "giota"] + ([f"gdone{t-1}"] if t >= 1 else []), event=f"grs{t}")
        s.add("vector", lambda v: v.max(out=gm8[:], in_=gsc2[:]), deps=[f"grs{t}"], event=f"gm8_{t}")
        s.add("vector", lambda v: v.max_index(out=gmi[:], in_max=gm8[:], in_values=gsc2[:]),
              deps=[f"gm8_{t}"], event=f"gmi_{t}")
        s.add("vector", lambda v: v.tensor_copy(gmif[:], gmi[:]), deps=[f"gmi_{t}"], event=f"gmif_{t}")
        s.add("vector", lambda v: v.tensor_add(gs12[:], gm8[:, 0:1], gm8[:, 1:2]),
              deps=[f"gm8_{t}"], event=f"gs12_{t}")
        s.add("vector", lambda v: v.reciprocal(grcp[:], gs12[:]), deps=[f"gs12_{t}"], event=f"grcp_{t}")
        s.add("vector", lambda v: v.tensor_mul(gw1[:], gm8[:, 0:1], grcp[:]), deps=[f"grcp_{t}"], event=f"gw1_{t}")
        s.add("vector", lambda v: v.tensor_mul(gw2[:], gm8[:, 1:2], grcp[:]), deps=[f"grcp_{t}"], event=f"gw2_{t}")
        s.add("vector", lambda v: v.tensor_scalar(goh[:], giota[:], gmif[:, 0:1], None, ALU.is_equal),
              deps=[f"gmif_{t}"], event=f"goh_{t}")
        s.add("vector", lambda v: v.tensor_scalar(goh2[:], giota[:], gmif[:, 1:2], None, ALU.is_equal),
              deps=[f"gmif_{t}"], event=f"goh2_{t}")
        s.add("vector", lambda v, t=t: v.tensor_scalar(gt[:, t, :], goh[:], gw1[:], None, ALU.mult),
              deps=[f"goh_{t}", f"gw1_{t}"], event=f"gtp1_{t}")
        s.add("vector", lambda v, t=t: v.scalar_tensor_tensor(gt[:, t, :], goh2[:], gw2[:], gt[:, t, :],
                                                              ALU.mult, ALU.add),
              deps=[f"gtp1_{t}", f"goh2_{t}", f"gw2_{t}"], event=f"gfull_{t}")
        s.add("vector", lambda v, t=t: v.tensor_copy(gfb[:], gt[:, t, :]),
              deps=[f"gfull_{t}"], event=f"gdone{t}")
        s.add("tensor", lambda tt: tt.transpose(ptr[0:8, 0:128], gfb[:, :], ident[:]),
              deps=[f"gdone{t}", f"xTe{t}"], event=f"gtp{t}")
        s.add("vector", lambda v, t=t: v.tensor_copy(gTt[:8, t, :], ptr[0:8, 0:128]),
              deps=[f"gtp{t}"], event=f"gTe{t}")

    phk = 0
    last_ph_ev = [None, None]
    last_py_ev = [None, None]
    last_psh_ev = [None, None]
    last_w_ev = {}

    def h_chunks(tag, gi, lhs_sb, bias_sb_fn, out_sbT, extra_deps):
        nonlocal phk
        evs = []
        for hc in range(4):
            pk = phk % 2
            phk += 1
            deps = list(extra_deps) + [f"xTe{gi*4+3}"]
            if last_ph_ev[pk] is not None:
                deps.append(last_ph_ev[pk])
            for dc in range(8):
                s.add("tensor", lambda tt, gi=gi, hc=hc, dc=dc, pk=pk, lhs_sb=lhs_sb: tt.matmul(
                    ph[pk][:, :], lhs_sb[:, dc, hc, :], xT[:, dc, gi * 512:(gi + 1) * 512],
                    start=(dc == 0), stop=(dc == 7)),
                    deps=deps if dc == 0 else [],
                    event=f"hmm_{tag}_{hc}" if dc == 7 else None)
            s.add("scalar", lambda e, hc=hc, pk=pk: e.activation(
                out_sbT[:, hc, :], ph[pk][:, :], AF.Gelu, bias=bias_sb_fn(hc)),
                deps=[f"hmm_{tag}_{hc}"], event=f"hev_{tag}_{hc}")
            last_ph_ev[pk] = f"hev_{tag}_{hc}"
            evs.append(f"hev_{tag}_{hc}")
        return evs

    st_events = []
    for gi in range(NG):
        zdeps = []
        if gi > 0:
            zdeps = [f"radd_{gi-1}_{ti}_{dh}" for ti in range(4) for dh in range(2)]
        s.add("vector", lambda v: v.memset(acc[0][:], 0.0), deps=zdeps, event=f"acc0_{gi}")
        accev = {(ti, dh): f"acc0_{gi}" for ti in range(4) for dh in range(2)}

        shevs = h_chunks(f"sh{gi}", gi, ws1_sb, lambda hc: bs1_sb[:, hc, 0:1], hsT,
                         ["ld_ws1", "cols_bs1"])

        for e in range(8):
            k = gi * 8 + e
            buf = k % 2
            wdeps = [last_w_ev[k - 2]] if (k - 2) in last_w_ev else []
            s.add("gpsimd", lambda g, e=e, buf=buf: g.dma_start(
                out=w1_sb[buf][:], in_=W1[e].rearrange("(dc p) (hc q) -> p dc hc q", p=128, q=128)),
                deps=wdeps, dma=True, event=f"wl1_{k}")
            s.add("gpsimd", lambda g, e=e, buf=buf: g.dma_start(
                out=w2_sb[buf][:], in_=W2[e].rearrange("(hc p) (dh q) -> p hc dh q", p=128, q=512)),
                dma=True, event=f"wl2_{k}")
            hevs = h_chunks(f"e{k}", gi, w1_sb[buf],
                            lambda hc, e=e: b1_sb[:, hc, e:e + 1], h1T,
                            [f"wl1_{k}", "cols_b1"])
            for ti in range(4):
                tglob = gi * 4 + ti
                for dh in range(2):
                    deps = list(hevs) + [f"wl2_{k}"]
                    if last_py_ev[dh] is not None:
                        deps.append(last_py_ev[dh])
                    for hc in range(4):
                        s.add("tensor", lambda tt, ti=ti, dh=dh, hc=hc, buf=buf: tt.matmul(
                            py[dh][:, :], h1T[:, hc, ti * 128:(ti + 1) * 128],
                            w2_sb[buf][:, hc, dh, :],
                            start=(hc == 0), stop=(hc == 3)),
                            deps=deps if hc == 0 else [],
                            event=f"ymm_{k}_{ti}_{dh}" if hc == 3 else None)
                    nxt, cur = (e + 1) % 2, e % 2
                    s.add("vector", lambda v, ti=ti, dh=dh, tglob=tglob, e=e, nxt=nxt, cur=cur: v.scalar_tensor_tensor(
                        acc[nxt][:, ti, dh * 512:(dh + 1) * 512], py[dh][:, :],
                        gt[:, tglob, e:e + 1], acc[cur][:, ti, dh * 512:(dh + 1) * 512],
                        ALU.mult, ALU.add),
                        deps=[f"ymm_{k}_{ti}_{dh}", accev[(ti, dh)]],
                        event=f"yev_{k}_{ti}_{dh}")
                    accev[(ti, dh)] = f"yev_{k}_{ti}_{dh}"
                    last_py_ev[dh] = f"yev_{k}_{ti}_{dh}"
            last_w_ev[k] = f"ymm_{k}_{3}_{1}"

        accfin = acc[8 % 2]
        for ti in range(4):
            rdeps = [f"x3l{TILES_PC-1}"] if gi == 0 else []
            s.add("scalar", lambda e, gi=gi, ti=ti: e.dma_start(
                out=x3r[:, ti, :], in_=x3[(gi * 4 + ti) * 128:(gi * 4 + ti + 1) * 128, :]),
                deps=rdeps, dma=True, event=f"x3r_{gi}_{ti}")

        for ti in range(4):
            tglob = gi * 4 + ti
            for dh in range(2):
                deps = list(shevs) + [f"gTe{tglob}", "ld_b2", "ld_bs2", "ld_ws2", "ones"]
                if last_psh_ev[dh] is not None:
                    deps.append(last_psh_ev[dh])
                for hc in range(4):
                    s.add("tensor", lambda tt, ti=ti, dh=dh, hc=hc: tt.matmul(
                        psh[dh][:, :], hsT[:, hc, ti * 128:(ti + 1) * 128],
                        ws2_sb[:, hc, dh, :], start=(hc == 0), stop=False),
                        deps=deps if hc == 0 else [], event=None)
                s.add("tensor", lambda tt, tglob=tglob, dh=dh: tt.matmul(
                    psh[dh][:, :], gTt[:8, tglob, :], b2_sb[:8, dh * 512:(dh + 1) * 512],
                    start=False, stop=False), event=None)
                s.add("tensor", lambda tt, dh=dh: tt.matmul(
                    psh[dh][:, :], onesrow[:, :], bs2_sb[:, dh * 512:(dh + 1) * 512],
                    start=False, stop=True), event=f"fmm_{gi}_{ti}_{dh}")
                s.add("vector", lambda v, ti=ti, dh=dh, accfin=accfin: v.scalar_tensor_tensor(
                    radd[:, dh * 512:(dh + 1) * 512], psh[dh][:, :], 1.0,
                    accfin[:, ti, dh * 512:(dh + 1) * 512], ALU.mult, ALU.add),
                    deps=[f"fmm_{gi}_{ti}_{dh}", accev[(ti, dh)]] +
                         ([f"rg_{gi}_{ti}_{dh}"] if False else []) +
                         ([f"radd_prev_guard"] if False else []),
                    event=f"ra_{gi}_{ti}_{dh}")
                s.add("vector", lambda v, gi=gi, ti=ti, dh=dh: v.tensor_add(
                    rgrp[:, ti, dh * 512:(dh + 1) * 512], radd[:, dh * 512:(dh + 1) * 512],
                    x3r[:, ti, dh * 512:(dh + 1) * 512]),
                    deps=[f"ra_{gi}_{ti}_{dh}", f"x3r_{gi}_{ti}"],
                    event=f"radd_{gi}_{ti}_{dh}")
                last_psh_ev[dh] = f"radd_{gi}_{ti}_{dh}"

        for ti in range(4):
            tglob = gi * 4 + ti
            tag = f"d{tglob}"
            emit_layernorm(
                s, tag, rgrp[:, ti, :], outt[:],
                lnsc,
                deps=[f"radd_{gi}_{ti}_0", f"radd_{gi}_{ti}_1"] +
                     ([f"st_d{tglob-1}"] if tglob >= 1 else []),
                store_fn=lambda e, tglob=tglob: e.dma_start(
                    out=x4[tglob * 128:(tglob + 1) * 128, :], in_=outt[:]),
                store_dep_event=f"st_d{tglob-1}" if tglob >= 1 else None,
            )
            st_events.append(f"st_d{tglob}")

    if ctx is None:
        s.finish(*st_events)
        s.emit(block)
        return p
    return st_events




BUCKET = 768
CAPACITY = 4096
RMSEPS = 1e-6
BIGIDX = float(1 << 23)
SBK = 384


def build_phase_b(debug=False, nexp=8, ctx=None):
    if ctx is None:
        p = Prog()
        nc = p.nc
        x1 = nc.dram_tensor("x1s", [TPC, D], F32, kind="ExternalInput").ap()
        Wr = nc.dram_tensor("Wr", [D, 8], F32, kind="ExternalInput").ap()
        normw = nc.dram_tensor("norm_w", [8, D], F32, kind="ExternalInput").ap()
        Wg = nc.dram_tensor("Wg", [8, D, 2048], F32, kind="ExternalInput").ap()
        Wu = nc.dram_tensor("Wu", [8, D, 2048], F32, kind="ExternalInput").ap()
        Wd = nc.dram_tensor("Wd", [8, 2048, D], F32, kind="ExternalInput").ap()
        fnorm = nc.dram_tensor("f_norm", [D], F32, kind="ExternalInput").ap()
        fWg = nc.dram_tensor("fWg", [D, 512], F32, kind="ExternalInput").ap()
        fWu = nc.dram_tensor("fWu", [D, 512], F32, kind="ExternalInput").ap()
        fWd = nc.dram_tensor("fWd", [512, D], F32, kind="ExternalInput").ap()
        x2 = nc.dram_tensor("x2s", [TPC, D], F32, kind="ExternalOutput").ap()
        pmsk = nc.dram_tensor("prevmask", [1, 8], F32, kind="ExternalInput").ap()
        _pfx = ""
    else:
        p, s = ctx["p"], ctx["s"]
        nc = p.nc
        x1 = ctx["x1"]; Wr = ctx["Wr"]; normw = ctx["norm_w"]; Wg = ctx["Wg"]
        Wu = ctx["Wu"]; Wd = ctx["Wd"]; fnorm = ctx["f_norm"]; fWg = ctx["fWg"]
        fWu = ctx["fWu"]; fWd = ctx["fWd"]; x2 = ctx["x2"]; pmsk = ctx["prevmask"]
        _pfx = ctx["pfx"]

    if debug:
        dbg_dst = nc.dram_tensor("dbg_dst", [2, 128, TILES_PC], I32, kind="ExternalOutput").ap()
        dbg_wk = nc.dram_tensor("dbg_wk", [128, TILES_PC, 2], F32, kind="ExternalOutput").ap()
        dbg_cnt = nc.dram_tensor("dbg_cnt", [8, TILES_PC], F32, kind="ExternalOutput").ap()
        dbg_bo = nc.dram_tensor("dbg_bo", [8, TILES_PC], F32, kind="ExternalOutput").ap()
        dbg_bkt = nc.dram_tensor("dbg_bkt", [8 * BUCKET, D], F32, kind="ExternalOutput").ap()
        dbg_ybkt = nc.dram_tensor("dbg_ybkt", [8 * BUCKET, D], F32, kind="ExternalOutput").ap()
        dbg_ovf = nc.dram_tensor("dbg_ovf", [128, TILES_PC], F32, kind="ExternalOutput").ap()
        dbg_hT = nc.dram_tensor("dbg_hT", [128, 8, BUCKET], F32, kind="ExternalOutput").ap()
        dbg_hTs = nc.dram_tensor("dbg_hTs", [128, 8, BUCKET], F32, kind="ExternalOutput").ap()
        dbg_rms = nc.dram_tensor("dbg_rms", [128, BUCKET], F32, kind="ExternalOutput").ap()
    buckets = nc.dram_tensor(_pfx + "buckets", [8 * BUCKET, D], BF16)
    ybuckets = nc.dram_tensor(_pfx + "ybuckets", [8 * BUCKET, D], BF16)
    agi = nc.dram_tensor(_pfx + "agi", [8, TILES_PC], F32)
    ago = nc.dram_tensor(_pfx + "ago", [64, TILES_PC], F32)

    if ctx is None:
        s = Sched(p)
        block = p.es.enter_context(nc.Block())

    identf = p.sb("identf", [128, 128], F32)
    ident = p.sb("ident", [128, 128], BF16)
    identF = p.sb("identF", [128, 128], F32)
    tribf = p.sb("tribf", [128, 128], BF16)
    onesrow = p.sb("onesrow", [1, 128], BF16)
    onesrowF = p.sb("onesrowF", [1, 128], F32)
    onescol = p.sb("onescol", [128, 1], BF16)
    giota = p.sb("giota", [128, 8], F32)

    wr_sb = p.sb("wr_sb", [128, 8, 8], BF16)
    nwrow = p.sb("nwrow", [8, D], F32)
    nw_sb = p.sb("nw_sb", [128, 8, 8], F32)
    fnrow = p.sb("fnrow", [1, D], F32)
    fnw_col = p.sb("fnw_col", [128, 8, 1], F32)
    fwg_sb = p.sb("fwg_sb", [128, 8, 4, 128], BF16)
    fwu_sb = p.sb("fwu_sb", [128, 8, 4, 128], BF16)
    fwd_sb = p.sb("fwd_sb", [128, 4, 8, 128], BF16)

    x1t = [p.sb(f"x1t{i}", [128, D], F32) for i in range(2)]
    x1s3 = p.sb("x1s3", [128, D], F32)
    x1s6 = x1s3
    xT = p.sb("xT", [128, 8, TPC], BF16)
    xb16 = p.sb("xb16", [128, D], BF16)

    rprob = p.sb("rprob", [128, 8], F32)
    rexp = p.sb("rexp", [128, 8], F32)
    rscr = p.sb("rscr", [128, 8], F32)
    rm = p.sb("rm", [128, 1], F32)
    rsum = p.sb("rsum", [128, 1], F32)
    rrcp = p.sb("rrcp", [128, 1], F32)
    rm8 = p.sb("rm8", [128, 8], F32)
    rmi = p.sb("rmi", [128, 8], U32)
    rmif = p.sb("rmif", [128, 8], F32)
    roh = [p.sb(f"roh{k}", [128, 8], F32) for k in range(2)]
    g01b = p.sb("g01b", [128, 8], BF16)
    cex = p.sb("cex", [128, 8], F32)
    cnt_own = p.sb("cnt_own", [8, TILES_PC], F32)
    cnt_glob = p.sb("cnt_glob", [8, NCORES * TILES_PC], F32)
    base_l = p.sb("base_l", [8, TILES_PC], F32)
    base_o = p.sb("base_o", [8, TILES_PC], F32)
    tot_all = p.sb("tot_all", [8, 8], F32)
    pm_sb = p.sb("pm_sb", [1, 8], BF16)
    pm_bc = p.sb("pm_bc", [8, 8], F32)
    cbase = p.sb("cbase", [8, 1], F32)
    totsc = p.sb("totsc", [8, 8], F32)
    zrow8 = p.sb("zrow8", [8, NCORES * TILES_PC], F32)
    brow = p.sb("brow", [1, 8], F32)
    bbc = p.sb("bbc", [128, 8], F32)
    brow2 = p.sb("brow2", [1, 8], F32)
    bbc2 = p.sb("bbc2", [128, 8], F32)

    e_ka = p.sb("e_ka", [128, TILES_PC, 2], F32)
    w_ka = p.sb("w_ka", [128, TILES_PC, 2], F32)
    c_ka = p.sb("c_ka", [128, TILES_PC, 2], F32)
    wk = p.sb("wk", [128, TILES_PC, 2], F32)
    ovf = p.sb("ovf", [128, TILES_PC], F32)
    ovfu = p.sb("ovfu", [128, TILES_PC], U32)
    dsti = [p.sb(f"dsti{k}", [128, TILES_PC], I32) for k in range(2)]
    t1 = p.sb("t1", [128, 1], F32)
    t2 = p.sb("t2", [128, 1], F32)
    t3 = p.sb("t3", [128, 1], F32)
    t4 = p.sb("t4", [128, 1], F32)
    t5 = p.sb("t5", [128, 1], F32)
    ohs = p.sb("ohs", [128, 8], F32)

    hT = p.sb("hT", [128, 8, BUCKET], BF16)
    hTs = p.sb("hTs", [128, 8, BUCKET], BF16)
    sqs = p.sb("sqs", [128, BUCKET], BF16)
    rmsrow = p.sb("rmsrow", [1, BUCKET], F32)
    rms_bc = p.sb("rms_bc", [128, BUCKET], F32)
    wgc = [p.sb(f"wgc{i}", [128, 8, 128], BF16) for i in range(2)]
    wuc = [p.sb(f"wuc{i}", [128, 8, 128], BF16) for i in range(2)]
    wdc = [p.sb("wdc0", [128, 16, 128], BF16)] * 2
    gg = p.sb("gg", [128, SBK], BF16)
    uu = p.sb("uu", [128, SBK], BF16)
    guT = p.sb("guT", [128, 16, SBK], BF16)
    yT = p.sb("yT", [128, 8, BUCKET], BF16)
    ytm = xb16

    fhT = p.sb("fhT", [128, 8, 512], BF16)
    fsqs = p.sb("fsqs", [128, 512], BF16)
    frms_row = p.sb("frms_row", [1, 512], F32)
    frms_bc = rms_bc[:, 0:512]
    fgg = p.sb("fgg", [128, 512], BF16)
    fuu = p.sb("fuu", [128, 512], BF16)
    fgu = p.sb("fgu", [128, 4, 512], BF16)
    foutT = p.sb("foutT", [128, 8, 512], BF16)
    fout_tm = p.sb("fout_tm", [128, D], BF16)

    gy = [p.sb(f"gy{k}", [128, D], BF16) for k in range(2)]
    comb = p.sb("comb", [128, D], F32)
    comb2 = p.sb("comb2", [128, D], F32)
    selo = p.sb("selo", [128, D], F32)
    rfin = p.sb("rfin", [128, D], F32)
    outt = comb2
    lnsc = dict(
        lnsq=comb,
        m=p.sb("lnm", [128, 1], F32), s2=p.sb("lns2", [128, 1], F32),
        mu=p.sb("lnmu", [128, 1], F32), var=p.sb("lnvar", [128, 1], F32),
        rstd=p.sb("lnrstd", [128, 1], F32), tmp=p.sb("lntmp", [128, 1], F32))

    ptr = p.ps("ptr", [128, 1024], BF16)
    pa = p.ps("pa", [128, 512], F32)
    pc_ = [p.ps(f"pc{i}", [128, 512], F32) for i in range(2)]
    pd_ = [p.ps(f"pd{i}", [128, 512], F32) for i in range(2)]
    pe_ = [p.ps(f"pe{i}", [128, 512], F32) for i in range(2)]

    breg_h = {}

    def _mk_breg(g):
        breg_h["r"] = g.alloc_register("bnd")
        return g.reg_mov(breg_h["r"], 8 * BUCKET - 1)

    s.add("gpsimd", _mk_breg, event="breg")
    emit_setup_ident(s, identf, ident, identF)
    s.add("vector", lambda v: v.tensor_single_scalar(tribf[:], identf[:], 0.0, ALU.is_gt),
          deps=["identf"], event="tri")
    s.add("vector", lambda v: v.memset(onesrow[:], 1.0), event="ones")
    s.add("vector", lambda v: v.memset(onesrowF[:], 1.0), event="onesF")
    s.add("vector", lambda v: v.memset(onescol[:], 1.0), event="onesc")
    s.add("vector", lambda v: v.memset(zrow8[:], 0.0), event="zrow")
    s.add("vector", lambda v: v.memset(gy[0][:], 0.0), event="gyz0")
    s.add("vector", lambda v: v.memset(gy[1][:], 0.0), event="gyz1")
    s.add("gpsimd", lambda g: g.iota(giota[:], pattern=[[1, 8]], base=0, channel_multiplier=0,
                                     allow_small_or_imprecise_dtypes=True), event="giota")
    s.add("gpsimd", lambda g: g.dma_start(out=wr_sb[:], in_=Wr.rearrange("(dc p) e -> p dc e", p=128)),
          dma=True, event="ld_wr")
    s.add("sync", lambda e: e.dma_start(out=nwrow[:], in_=normw[:, :]), dma=True, event="ld_nwr")
    emit_rows_to_cols(s, "nw", nwrow, 8, 8, nw_sb, identF, pa, "ld_nwr")
    s.add("sync", lambda e: e.dma_start(out=fnrow[:], in_=fnorm[None, :]), dma=True, event="ld_fnr")
    emit_rows_to_cols(s, "fnw", fnrow, 1, 8, fnw_col, identF, pa, "ld_fnr")
    s.add("gpsimd", lambda g: g.dma_start(out=fwg_sb[:], in_=fWg.rearrange("(dc p) (fc q) -> p dc fc q", p=128, q=128)),
          dma=True, event="ld_fwg")
    s.add("gpsimd", lambda g: g.dma_start(out=fwu_sb[:], in_=fWu.rearrange("(dc p) (fc q) -> p dc fc q", p=128, q=128)),
          dma=True, event="ld_fwu")
    s.add("gpsimd", lambda g: g.dma_start(out=fwd_sb[:], in_=fWd.rearrange("(fc p) (dc q) -> p fc dc q", p=128, q=128)),
          dma=True, event="ld_fwd")

    for t in range(TILES_PC):
        s.add("scalar", lambda e, t=t: e.dma_start(out=x1t[t % 2][:], in_=x1[t * 128:(t + 1) * 128, :]),
              deps=[f"xcast{t-2}"] if t >= 2 else [], dma=True, event=f"x1l{t}")
        s.add("vector", lambda v, t=t: v.tensor_copy(xb16[:], x1t[t % 2][:]),
              deps=[f"x1l{t}"] + ([f"tp{t-1}"] if t else []), event=f"xcast{t}")
        fdeps = [f"xcast{t}", "ident"] + ([f"xTe{t-1}"] if t else [])
        for dc in range(8):
            s.add("tensor", lambda tt, dc=dc: tt.transpose(
                ptr[:, dc * 128:(dc + 1) * 128], xb16[:, dc * 128:(dc + 1) * 128], ident[:]),
                deps=fdeps if dc == 0 else [], event=f"tp{t}" if dc == 7 else None)
        s.add("vector", lambda v, t=t: v.tensor_copy(
            xT[:, :, t * 128:(t + 1) * 128], ptr[:, :].rearrange("p (dc q) -> p dc q", q=128)),
            deps=[f"tp{t}"], event=f"xTe{t}")
        rdeps = [f"xTe{t}", "ld_wr"] + ([f"rexp{t-1}"] if t else ["cols_nw", "cols_fnw"])
        for dc in range(8):
            s.add("tensor", lambda tt, t=t, dc=dc: tt.matmul(
                pa[:, 0:8], xT[:, dc, t * 128:(t + 1) * 128], wr_sb[:, dc, :],
                start=(dc == 0), stop=(dc == 7)),
                deps=rdeps if dc == 0 else [], event=f"rmm{t}" if dc == 7 else None)
        sdeps = [f"rmm{t}"] + ([f"rdone{t-1}"] if t else [])
        s.add("vector", lambda v: v.tensor_reduce(rm[:], pa[:, 0:8], axis=AX.X, op=ALU.max),
              deps=sdeps, event=f"rmax{t}")
        s.add("vector", lambda v: v.tensor_scalar(rm[:], rm[:], -1.0, None, ALU.mult),
              deps=[f"rmax{t}"], event=f"rneg{t}")
        s.add("scalar", lambda e: e.activation(rexp[:], pa[:, 0:8], AF.Exp, bias=rm[:]),
              deps=[f"rneg{t}"], event=f"rexp{t}")
        s.add("vector", lambda v: v.tensor_reduce(rsum[:], rexp[:], axis=AX.X, op=ALU.add),
              deps=[f"rexp{t}"], event=f"rsum{t}")
        s.add("vector", lambda v: v.reciprocal(rrcp[:], rsum[:]), deps=[f"rsum{t}"], event=f"rrcp{t}")
        s.add("vector", lambda v: v.tensor_scalar(rprob[:], rexp[:], rrcp[:], None, ALU.mult),
              deps=[f"rrcp{t}"], event=f"rprob{t}")
        s.add("vector", lambda v: v.max(out=rm8[:], in_=rprob[:]), deps=[f"rprob{t}"], event=f"rm8{t}")
        s.add("vector", lambda v: v.max_index(out=rmi[:], in_max=rm8[:], in_values=rprob[:]),
              deps=[f"rm8{t}"], event=f"rmi{t}")
        s.add("vector", lambda v: v.tensor_copy(rmif[:], rmi[:]), deps=[f"rmi{t}"], event=f"rmif{t}")
        for k in range(2):
            s.add("vector", lambda v, k=k: v.tensor_scalar(roh[k][:], giota[:], rmif[:, k:k + 1], None, ALU.is_equal),
                  deps=[f"rmif{t}", "giota"], event=f"roh{k}_{t}")
        s.add("vector", lambda v: v.tensor_add(cex[:], roh[0][:], roh[1][:]),
              deps=[f"roh0_{t}", f"roh1_{t}"], event=f"g01_{t}")
        s.add("vector", lambda v: v.tensor_copy(g01b[:], cex[:]), deps=[f"g01_{t}"], event=f"g01b_{t}")
        s.add("tensor", lambda tt: tt.matmul(pd_[0][:, 0:8], tribf[:, :], g01b[:, :], start=True, stop=True),
              deps=[f"g01b_{t}", "tri"] + ([f"cexe{t-1}"] if t else []), event=f"cexm{t}")
        s.add("vector", lambda v: v.tensor_copy(cex[:], pd_[0][:, 0:8]), deps=[f"cexm{t}"], event=f"cexe{t}")
        s.add("tensor", lambda tt: tt.matmul(pd_[1][:8, 0:1], g01b[:, :], onescol[:, :], start=True, stop=True),
              deps=[f"g01b_{t}", "onesc"] + ([f"cnte{t-1}"] if t else []), event=f"cntm{t}")
        s.add("vector", lambda v, t=t: v.tensor_copy(cnt_own[:8, t:t + 1], pd_[1][:8, 0:1]),
              deps=[f"cntm{t}"], event=f"cnte{t}")
        s.add("vector", lambda v, t=t: v.tensor_copy(w_ka[:, t, :], rm8[:, 0:2]),
              deps=[f"rm8{t}"], event=f"wst{t}")
        s.add("vector", lambda v, t=t: v.tensor_copy(e_ka[:, t, :], rmif[:, 0:2]),
              deps=[f"rmif{t}"], event=f"est{t}")
        for k in range(2):
            s.add("vector", lambda v, k=k: v.tensor_mul(rscr[:], cex[:], roh[k][:]),
                  deps=[f"cexe{t}", f"roh{k}_{t}"] + ([f"cst{k}_{t-1}"] if t else []), event=f"cm{k}_{t}")
            s.add("vector", lambda v, k=k, t=t: v.tensor_reduce(c_ka[:, t, k:k + 1], rscr[:], axis=AX.X, op=ALU.add),
                  deps=[f"cm{k}_{t}"], event=f"cst{k}_{t}")
        s.add("vector", lambda v: v.engine_nop(),
              deps=[f"cst0_{t}", f"cst1_{t}", f"wst{t}", f"est{t}"], event=f"rdone{t}")

    s.add("gpsimd", lambda g: g.dma_start(out=agi.ap()[:, :], in_=cnt_own[:8, :]),
          deps=[f"cnte{TILES_PC-1}"], dma=True, event="cnt_out")
    s.add("gpsimd", lambda g: g.collective_compute(
        "AllGather", ALU.bypass, replica_groups=[list(range(NCORES))],
        ins=[agi.ap().opt()], outs=[ago.ap().opt()]),
        deps=["cnt_out"], event="cnt_ag")
    s.add("gpsimd", lambda g: g.dma_start(out=cnt_glob[:8, :].rearrange("e (c t) -> e c t", c=8), in_=ago.ap().rearrange("(c e) t -> e c t", e=8)),
          deps=["cnt_ag"], dma=True, event="cnt_in")
    s.add("gpsimd", lambda g: g.dma_start(out=pm_sb[:], in_=pmsk[:, :]), dma=True, event="ld_pm")
    s.add("vector", lambda v: v.tensor_reduce(tot_all[:8, :], cnt_glob[:8, :].rearrange("e (c t) -> e c t", c=8),
                                              axis=AX.X, op=ALU.add),
          deps=["cnt_in"], event="tot_all")
    s.add("tensor", lambda t: t.matmul(pa[:8, 32:40], onesrow[:, 0:8], pm_sb[:, :], start=True, stop=True),
          deps=["ld_pm", "ones", f"bbcanswer" if False else "tot_all"], event="pmb_mm")
    s.add("vector", lambda v: v.tensor_copy(pm_bc[:8, :], pa[:8, 32:40]), deps=["pmb_mm"], event="pm_bc")
    s.add("vector", lambda v: v.tensor_mul(totsc[:8, :], tot_all[:8, :], pm_bc[:8, :]),
          deps=["pm_bc"], event="totsc")
    s.add("vector", lambda v: v.tensor_reduce(cbase[:8, :], totsc[:8, :], axis=AX.X, op=ALU.add),
          deps=["totsc"], event="cbase")
    s.add("vector", lambda v: v.tensor_tensor_scan(base_l[:8, :], cnt_own[:8, :], zrow8[:8, 0:TILES_PC],
                                                   0.0, ALU.add, ALU.add),
          deps=[f"cnte{TILES_PC-1}", "zrow"], event="scan_l")
    s.add("vector", lambda v: v.tensor_sub(base_l[:8, :], base_l[:8, :], cnt_own[:8, :]),
          deps=["scan_l"], event="base_l")
    s.add("vector", lambda v: v.tensor_scalar(base_o[:8, :], base_l[:8, :], cbase[:8, :], None, ALU.add),
          deps=["base_l", "cbase"], event="base_o")

    scat_evs = []
    for t in range(TILES_PC):
        s.add("tensor", lambda tt, t=t: tt.transpose(pa[0:1, 0:8], base_o[:8, t:t + 1], identF[:8, :8]),
              deps=["base_o", "identF"] + ([f"bbce{t-1}"] if t else []), event=f"btp{t}")
        s.add("vector", lambda v: v.tensor_copy(brow[:], pa[0:1, 0:8]), deps=[f"btp{t}"], event=f"brw{t}")
        s.add("tensor", lambda tt: tt.matmul(pa[:, 16:24], onesrowF[:, :], brow[:, :], start=True, stop=True),
              deps=[f"brw{t}", "onesF"], event=f"bmm{t}")
        s.add("vector", lambda v: v.tensor_copy(bbc[:], pa[:, 16:24]), deps=[f"bmm{t}"], event=f"bbce{t}")
        s.add("tensor", lambda tt, t=t: tt.transpose(pa[0:1, 8:16], base_l[:8, t:t + 1], identF[:8, :8]),
              deps=["base_l", f"bbce{t}"], event=f"btp2{t}")
        s.add("vector", lambda v: v.tensor_copy(brow2[:], pa[0:1, 8:16]), deps=[f"btp2{t}"], event=f"brw2{t}")
        s.add("tensor", lambda tt: tt.matmul(pa[:, 24:32], onesrowF[:, :], brow2[:, :], start=True, stop=True),
              deps=[f"brw2{t}", "onesF"], event=f"bmm2{t}")
        s.add("vector", lambda v: v.tensor_copy(bbc2[:], pa[:, 24:32]), deps=[f"bmm2{t}"], event=f"bbce2{t}")
        kevs = []
        for k in range(2):
            kb = f"{t}_{k}"
            s.add("vector", lambda v, t=t, k=k: v.tensor_scalar(ohs[:], giota[:], e_ka[:, t, k:k + 1], None, ALU.is_equal),
                  deps=[f"bbce2{t}"] + ([f"dsts_{t}_{k-1}"] if k else []), event=f"soh{kb}")
            s.add("vector", lambda v: v.tensor_mul(rscr[:], ohs[:], bbc[:]), deps=[f"soh{kb}"], event=f"sg0{kb}")
            s.add("vector", lambda v: v.tensor_reduce(t1[:], rscr[:], axis=AX.X, op=ALU.add),
                  deps=[f"sg0{kb}"], event=f"sg{kb}")
            s.add("vector", lambda v: v.tensor_mul(rscr[:], ohs[:], bbc2[:]), deps=[f"sg{kb}"], event=f"sl0{kb}")
            s.add("vector", lambda v: v.tensor_reduce(t2[:], rscr[:], axis=AX.X, op=ALU.add),
                  deps=[f"sl0{kb}"], event=f"sl{kb}")
            s.add("vector", lambda v, t=t, k=k: v.tensor_add(t3[:], t1[:], c_ka[:, t, k:k + 1]),
                  deps=[f"sg{kb}"], event=f"pg{kb}")
            s.add("vector", lambda v: v.tensor_scalar(t4[:], t3[:], float(CAPACITY - 1), None, ALU.is_le),
                  deps=[f"pg{kb}"], event=f"kp{kb}")
            s.add("vector", lambda v, t=t, k=k: v.tensor_add(t3[:], t2[:], c_ka[:, t, k:k + 1]),
                  deps=[f"sl{kb}", f"kp{kb}"], event=f"pl{kb}")
            s.add("vector", lambda v: v.tensor_scalar(t5[:], t3[:], float(BUCKET - 1), None, ALU.is_le),
                  deps=[f"pl{kb}"], event=f"ib{kb}")
            s.add("vector", lambda v: v.tensor_mul(t5[:], t5[:], t4[:]), deps=[f"ib{kb}"], event=f"kk{kb}")
            s.add("vector", lambda v, t=t, k=k: v.tensor_mul(wk[:, t, k:k + 1], w_ka[:, t, k:k + 1], t5[:]),
                  deps=[f"kk{kb}"], event=f"wkf{kb}")
            s.add("vector", lambda v, t=t, k=k: v.tensor_scalar(rscr[:, 0:1], e_ka[:, t, k:k + 1], float(BUCKET), None, ALU.mult),
                  deps=[f"kk{kb}"], event=f"d0{kb}")
            s.add("vector", lambda v: v.tensor_add(t3[:], t3[:], rscr[:, 0:1]), deps=[f"d0{kb}"], event=f"d1{kb}")
            s.add("vector", lambda v: v.tensor_scalar(t5[:], t5[:], 1.0, -BIGIDX, ALU.subtract, ALU.mult),
                  deps=[f"d1{kb}", f"wkf{kb}"], event=f"d2{kb}")
            s.add("vector", lambda v: v.tensor_add(t3[:], t3[:], t5[:]), deps=[f"d2{kb}"], event=f"d3{kb}")
            s.add("vector", lambda v, t=t, k=k: v.tensor_copy(dsti[k][:, t:t + 1], t3[:]),
                  deps=[f"d3{kb}"], event=f"dsts_{t}_{k}")
            kevs.append(f"dsts_{t}_{k}")
        for k in range(2):
            s.add("vector", lambda v, t=t, k=k: v.tensor_scalar(ohs[:], giota[:], e_ka[:, t, k:k + 1], None, ALU.is_equal),
                  deps=kevs if k == 0 else [f"ov0_{t}"], event=f"xoh{t}_{k}")
            s.add("vector", lambda v: v.tensor_mul(rscr[:], ohs[:], bbc[:]), deps=[f"xoh{t}_{k}"], event=f"xg0{t}_{k}")
            s.add("vector", lambda v: v.tensor_reduce(t1[:], rscr[:], axis=AX.X, op=ALU.add),
                  deps=[f"xg0{t}_{k}"], event=f"xg{t}_{k}")
            s.add("vector", lambda v, t=t, k=k: v.tensor_add(t3[:], t1[:], c_ka[:, t, k:k + 1]),
                  deps=[f"xg{t}_{k}"], event=f"xp{t}_{k}")
            if k == 0:
                s.add("vector", lambda v: v.tensor_scalar(t4[:], t3[:], float(CAPACITY - 1), None, ALU.is_le),
                      deps=[f"xp{t}_{k}"], event=f"ov0_{t}")
            else:
                s.add("vector", lambda v: v.tensor_scalar(t5[:], t3[:], float(CAPACITY - 1), None, ALU.is_le),
                      deps=[f"xp{t}_{k}"], event=f"ov1_{t}")
        s.add("vector", lambda v: v.tensor_mul(t4[:], t4[:], t5[:]), deps=[f"ov1_{t}", f"ov0_{t}"], event=f"ovm_{t}")
        s.add("vector", lambda v, t=t: v.tensor_scalar(ovf[:, t:t + 1], t4[:], 1.0, -1.0, ALU.subtract, ALU.mult),
              deps=[f"ovm_{t}"], event=f"ovff_{t}")
        s.add("vector", lambda v, t=t: v.tensor_copy(ovfu[:, t:t + 1], ovf[:, t:t + 1]),
              deps=[f"ovff_{t}"], event=f"ovf_{t}")
        s.add("scalar", lambda e, t=t: e.dma_start(out=x1s3[:], in_=x1[t * 128:(t + 1) * 128, :]),
              deps=[f"xsc{t-1}"] if t else [f"xcast{TILES_PC-1}"], dma=True, event=f"x1l3_{t}")
        s.add("vector", lambda v, t=t: v.tensor_copy(xb16[:], x1s3[:]),
              deps=[f"x1l3_{t}", f"xcast{TILES_PC-1}", f"dsts_{t}_1"] + ([f"scat_{t-1}_1"] if t else []),
              event=f"xsc{t}")
        for k in range(2):
            s.add("gpsimd", lambda g, t=t, k=k: g.indirect_dma_start(
                buckets.ap()[:, :], IndirectOffsetOnAxis(dsti[k][:, t:t + 1], 0), xb16[:],
                None, bounds_check=breg_h["r"], oob_is_err=False),
                deps=[f"xsc{t}", f"dsts_{t}_{k}", "breg"], dma=True, event=f"scat_{t}_{k}")
            scat_evs.append(f"scat_{t}_{k}")

    yst_evs = []
    prev_guT = None
    for e in range(nexp):
        eb = f"e{e}"
        hl_evs = []
        for dc in range(8):
            deps = scat_evs if (e == 0 and dc == 0) else (
                [f"hTs_done_e{e-1}", f"yTe{e-1}_1_7"] if dc == 0 else [])
            s.add("sync", lambda ee, e=e, dc=dc: ee.dma_start(
                out=hT[:, dc, :], in_=buckets.ap()[e * BUCKET:(e + 1) * BUCKET, dc * 128:(dc + 1) * 128],
                transpose=True),
                deps=deps, dma=True, event=f"hl{eb}_{dc}")
            hl_evs.append(f"hl{eb}_{dc}")
        for dc in range(8):
            s.add("vector", lambda v, dc=dc: v.tensor_mul(sqs[:], hT[:, dc, :], hT[:, dc, :]),
                  deps=[f"hl{eb}_{dc}"] + ([f"sq{eb}_{dc-1}e"] if dc else ([f"sqm{f'e{e-1}'}_7_1"] if e else [])),
                  event=f"sq{eb}_{dc}")
            for blk in range(2):
                pbank = pa if blk == 0 else pc_[0]
                s.add("tensor", lambda tt, dc=dc, blk=blk, pbank=pbank: tt.matmul(
                    pbank[0:1, 0:SBK], onescol[:, :], sqs[:, blk * SBK:(blk + 1) * SBK],
                    start=(dc == 0), stop=(dc == 7)),
                    deps=([f"sq{eb}_{dc}", "onesc"] +
                          ([f"bbce{TILES_PC-1}", f"bbce2{TILES_PC-1}"] if (e == 0 and dc == 0) else []) +
                          ([f"rmsrow{f'e{e-1}'}_{blk}"] if (e > 0 and dc == 0) else [])),
                    event=f"sqm{eb}_{dc}_{blk}" if dc == 7 else None)
            s.add("vector", lambda v: v.engine_nop(), deps=[f"sq{eb}_{dc}"], event=f"sq{eb}_{dc}e")
        for blk in range(2):
            pbank = pa if blk == 0 else pc_[0]
            s.add("vector", lambda v, blk=blk, pbank=pbank: v.tensor_scalar(
                rmsrow[0:1, blk * SBK:(blk + 1) * SBK], pbank[0:1, 0:SBK], 1.0 / D, RMSEPS, ALU.mult, ALU.add),
                deps=[f"sqm{eb}_{7}_{blk}"], event=f"rmsrow{eb}_{blk}")
        s.add("scalar", lambda ee: ee.activation(rmsrow[:], rmsrow[:], AF.Ln),
              deps=[f"rmsrow{eb}_0", f"rmsrow{eb}_1"], event=f"rln{eb}")
        s.add("scalar", lambda ee: ee.activation(rmsrow[:], rmsrow[:], AF.Exp, scale=-0.5),
              deps=[f"rln{eb}"], event=f"rexp{eb}")
        for blk in range(2):
            s.add("tensor", lambda tt, blk=blk: tt.matmul(
                pd_[blk][:, 0:SBK], onesrowF[:, :], rmsrow[0:1, blk * SBK:(blk + 1) * SBK],
                start=True, stop=True),
                deps=([f"rexp{eb}", "onesF"] +
                      ([f"cexe{TILES_PC-1}", f"cnte{TILES_PC-1}"] if e == 0 else
                       [f"rbc{f'e{e-1}'}_{blk}"])),
                event=f"rbm{eb}_{blk}")
            s.add("vector", lambda v, blk=blk: v.tensor_copy(rms_bc[:, blk * SBK:(blk + 1) * SBK], pd_[blk][:, 0:SBK]),
                  deps=[f"rbm{eb}_{blk}"], event=f"rbc{eb}_{blk}")
        for dc in range(8):
            s.add("vector", lambda v, dc=dc, e=e: v.scalar_tensor_tensor(
                hTs[:, dc, :], hT[:, dc, :], nw_sb[:, dc, e:e + 1], rms_bc[:, :], ALU.mult, ALU.mult),
                deps=[f"rbc{eb}_0", f"rbc{eb}_1", "cols_nw"] +
                     ([f"gmm{e-1}_1_15", f"umm{e-1}_1_15"] if (e and dc == 0) else []),
                event=f"hTs{eb}_{dc}")
        s.add("vector", lambda v: v.engine_nop(), deps=[f"hTs{eb}_{dc}" for dc in range(8)],
              event=f"hTs_done_e{e}")
        for half in range(2):
            hof = half * SBK
            for fc in range(16):
                kk = (e * 2 + half) * 16 + fc
                buf = kk % 2
                wdeps = [f"gue{e}_{half}_{fc-2}"] if fc >= 2 else (
                    [f"gue{e}_{half-1}_{14+fc}"] if half else (
                        [f"gue{e-1}_1_{14+fc}"] if e else []))
                s.add("gpsimd", lambda g, e=e, fc=fc, buf=buf: g.dma_start(
                    out=wgc[buf][:], in_=Wg[e, :, fc * 128:(fc + 1) * 128].rearrange("(dc p) q -> p dc q", p=128)),
                    deps=wdeps, dma=True, event=f"wgl{e}_{half}_{fc}")
                s.add("gpsimd", lambda g, e=e, fc=fc, buf=buf: g.dma_start(
                    out=wuc[buf][:], in_=Wu[e, :, fc * 128:(fc + 1) * 128].rearrange("(dc p) q -> p dc q", p=128)),
                    dma=True, event=f"wul{e}_{half}_{fc}")
                gdeps = [f"wgl{e}_{half}_{fc}", f"hTs_done_e{e}"] + (
                    [f"gge{e}_{half}_{fc-1}"] if fc else (
                        [f"gge{e}_{half-1}_15"] if half else (
                            [f"rmsrow{eb}_1"] + ([f"gge{e-1}_1_15"] if e else []))))
                for dc in range(8):
                    s.add("tensor", lambda tt, dc=dc, fc=fc, buf=buf, hof=hof: tt.matmul(
                        pc_[0][:, 0:SBK], wgc[buf][:, dc, :], hTs[:, dc, hof:hof + SBK],
                        start=(dc == 0), stop=(dc == 7)),
                        deps=gdeps if dc == 0 else [], event=f"gmm{e}_{half}_{fc}" if dc == 7 else None)
                s.add("scalar", lambda ee: ee.activation(gg[:], pc_[0][:, 0:SBK], AF.Gelu),
                      deps=[f"gmm{e}_{half}_{fc}"] + ([f"gue{e}_{half}_{fc-1}"] if fc else []),
                      event=f"gge{e}_{half}_{fc}")
                udeps = [f"wul{e}_{half}_{fc}", f"hTs_done_e{e}"] + (
                    [f"uue{e}_{half}_{fc-1}"] if fc else (
                        [f"uue{e}_{half-1}_15"] if half else [f"rbc{eb}_{1}"] + ([f"uue{e-1}_1_15"] if e else [])))
                for dc in range(8):
                    s.add("tensor", lambda tt, dc=dc, fc=fc, buf=buf, hof=hof: tt.matmul(
                        pc_[1][:, 0:SBK], wuc[buf][:, dc, :], hTs[:, dc, hof:hof + SBK],
                        start=(dc == 0), stop=(dc == 7)),
                        deps=udeps if dc == 0 else [], event=f"umm{e}_{half}_{fc}" if dc == 7 else None)
                s.add("vector", lambda v: v.tensor_copy(uu[:], pc_[1][:, 0:SBK]),
                      deps=[f"umm{e}_{half}_{fc}"] + ([f"gue{e}_{half}_{fc-1}"] if fc else []),
                      event=f"uue{e}_{half}_{fc}")
                s.add("vector", lambda v, fc=fc: v.tensor_mul(guT[:, fc, :], gg[:], uu[:]),
                      deps=[f"gge{e}_{half}_{fc}", f"uue{e}_{half}_{fc}"] +
                           ([] if fc else ([f"dmm{e}_{half-1}_7"] if half else
                                           ([f"dmm{e-1}_1_7"] if e else []))),
                      event=f"gue{e}_{half}_{fc}")
            for dc in range(8):
                kk = (e * 2 + half) * 8 + dc
                wdeps = [f"dmm{e}_{half}_{dc-1}"] if dc >= 1 else (
                    [f"dmm{e}_{half-1}_7"] if half else ([f"dmm{e-1}_1_7"] if e else []))
                s.add("gpsimd", lambda g, e=e, dc=dc: g.dma_start(
                    out=wdc[0][:], in_=Wd[e, :, dc * 128:(dc + 1) * 128].rearrange("(fc p) q -> p fc q", p=128)),
                    deps=wdeps, dma=True, event=f"wdl{e}_{half}_{dc}")
                ddeps = [f"wdl{e}_{half}_{dc}"] + [f"gue{e}_{half}_{fc}" for fc in range(16)] + (
                    [f"yTe{e}_{half}_{dc-1}"] if dc else (
                        [f"yTe{e}_{half-1}_7"] if half else (
                            [f"ytm_done_e{e-1}"] if e else [])))
                for fc in range(16):
                    s.add("tensor", lambda tt, fc=fc, hof=hof: tt.matmul(
                        pe_[0][:, 0:SBK], wdc[0][:, fc, :], guT[:, fc, :],
                        start=(fc == 0), stop=(fc == 15)),
                        deps=ddeps if fc == 0 else [], event=f"dmm{e}_{half}_{dc}" if fc == 15 else None)
                s.add("vector", lambda v, dc=dc, hof=hof: v.scalar_tensor_tensor(
                    yT[:, dc, hof:hof + SBK], pe_[0][:, 0:SBK], 1.0,
                    hT[:, dc, hof:hof + SBK], ALU.mult, ALU.add),
                    deps=[f"dmm{e}_{half}_{dc}"], event=f"yTe{e}_{half}_{dc}")
        for dc in range(8):
            s.add("vector", lambda v: v.engine_nop(),
                  deps=[f"yTe{e}_0_{dc}", f"yTe{e}_1_{dc}"], event=f"dme2_{e}_{dc}")
        for sb in range(BUCKET // 128):
            tdeps = [f"dme2_{e}_{dc}" for dc in range(8)] + ([f"ytme{e}_{sb-1}"] if sb else []) + \
                    ([f"xTe{TILES_PC-1}"] if e == 0 and sb == 0 else [])
            for dc in range(8):
                s.add("tensor", lambda tt, dc=dc, sb=sb: tt.transpose(
                    ptr[:, dc * 128:(dc + 1) * 128], yT[:, dc, sb * 128:(sb + 1) * 128], ident[:]),
                    deps=tdeps if dc == 0 else [], event=f"ytp{e}_{sb}" if dc == 7 else None)
            s.add("vector", lambda v: v.tensor_copy(ytm[:], ptr[:, :]),
                  deps=[f"ytp{e}_{sb}"] + ([f"yst{e}_{sb-1}"] if sb else
                       ([f"yst{e-1}_{BUCKET // 128 - 1}"] if e else [])),
                  event=f"ytme{e}_{sb}")
            s.add("scalar", lambda ee, e=e, sb=sb: ee.dma_start(
                out=ybuckets.ap()[e * BUCKET + sb * 128:e * BUCKET + (sb + 1) * 128, :], in_=ytm[:]),
                deps=[f"ytme{e}_{sb}"], dma=True, event=f"yst{e}_{sb}")
            yst_evs.append(f"yst{e}_{sb}")
        s.add("vector", lambda v: v.engine_nop(), deps=[f"ytme{e}_{sb}" for sb in range(BUCKET // 128)],
              event=f"ytm_done_e{e}")

    st_evs = []
    for blk in range(TPC // 512):
        fb = f"f{blk}"
        for dc in range(8):
            s.add("vector", lambda v, dc=dc, blk=blk: v.tensor_mul(
                fsqs[:], xT[:, dc, blk * 512:(blk + 1) * 512], xT[:, dc, blk * 512:(blk + 1) * 512]),
                deps=[f"xTe{TILES_PC-1}"] + ([f"fsqm{fb}_{dc-1}"] if dc else
                     ([f"rmsrowe{nexp-1}_0", f"frbcf{blk-1}"] if blk else [f"rmsrowe{nexp-1}_0"])),
                event=f"fsq{fb}_{dc}")
            s.add("tensor", lambda tt, dc=dc: tt.matmul(
                pa[0:1, 0:512], onescol[:, :], fsqs[:, :],
                start=(dc == 0), stop=(dc == 7)),
                deps=[f"fsq{fb}_{dc}", "onesc"] +
                     ([f"bbce{TILES_PC-1}"] if False else []),
                event=f"fsqm{fb}_{dc}")
        s.add("vector", lambda v: v.tensor_scalar(frms_row[:], pa[0:1, 0:512], 1.0 / D, RMSEPS, ALU.mult, ALU.add),
              deps=[f"fsqm{fb}_7"], event=f"frr{fb}")
        s.add("scalar", lambda ee: ee.activation(frms_row[:], frms_row[:], AF.Ln),
              deps=[f"frr{fb}"], event=f"frln{fb}")
        s.add("scalar", lambda ee: ee.activation(frms_row[:], frms_row[:], AF.Exp, scale=-0.5),
              deps=[f"frln{fb}"], event=f"frexp{fb}")
        s.add("tensor", lambda tt: tt.matmul(pa[:, 0:512], onesrowF[:, :], frms_row[:, :], start=True, stop=True),
              deps=[f"frexp{fb}", "onesF"], event=f"frbm{fb}")
        s.add("vector", lambda v: v.tensor_copy(frms_bc, pa[:, 0:512]),
              deps=[f"frbm{fb}", f"hTs_done_e{nexp-1}"], event=f"frbc{fb}")
        for dc in range(8):
            s.add("vector", lambda v, dc=dc, blk=blk: v.scalar_tensor_tensor(
                fhT[:, dc, :], xT[:, dc, blk * 512:(blk + 1) * 512],
                fnw_col[:, dc, 0:1], frms_bc, ALU.mult, ALU.mult),
                deps=[f"frbc{fb}", "cols_fnw"] + ([f"fdm{f'f{blk-1}'}_7"] if blk else []),
                event=f"fh{fb}_{dc}")
        fh_evs = [f"fh{fb}_{dc}" for dc in range(8)]
        for fc in range(4):
            gdeps = fh_evs + ["ld_fwg"] + ([f"fgge{fb}_{fc-1}"] if fc else
                     ([f"gge{nexp-1}_1_15"] if blk == 0 else [f"fgge{f'f{blk-1}'}_3"]))
            for dc in range(8):
                s.add("tensor", lambda tt, dc=dc, fc=fc, blk=blk: tt.matmul(
                    pc_[0][:, 0:512], fwg_sb[:, dc, fc, :], fhT[:, dc, :],
                    start=(dc == 0), stop=(dc == 7)),
                    deps=gdeps if dc == 0 else [], event=f"fgm{fb}_{fc}" if dc == 7 else None)
            s.add("scalar", lambda ee: ee.activation(fgg[:], pc_[0][:, 0:512], AF.Gelu),
                  deps=[f"fgm{fb}_{fc}"] + ([f"fgu{fb}_{fc-1}"] if fc else []), event=f"fgge{fb}_{fc}")
            udeps = fh_evs + ["ld_fwu"] + ([f"fuue{fb}_{fc-1}"] if fc else
                     ([f"uue{nexp-1}_1_15"] if blk == 0 else [f"fuue{f'f{blk-1}'}_3"]))
            for dc in range(8):
                s.add("tensor", lambda tt, dc=dc, fc=fc, blk=blk: tt.matmul(
                    pc_[1][:, 0:512], fwu_sb[:, dc, fc, :], fhT[:, dc, :],
                    start=(dc == 0), stop=(dc == 7)),
                    deps=udeps if dc == 0 else [], event=f"fum{fb}_{fc}" if dc == 7 else None)
            s.add("vector", lambda v: v.tensor_copy(fuu[:], pc_[1][:, 0:512]),
                  deps=[f"fum{fb}_{fc}"] + ([f"fgu{fb}_{fc-1}"] if fc else []), event=f"fuue{fb}_{fc}")
            s.add("vector", lambda v, fc=fc: v.tensor_mul(fgu[:, fc, :], fgg[:], fuu[:]),
                  deps=[f"fgge{fb}_{fc}", f"fuue{fb}_{fc}"], event=f"fgu{fb}_{fc}")
        for dc in range(8):
            ddeps = [f"fgu{fb}_{fc}" for fc in range(4)] + ["ld_fwd"] + \
                    ([f"fde{fb}_{dc-1}"] if dc else
                     ([f"dmm{nexp-1}_1_7"] if blk == 0 else [f"fde{f'f{blk-1}'}_7"]))
            for fc in range(4):
                s.add("tensor", lambda tt, dc=dc, fc=fc: tt.matmul(
                    pd_[0][:, 0:512], fwd_sb[:, fc, dc, :], fgu[:, fc, :],
                    start=(fc == 0), stop=(fc == 3)),
                    deps=ddeps if fc == 0 else [], event=f"fdm{fb}_{dc}" if fc == 3 else None)
            s.add("vector", lambda v, dc=dc, blk=blk: v.scalar_tensor_tensor(
                foutT[:, dc, :], pd_[0][:, 0:512], 1.0,
                xT[:, dc, blk * 512:(blk + 1) * 512], ALU.mult, ALU.add),
                deps=[f"fdm{fb}_{dc}"] + ([f"ftp{blk*4-1}"] if (blk and dc == 0) else []),
                event=f"fde{fb}_{dc}")
        fall_blk = [f"fde{fb}_{dc}" for dc in range(8)]

        for ti in range(4):
            t = blk * 4 + ti
            tb = f"c{t}"
            for k in range(2):
                s.add("gpsimd", lambda g, t=t, k=k: g.indirect_dma_start(
                    gy[k][:], None, ybuckets.ap()[:, :], IndirectOffsetOnAxis(dsti[k][:, t:t + 1], 0),
                    bounds_check=breg_h["r"], oob_is_err=False),
                    deps=(yst_evs if t == 0 else []) + [f"gyz{k}"] + ([f"cb{t-1}"] if t else []),
                    dma=True, event=f"gy{tb}_{k}")
            s.add("scalar", lambda e, t=t: e.dma_start(out=x1s6[:], in_=x1[t * 128:(t + 1) * 128, :]),
                  deps=[f"rfin{t-1}"] if t else [f"xsc{TILES_PC-1}"], dma=True, event=f"x1l6_{t}")
            s.add("vector", lambda v, t=t: v.scalar_tensor_tensor(
                comb[:], gy[0][:], wk[:, t, 0:1], x1s6[:], ALU.mult, ALU.add),
                deps=[f"gy{tb}_0", f"wkf{t}_0", f"x1l6_{t}"] + ([f"st_b{t-1}"] if t else []), event=f"cmb0{tb}")
            s.add("vector", lambda v, t=t: v.scalar_tensor_tensor(
                comb2[:], gy[1][:], wk[:, t, 1:2], comb[:], ALU.mult, ALU.add),
                deps=[f"gy{tb}_1", f"cmb0{tb}", f"wkf{t}_1"], event=f"cb{t}")
            for dc in range(8):
                s.add("tensor", lambda tt, ti=ti, dc=dc: tt.transpose(
                    ptr[:, dc * 128:(dc + 1) * 128], foutT[:, dc, ti * 128:(ti + 1) * 128], ident[:]),
                    deps=(fall_blk + ([f"ytm_done_e{nexp-1}"] if t == 0 else []) + ([f"ftme{t-1}"] if t else [])) if dc == 0 else [],
                    event=f"ftp{t}" if dc == 7 else None)
            s.add("vector", lambda v: v.tensor_copy(fout_tm[:], ptr[:, :]), deps=[f"ftp{t}"], event=f"ftme{t}")
            s.add("vector", lambda v, t=t: v.select(
                selo[:], ovfu[:, t:t + 1].to_broadcast([128, D]), fout_tm[:], comb2[:]),
                deps=[f"ftme{t}", f"cb{t}", f"ovf_{t}"], event=f"sel{t}")
            s.add("vector", lambda v, t=t: v.tensor_add(rfin[:], selo[:], x1s6[:]),
                  deps=[f"sel{t}"], event=f"rfin{t}")
            emit_layernorm(s, f"b{t}", rfin[:], outt[:], lnsc,
                           deps=[f"rfin{t}"] + ([f"st_b{t-1}"] if t else []),
                           store_fn=lambda e, t=t: e.dma_start(out=x2[t * 128:(t + 1) * 128, :], in_=outt[:]),
                           store_dep_event=f"st_b{t-1}" if t else None)
            st_evs.append(f"st_b{t}")

    if debug:
        dbg_scr = p.sb("dbg_scr", [128, D], F32)
        lastev = []
        for k in range(2):
            s.add("gpsimd", lambda g, k=k: g.dma_start(out=dbg_dst[k], in_=dsti[k][:, :]),
                  deps=st_evs, dma=True, event=f"dbgd{k}")
        s.add("gpsimd", lambda g: g.dma_start(out=dbg_wk[:, :, :], in_=wk[:, :, :]),
              deps=st_evs, dma=True, event="dbgwk")
        s.add("gpsimd", lambda g: g.dma_start(out=dbg_cnt[:, :], in_=cnt_own[:8, :]),
              deps=st_evs, dma=True, event="dbgcnt")
        s.add("gpsimd", lambda g: g.dma_start(out=dbg_bo[:, :], in_=base_o[:8, :]),
              deps=st_evs, dma=True, event="dbgbo")
        s.add("gpsimd", lambda g: g.dma_start(out=dbg_ovf[:, :], in_=ovf[:, :]),
              deps=st_evs, dma=True, event="dbgovf")
        for dc in range(8):
            s.add("vector", lambda v, dc=dc: v.tensor_copy(comb[:, 0:BUCKET], hT[:, dc, :]),
                  deps=st_evs + ([f"dbghs{dc-1}"] if dc else []), event=f"dbghv{dc}")
            s.add("gpsimd", lambda g, dc=dc: g.dma_start(out=dbg_hT[:, dc, :], in_=comb[:, 0:BUCKET]),
                  deps=[f"dbghv{dc}"], dma=True, event=f"dbgh{dc}")
            s.add("vector", lambda v, dc=dc: v.tensor_copy(comb[:, 0:BUCKET], hTs[:, dc, :]),
                  deps=[f"dbgh{dc}"], event=f"dbghsv{dc}")
            s.add("gpsimd", lambda g, dc=dc: g.dma_start(out=dbg_hTs[:, dc, :], in_=comb[:, 0:BUCKET]),
                  deps=[f"dbghsv{dc}"], dma=True, event=f"dbghs{dc}")
        s.add("vector", lambda v: v.tensor_copy(comb[:, 0:BUCKET], rms_bc[:, :]),
              deps=st_evs + ["dbghs7"], event="dbgrv")
        s.add("gpsimd", lambda g: g.dma_start(out=dbg_rms[:, :], in_=comb[:, 0:BUCKET]),
              deps=["dbgrv"], dma=True, event="dbgrm")
        for r in range(0, 8 * BUCKET, 128):
            ev1 = f"dbgb{r}"
            s.add("scalar", lambda e, r=r: e.dma_start(out=dbg_scr[:].bitcast(BF16)[:, 0:D], in_=buckets.ap()[r:r+128, :]),
                  deps=st_evs + ([f"dbgbc{r-128}"] if r else []), dma=True, event=f"dbgbl{r}")
            s.add("vector", lambda v: v.tensor_copy(outt[:], dbg_scr[:].bitcast(BF16)[:, 0:D]),
                  deps=[f"dbgbl{r}"], event=f"dbgbv{r}")
            s.add("scalar", lambda e, r=r: e.dma_start(out=dbg_bkt[r:r+128, :], in_=outt[:]),
                  deps=[f"dbgbv{r}"], dma=True, event=f"dbgbc{r}")
            s.add("scalar", lambda e, r=r: e.dma_start(out=dbg_scr[:].bitcast(BF16)[:, 0:D], in_=ybuckets.ap()[r:r+128, :]),
                  deps=[f"dbgbc{r}"], dma=True, event=f"dbgyl{r}")
            s.add("vector", lambda v: v.tensor_copy(outt[:], dbg_scr[:].bitcast(BF16)[:, 0:D]),
                  deps=[f"dbgyl{r}"], event=f"dbgyv{r}")
            s.add("scalar", lambda e, r=r: e.dma_start(out=dbg_ybkt[r:r+128, :], in_=outt[:]),
                  deps=[f"dbgyv{r}"], dma=True, event=f"dbgyc{r}")
            lastev.append(f"dbgyc{r}")
        s.finish(*(st_evs + lastev + ["dbgd0", "dbgd1", "dbgwk", "dbgcnt", "dbgbo", "dbgovf", "dbgrm"]))
    elif ctx is None:
        s.finish(*st_evs)
    if ctx is None:
        s.emit(block)
        return p
    return st_evs


def _phase_b_in_maps(inputs):
    x1full = np.ascontiguousarray(inputs["x1"], dtype=np.float32)
    maps = []
    for c in range(NCORES):
        m = dict(
            x1s=x1full[c * TPC:(c + 1) * TPC],
            Wr=inputs["L0_ffn_Wr"], norm_w=inputs["L0_ffn_norm_w"],
            Wg=inputs["L0_ffn_Wg"], Wu=inputs["L0_ffn_Wu"], Wd=inputs["L0_ffn_Wd"],
            f_norm=inputs["L0_ffn_f_norm"], fWg=inputs["L0_ffn_fWg"],
            fWu=inputs["L0_ffn_fWu"], fWd=inputs["L0_ffn_fWd"],
            prevmask=(np.arange(8) < c).astype(np.float32).reshape(1, 8),
        )
        maps.append({k: np.ascontiguousarray(v, np.float32) for k, v in m.items()})
    return maps


def run_phase_b(inputs, debug=False, nexp=8):
    p = build_phase_b(debug=debug, nexp=nexp)
    maps = _phase_b_in_maps(inputs)
    res = run_bass_kernel_spmd(p.nc, maps, core_ids=list(range(NCORES)))
    out = np.concatenate([res.results[c]["x2s"] for c in range(NCORES)], axis=0)
    if debug:
        return out, res.results
    return out




def build_phase_c(ctx=None):
    if ctx is None:
        p = Prog()
        nc = p.nc
        x2b = nc.dram_tensor("x2batch", [T, D], F32, kind="ExternalInput").ap()
        x2o = nc.dram_tensor("x2own", [TPC, D], F32, kind="ExternalInput").ap()
        x3 = nc.dram_tensor("x3s", [TPC, D], F32, kind="ExternalOutput").ap()
        _pfx = ""
    else:
        p, s = ctx["p"], ctx["s"]
        nc = p.nc
        x2b = ctx["x2b"]; x2o = ctx["x2o"]; x3 = ctx["x3"]
        _pfx = ctx["pfx"]

    xck_d = nc.dram_tensor(_pfx + "xck_d", [T, 16 * 128], BF16)
    vpad_d = nc.dram_tensor(_pfx + "vpad_d", [T, 16 * 65], BF16)
    negBT_d = nc.dram_tensor(_pfx + "negBT_d", [16, TPC], F32)
    zrows_d = nc.dram_tensor(_pfx + "zrows_d", [16, TPC], F32)

    if ctx is None:
        s = Sched(p)
        block = p.es.enter_context(nc.Block())

    NBT = T // 128
    NQT = 4
    NHG = 4
    NSMC = T // 512
    HPG = 4

    identf = p.sb("identf", [128, 128], F32)
    ident = p.sb("ident", [128, 128], BF16)
    identF = p.sb("identF", [128, 128], F32)
    onesrowF = p.sb("onesrowF", [1, 128], F32)
    onescolB = p.sb("onescolB", [128, 1], BF16)

    x2t = [p.sb(f"x2t{i}", [128, D], F32) for i in range(2)]
    xb16 = [p.sb(f"xb16{i}", [128, D], BF16) for i in range(2)]
    vint = p.sb("vint", [128, 16, 65], BF16)
    xint = p.sb("xint", [128, 16, 128], BF16)
    meanrow = p.sb("meanrow", [1, D], F32)
    mean_bc = p.sb("mean_bc", [128, D], F32)
    xcf = p.sb("xcf", [128, D], F32)
    xcb = p.sb("xcb", [128, D], BF16)
    sqb = p.sb("sqb", [128, D], BF16)
    n2t = p.sb("n2t", [128, 16], F32)
    nmax = p.sb("nmax", [128, 16], F32)
    n2own = p.sb("n2own", [128, TILES_PC, 16], F32)
    m2r = p.sb("m2r", [16, 1], F32)
    lnm2bc = p.sb("lnm2bc", [128, 16], F32)
    lnn2 = p.sb("lnn2", [128, 16], F32)
    bneg = p.sb("bneg", [128, 16], F32)
    negBT = p.sb("negBT", [16, TPC], F32)
    xcT = p.sb("xcT", [128, 8, TPC], BF16)

    qrhs = p.sb("qrhs", [65, HPG, 512], BF16)
    kbuf = [p.sb(f"kbuf{i}", [128, HPG, 512], BF16) for i in range(3)]
    vbuf = [p.sb(f"vbuf{i}", [128, 4, 16 * 65], BF16) for i in range(3)]
    Ebuf = [p.sb(f"Ebuf{i}", [128, 512], BF16) for i in range(4)]
    cstage = p.sb("cstage", [64, 512], BF16)
    ctxe = p.sb("ctxe", [65, 512], F32)
    zrows = p.sb("zrows", [16, TPC], F32)
    zinv = p.sb("zinv", [128, 16], F32)
    attn_tok = p.sb("attn_tok", [128, 4, D], F32)
    scl = p.sb("scl", [128, D], F32)
    rfin = p.sb("rfin", [128, D], F32)
    outt = p.sb("outt", [128, D], F32)
    lnsc = make_ln_scratch(p, "c")

    ptr = p.ps("ptr", [128, 1024], BF16)
    psc = [p.ps(f"psc{i}", [128, 512], F32) for i in range(3)]
    pctx = [p.ps(f"pctx{i}", [128, 512], F32) for i in range(4)]

    emit_setup_ident(s, identf, ident, identF)
    s.add("vector", lambda v: v.memset(onesrowF[:], 1.0), event="onesF")
    s.add("vector", lambda v: v.memset(onescolB[:], 1.0), event="onesB")
    s.add("vector", lambda v: v.memset(vint[:, :, 64:65], 1.0), event="vones")
    s.add("vector", lambda v: v.memset(xint[:, :, 64:128], 0.0), event="xz")
    s.add("vector", lambda v: v.memset(xint[:, :, 64:65], 1.0), deps=["xz"], event="xones")
    s.add("vector", lambda v: v.engine_nop(), event="kones0")
    s.add("vector", lambda v: v.engine_nop(), event="kones1")

    for bt in range(NBT):
        bi = bt % 2
        s.add("scalar", lambda e, bt=bt, bi=bi: e.dma_start(out=x2t[bi][:], in_=x2b[bt * 128:(bt + 1) * 128, :]),
              deps=[f"vpe{bt-2}"] if bt >= 2 else [], dma=True, event=f"p1l{bt}")
        s.add("vector", lambda v, bi=bi: v.tensor_copy(xb16[bi][:], x2t[bi][:]),
              deps=[f"p1l{bt}"] + ([f"msum{bt-1}_1"] if bt else []), event=f"p1c{bt}")
        for h2 in range(2):
            s.add("tensor", lambda t, bi=bi, h2=h2, bt=bt: t.matmul(
                psc[h2][0:1, 0:512], onescolB[:, :], xb16[bi][:, h2 * 512:(h2 + 1) * 512],
                start=(bt == 0), stop=(bt == NBT - 1)),
                deps=([f"p1c{bt}", "onesB"] if bt == 0 else [f"p1c{bt}"]) if h2 == 0 else [],
                event=f"msum{bt}_{h2}")
        s.add("vector", lambda v, bi=bi: v.tensor_copy(
            vint[:, :, 0:64], xb16[bi][:, :].rearrange("p (h j) -> p h j", j=64)),
            deps=[f"p1c{bt}", "vones"] + ([f"vps{bt-1}"] if bt else []), event=f"vpi{bt}")
        s.add("gpsimd", lambda g, bt=bt: g.dma_start(
            out=vpad_d.ap()[bt * 128:(bt + 1) * 128, :], in_=vint[:, :, :].rearrange("p h c -> p (h c)")),
            deps=[f"vpi{bt}"], dma=True, event=f"vps{bt}")
        s.add("vector", lambda v: v.engine_nop(), deps=[f"vpi{bt}"], event=f"vpe{bt}")
    s.add("vector", lambda v: v.tensor_scalar(meanrow[0:1, 0:512], psc[0][0:1, 0:512], 1.0 / T, None, ALU.mult),
          deps=[f"msum{NBT-1}_0", f"msum{NBT-1}_1"], event="mr0")
    s.add("vector", lambda v: v.tensor_scalar(meanrow[0:1, 512:1024], psc[1][0:1, 0:512], 1.0 / T, None, ALU.mult),
          deps=["mr0"], event="mr1")
    for h2 in range(2):
        s.add("tensor", lambda t, h2=h2: t.matmul(psc[h2][:, 0:512], onesrowF[:, :],
                                                  meanrow[0:1, h2 * 512:(h2 + 1) * 512], start=True, stop=True),
              deps=["mr1", "onesF"], event=f"mbm{h2}")
        s.add("vector", lambda v, h2=h2: v.tensor_copy(mean_bc[:, h2 * 512:(h2 + 1) * 512], psc[h2][:, 0:512]),
              deps=[f"mbm{h2}"], event=f"mbc{h2}")

    s.add("vector", lambda v: v.memset(nmax[:], 0.0), event="nm0")
    for bt in range(NBT):
        bi = bt % 2
        s.add("scalar", lambda e, bt=bt, bi=bi: e.dma_start(out=x2t[bi][:], in_=x2b[bt * 128:(bt + 1) * 128, :]),
              deps=[f"nrm{bt-2}"] if bt >= 2 else ["mbc0", "mbc1"], dma=True, event=f"p2l{bt}")
        s.add("vector", lambda v, bi=bi: v.tensor_sub(xcf[:], x2t[bi][:], mean_bc[:]),
              deps=[f"p2l{bt}"] + ([f"xcks{bt-1}"] if bt else ["mbc0", "mbc1"]), event=f"xc{bt}")
        s.add("vector", lambda v: v.tensor_copy(xcb[:], xcf[:]), deps=[f"xc{bt}"], event=f"xcb{bt}")
        s.add("vector", lambda v, bt=bt: v.tensor_copy(
            xint[:, :, 0:64], xcb[:, :].rearrange("p (h j) -> p h j", j=64)),
            deps=[f"xcb{bt}", "xones"] + ([f"xcks{bt-1}"] if bt else []), event=f"xci{bt}")
        s.add("gpsimd", lambda g, bt=bt: g.dma_start(
            out=xck_d.ap()[bt * 128:(bt + 1) * 128, :], in_=xint[:, :, :].rearrange("p h c -> p (h c)")),
            deps=[f"xci{bt}"], dma=True, event=f"xcks{bt}")
        s.add("vector", lambda v: v.tensor_mul(sqb[:], xcb[:], xcb[:]), deps=[f"xcb{bt}"], event=f"sq{bt}")
        s.add("vector", lambda v: v.tensor_reduce(n2t[:], sqb[:].rearrange("p (h j) -> p h j", j=64),
                                                  axis=AX.X, op=ALU.add),
              deps=[f"sq{bt}"], event=f"n2{bt}")
        s.add("vector", lambda v: v.tensor_max(nmax[:], nmax[:], n2t[:]),
              deps=[f"n2{bt}", "nm0"] + ([f"nrm{bt-1}"] if bt else []), event=f"nrm{bt}")
    s.add("tensor", lambda t: t.transpose(psc[0][0:16, 0:128], nmax[:, :], identF[:]),
          deps=[f"nrm{NBT-1}", "identF", "mbc0"], event="nmt")
    s.add("vector", lambda v: v.tensor_reduce(m2r[:16, :], psc[0][0:16, 0:128], axis=AX.X, op=ALU.max),
          deps=["nmt"], event="m2r")
    s.add("scalar", lambda e: e.activation(m2r[:16, :], m2r[:16, :], AF.Ln), deps=["m2r"], event="m2ln")
    s.add("tensor", lambda t: t.transpose(psc[0][0:1, 0:16], m2r[:16, 0:1], identF[:16, :16]),
          deps=["m2ln", "m2r"], event="m2t")
    s.add("vector", lambda v: v.tensor_copy(meanrow[0:1, 0:16], psc[0][0:1, 0:16]), deps=["m2t"], event="m2row")
    s.add("tensor", lambda t: t.matmul(psc[0][:, 0:16], onesrowF[:, :], meanrow[0:1, 0:16], start=True, stop=True),
          deps=["m2row", "onesF"], event="m2bm")
    s.add("vector", lambda v: v.tensor_copy(lnm2bc[:], psc[0][:, 0:16]), deps=["m2bm"], event="m2bc")

    for t0 in range(TILES_PC):
        bi = t0 % 2
        s.add("scalar", lambda e, t0=t0, bi=bi: e.dma_start(out=x2t[bi][:], in_=x2o[t0 * 128:(t0 + 1) * 128, :]),
              deps=[f"otp{t0-2}"] if t0 >= 2 else [f"p2l{NBT-1}"], dma=True, event=f"ol{t0}")
        s.add("vector", lambda v, bi=bi: v.tensor_sub(xcf[:], x2t[bi][:], mean_bc[:]),
              deps=[f"ol{t0}", f"nrm{NBT-1}"] + ([f"oxcb{t0-1}"] if t0 else [f"xcks{NBT-1}"]), event=f"oxc{t0}")
        s.add("vector", lambda v: v.tensor_copy(xcb[:], xcf[:]), deps=[f"oxc{t0}"], event=f"oxcb{t0}")
        for dc in range(8):
            s.add("tensor", lambda tt, dc=dc: tt.transpose(
                ptr[:, dc * 128:(dc + 1) * 128], xcb[:, dc * 128:(dc + 1) * 128], ident[:]),
                deps=([f"oxcb{t0}", "ident"] + ([f"oxT{t0-1}"] if t0 else [])) if dc == 0 else [],
                event=f"otp{t0}" if dc == 7 else None)
        s.add("vector", lambda v, t0=t0: v.tensor_copy(
            xcT[:, :, t0 * 128:(t0 + 1) * 128], ptr[:, :].rearrange("p (dc q) -> p dc q", q=128)),
            deps=[f"otp{t0}"], event=f"oxT{t0}")
        s.add("vector", lambda v: v.tensor_mul(sqb[:], xcb[:], xcb[:]), deps=[f"oxcb{t0}"], event=f"osq{t0}")
        s.add("vector", lambda v, t0=t0: v.tensor_reduce(n2own[:, t0, :], sqb[:].rearrange("p (h j) -> p h j", j=64),
                                                         axis=AX.X, op=ALU.add),
              deps=[f"osq{t0}"], event=f"on2{t0}")
    for t0 in range(TILES_PC):
        s.add("scalar", lambda e, t0=t0: e.activation(lnn2[:], n2own[:, t0, :], AF.Ln),
              deps=[f"on2{t0}", "m2bc"] + ([f"bT{t0-1}"] if t0 else []), event=f"bln{t0}")
        s.add("vector", lambda v: v.tensor_add(lnn2[:], lnn2[:], lnm2bc[:]), deps=[f"bln{t0}"], event=f"badd{t0}")
        s.add("scalar", lambda e: e.activation(bneg[:], lnn2[:], AF.Exp, scale=0.5), deps=[f"badd{t0}"], event=f"bexp{t0}")
        s.add("vector", lambda v: v.tensor_scalar(bneg[:], bneg[:], -1.0, None, ALU.mult),
              deps=[f"bexp{t0}"], event=f"bnegv{t0}")
        s.add("tensor", lambda tt, t0=t0: tt.transpose(psc[0][0:16, 0:128], bneg[:, :], identF[:]),
              deps=[f"bnegv{t0}"] + ([f"bT{t0-1}"] if t0 else ["m2bc"]), event=f"bt{t0}")
        s.add("vector", lambda v, t0=t0: v.tensor_copy(negBT[:16, t0 * 128:(t0 + 1) * 128], psc[0][0:16, 0:128]),
              deps=[f"bt{t0}"], event=f"bTv{t0}")
        s.add("gpsimd", lambda g, t0=t0: g.dma_start(
            out=negBT_d.ap()[:, t0 * 128:(t0 + 1) * 128], in_=negBT[:16, t0 * 128:(t0 + 1) * 128]),
            deps=[f"bTv{t0}"], dma=True, event=f"bT{t0}")

    ev_store = []
    last_psc = [f"bT{TILES_PC-1}", "mbc1", None]
    last_ptr = f"oxT{TILES_PC-1}"
    last_ctx_evac = [None] * HPG
    last_E_reader = [None] * 4
    last_qrhs = None
    prev_qb_c = None
    CTX_LAG = 3
    pend_c = []
    last_cmm_c = [None]

    def _emit_cctx(item):
        qb2, smc2, sc2, i2, h2, bi2, pq2 = item
        first = (smc2 == 0 and sc2 == 0)
        last = (smc2 == NSMC - 1 and sc2 == 3)
        cdeps = [f"exp{qb2}_{smc2}_{sc2}_{i2}"]
        if first and last_ctx_evac[i2]:
            cdeps.append(last_ctx_evac[i2])
        s.add("tensor", lambda tt, i=i2, h=h2, sc=sc2, bi=bi2, pq=pq2, first=first, last=last: tt.matmul(
            pctx[i][0:65, :], vbuf[bi][:, sc, h * 65:(h + 1) * 65], Ebuf[pq][:, :],
            start=first, stop=last),
            deps=cdeps, event=f"cmm{qb2}_{smc2}_{sc2}_{i2}")
        last_E_reader[pq2] = f"cmm{qb2}_{smc2}_{sc2}_{i2}"
        last_cmm_c[0] = f"cmm{qb2}_{smc2}_{sc2}_{i2}"
    last_attn_reader = None
    for qt in range(NQT):
        zs_evs = []
        for hg in range(NHG):
            heads = [hg * HPG + i for i in range(HPG)]
            qb = f"q{qt}h{hg}"
            qdeps = [f"oxT{TILES_PC-1}", f"bT{TILES_PC-1}"]
            if last_qrhs:
                qdeps.append(last_qrhs)
            for i, h in enumerate(heads):
                s.add("vector", lambda v, h=h, i=i, qt=qt: v.tensor_copy(
                    qrhs[0:64, i, :],
                    xcT[(h % 2) * 64:(h % 2) * 64 + 64, h // 2, qt * 512:(qt + 1) * 512]),
                    deps=qdeps if i == 0 else [], event=f"qr{qb}_{i}")
            s.add("gpsimd", lambda g, qt=qt, h0=heads[0]: g.dma_start(
                out=qrhs[64:65, :, :], in_=negBT_d.ap()[h0:h0 + HPG, qt * 512:(qt + 1) * 512]),
                deps=[f"qr{qb}_{i}" for i in range(HPG)], dma=True, event=f"qrb{qb}")
            ev_q = f"qrb{qb}"
            for smc in range(NSMC):
                bi = smc % 3
                kdeps = [ev_q, f"xcks{NBT-1}", "kones0", "kones1"]
                if smc >= 2:
                    kdeps.append(f"iter{qb}_{smc-2}")
                elif prev_qb_c is not None:
                    kdeps.append(f"iter{prev_qb_c}_{NSMC-2+smc}")
                for i, h in enumerate(heads):
                    s.add("sync", lambda e, h=h, i=i, smc=smc, bi=bi: e.dma_start(
                        out=kbuf[bi][:, i, :], in_=xck_d.ap()[smc * 512:(smc + 1) * 512, h * 128:(h + 1) * 128],
                        transpose=True),
                        deps=kdeps if i == 0 else [], dma=True, event=f"kl{qb}_{smc}_{i}")
                s.add("sync", lambda e, smc=smc, bi=bi: e.dma_start(
                    out=vbuf[bi][:, :, :],
                    in_=vpad_d.ap()[smc * 512:(smc + 1) * 512, :].rearrange("(a p) c -> p a c", p=128)),
                    deps=[f"vps{NBT-1}"] + kdeps[-1:] if smc >= 2 or prev_qb_c is not None else [f"vps{NBT-1}"],
                    dma=True, event=f"vl{qb}_{smc}")
                for sc in range(4):
                    for i, h in enumerate(heads):
                        it = smc * 4 * HPG + sc * HPG + i
                        pp = it % 3
                        pq = it % 4
                        mdeps = [f"kl{qb}_{smc}_{HPG-1}", f"vl{qb}_{smc}"]
                        if last_psc[pp]:
                            mdeps.append(last_psc[pp])
                        s.add("tensor", lambda tt, i=i, sc=sc, bi=bi, pp=pp: tt.matmul(
                            psc[pp][:, :], kbuf[bi][0:65, i, sc * 128:(sc + 1) * 128], qrhs[:, i, :],
                            start=True, stop=True),
                            deps=mdeps, event=f"smm{qb}_{smc}_{sc}_{i}")
                        edeps = [f"smm{qb}_{smc}_{sc}_{i}"]
                        if last_E_reader[pq]:
                            edeps.append(last_E_reader[pq])
                        s.add("scalar", lambda e, pp=pp, pq=pq: e.activation(Ebuf[pq][:], psc[pp][:, :], AF.Exp),
                              deps=edeps, event=f"exp{qb}_{smc}_{sc}_{i}")
                        last_psc[pp] = f"exp{qb}_{smc}_{sc}_{i}"
                        pend_c.append((qb, smc, sc, i, h, bi, pq))
                        if len(pend_c) > CTX_LAG:
                            _emit_cctx(pend_c.pop(0))
                if smc == NSMC - 1:
                    while pend_c:
                        _emit_cctx(pend_c.pop(0))
                s.add("vector", lambda v: v.engine_nop(),
                      deps=[f"exp{qb}_{smc}_{3}_{HPG-1}", last_cmm_c[0]],
                      event=f"iter{qb}_{smc}")
            for i, h in enumerate(heads):
                edeps = [f"cmm{qb}_{NSMC-1}_{3}_{i}"] + ([f"zs{qb}_{i-1}"] if i else [])
                s.add("vector", lambda v, i=i: v.tensor_copy(ctxe[:, :], pctx[i][0:65, :]),
                      deps=edeps, event=f"ce0_{qb}_{i}")
                last_ctx_evac[i] = f"ce0_{qb}_{i}"
                s.add("gpsimd", lambda g, h=h, qt=qt: g.dma_start(
                    out=zrows_d.ap()[h:h + 1, qt * 512:(qt + 1) * 512], in_=ctxe[64:65, :]),
                    deps=[f"ce0_{qb}_{i}"], dma=True, event=f"zs{qb}_{i}")
                s.add("vector", lambda v: v.tensor_copy(cstage[:, :], ctxe[0:64, :]),
                      deps=[f"ce0_{qb}_{i}"] + ([f"ctp_{qb}_{i-1}_3"] if i else []), event=f"cbf{qb}_{i}")
                for ti in range(4):
                    tdeps = [f"cbf{qb}_{i}"]
                    if ti == 0:
                        tdeps.append(last_ptr)
                        if last_attn_reader:
                            tdeps.append(last_attn_reader)
                    else:
                        tdeps.append(f"cte_{qb}_{i}_{ti-1}")
                    s.add("tensor", lambda tt, ti=ti: tt.transpose(
                        ptr[:, 0:64], cstage[:, ti * 128:(ti + 1) * 128], ident[:64, :64]),
                        deps=tdeps, event=f"ctp_{qb}_{i}_{ti}")
                    s.add("vector", lambda v, i=i, h=h, ti=ti: v.tensor_copy(
                        attn_tok[:, ti, h * 64:(h + 1) * 64], ptr[:, 0:64]),
                        deps=[f"ctp_{qb}_{i}_{ti}"], event=f"cte_{qb}_{i}_{ti}")
                last_ptr = f"cte_{qb}_{i}_{3}"
            zs_evs.append(f"zs{qb}_{HPG-1}")
            last_qrhs = f"smm{qb}_{NSMC-1}_{3}_{HPG-1}"
            prev_qb_c = qb
        ep_evs = []
        for ti in range(4):
            t0 = qt * 4 + ti
            tb = f"ep{t0}"
            if ti == 0:
                s.add("gpsimd", lambda g, qt=qt: g.dma_start(
                    out=zrows[:16, qt * 512:(qt + 1) * 512], in_=zrows_d.ap()[:, qt * 512:(qt + 1) * 512]),
                    deps=zs_evs, dma=True, event=f"zld{qt}")
            ztdeps = [f"zld{qt}", f"cte_q{qt}h{NHG-1}_{HPG-1}_3"] + [last_psc[0]]
            if ti:
                ztdeps.append(f"zi{qt}_{ti-1}")
            s.add("tensor", lambda tt, t0=t0: tt.transpose(
                psc[0][0:128, 0:16], zrows[:16, t0 * 128:(t0 + 1) * 128], identF[:16, :16]),
                deps=ztdeps, event=f"zt{tb}")
            s.add("vector", lambda v: v.tensor_copy(zinv[:], psc[0][0:128, 0:16]), deps=[f"zt{tb}"], event=f"zc{tb}")
            last_psc[0] = f"zc{tb}"
            s.add("vector", lambda v: v.reciprocal(zinv[:], zinv[:]), deps=[f"zc{tb}"], event=f"zi{qt}_{ti}")
            s.add("scalar", lambda e, t0=t0: e.dma_start(out=x2t[0][:], in_=x2o[t0 * 128:(t0 + 1) * 128, :]),
                  deps=[f"ol{TILES_PC-1}"] + ([f"rf{t0-1}"] if t0 else []), dma=True, event=f"el{t0}")
            s.add("vector", lambda v, ti=ti: v.tensor_mul(
                scl[:].rearrange("p (h j) -> p h j", j=64), attn_tok[:, ti, :].rearrange("p (h j) -> p h j", j=64),
                zinv[:, :, None].to_broadcast([128, 16, 64])),
                deps=[f"zi{qt}_{ti}"], event=f"sc{t0}")
            ep_evs.append(f"sc{t0}")
            s.add("vector", lambda v: v.tensor_add(rfin[:], scl[:], x2t[0][:]),
                  deps=[f"sc{t0}", f"el{t0}"], event=f"rf{t0}")
            emit_layernorm(s, f"c{t0}", rfin[:], outt[:], lnsc,
                           deps=[f"rf{t0}"] + ([f"st_c{t0-1}"] if t0 else []),
                           store_fn=lambda e, t0=t0: e.dma_start(out=x3[t0 * 128:(t0 + 1) * 128, :], in_=outt[:]),
                           store_dep_event=f"st_c{t0-1}" if t0 else None)
            ev_store.append(f"st_c{t0}")
        last_attn_reader = f"sc{qt * 4 + 3}"

    if ctx is None:
        s.finish(*ev_store)
        s.emit(block)
        return p
    return ev_store


def _phase_c_in_maps(inputs):
    x2full = np.ascontiguousarray(inputs["x2"], dtype=np.float32)
    maps = []
    for c in range(NCORES):
        b = c // 4
        m = dict(
            x2batch=x2full[b * T:(b + 1) * T],
            x2own=x2full[c * TPC:(c + 1) * TPC],
        )
        maps.append({k2: np.ascontiguousarray(v, np.float32) for k2, v in m.items()})
    return maps


def run_phase_c(inputs):
    p = build_phase_c()
    maps = _phase_c_in_maps(inputs)
    res = run_bass_kernel_spmd(p.nc, maps, core_ids=list(range(NCORES)))
    return np.concatenate([res.results[c]["x3s"] for c in range(NCORES)], axis=0)




NEGC = -1e30
NEGZ = -2e30


def build_phase_a(ctx=None):
    if ctx is None:
        p = Prog()
        nc = p.nc
        x0b = nc.dram_tensor("x0batch", [T, D], F32, kind="ExternalInput").ap()
        x0o = nc.dram_tensor("x0own", [TPC, D], F32, kind="ExternalInput").ap()
        qoff = nc.dram_tensor("qoff", [1, 1], F32, kind="ExternalInput").ap()
        Wq = nc.dram_tensor("Wq", [D, D], F32, kind="ExternalInput").ap()
        Wk = nc.dram_tensor("Wk", [D, D], F32, kind="ExternalInput").ap()
        Wv = nc.dram_tensor("Wv", [D, D], F32, kind="ExternalInput").ap()
        Wo = nc.dram_tensor("Wo", [D, D], F32, kind="ExternalInput").ap()
        bo = nc.dram_tensor("bo", [D], F32, kind="ExternalInput").ap()
        Wqi = nc.dram_tensor("Wqi", [D, 256], F32, kind="ExternalInput").ap()
        Wki = nc.dram_tensor("Wki", [D, 64], F32, kind="ExternalInput").ap()
        Ww = nc.dram_tensor("Ww", [D, 4], F32, kind="ExternalInput").ap()
        x1 = nc.dram_tensor("x1s", [TPC, D], F32, kind="ExternalOutput").ap()
        _pfx = ""
    else:
        p, s = ctx["p"], ctx["s"]
        nc = p.nc
        x0b = ctx["x0b"]; x0o = ctx["x0o"]; qoff = ctx["qoff"]
        Wq = ctx["Wq"]; Wk = ctx["Wk"]; Wv = ctx["Wv"]; Wo = ctx["Wo"]
        bo = ctx["bo"]; Wqi = ctx["Wqi"]; Wki = ctx["Wki"]; Ww = ctx["Ww"]
        x1 = ctx["x1"]
        _pfx = ctx["pfx"]

    kT_d = nc.dram_tensor(_pfx + "kT_d", [D, T], BF16)
    vpad_d = nc.dram_tensor(_pfx + "vpad_d", [T, 16 * 65], BF16)
    mT_d = nc.dram_tensor(_pfx + "mT_d", [T, TPC], BF16)
    qT_d = nc.dram_tensor(_pfx + "qT_d", [D, TPC], BF16)

    if ctx is None:
        s = Sched(p)
        block = p.es.enter_context(nc.Block())

    NQT = 4
    NHG = 4
    HPG = 4
    NSMC = T // 512
    NGRP = T // 512

    identf = p.sb("identf", [128, 128], F32)
    ident = p.sb("ident", [128, 128], BF16)
    identF = p.sb("identF", [128, 128], F32)
    onesrowF = p.sb("onesrowF", [1, 128], F32)

    wbig1 = p.sb("wbig1", [128, 8, 8, 128], BF16)
    wbig2 = p.sb("wbig2", [128, 8, 2, 512], BF16)
    wq_sb = wbig1
    wk_sb = wbig1
    wv_sb = wbig2
    wo_sb = wbig2[:, :, :, :].rearrange("p dc a q -> p dc (a q)").rearrange("p dc (jc j) -> p dc jc j", j=128)
    wqi_sb = p.sb("wqi_sb", [128, 8, 256], BF16)
    wki_sb = p.sb("wki_sb", [128, 8, 64], BF16)
    ww_sb = p.sb("ww_sb", [128, 8, 4], BF16)
    bo_row = p.sb("bo_row", [1, D], F32)
    bo_col = p.sb("bo_col", [128, 8, 1], F32)

    _x0t = p.sb("x0t0", [128, D], F32)
    x0t = [_x0t, _x0t]
    _xb16 = p.sb("xb160", [128, D], BF16)
    xb16 = [_xb16, _xb16]
    x0T = p.sb("x0T", [128, 8, 512], BF16)
    istage = p.sb("istage", [64, 4, 128], BF16)
    qstage = p.sb("qstage", [64, 4, 512], BF16)
    vint = p.sb("vint", [128, 16, 65], BF16)
    kevac = p.sb("kevac", [128, 512], BF16)
    vevac = p.sb("vevac", [128, 1024], BF16)
    kiT = p.sb("kiT", [64, T], BF16)

    wqiT = p.sb("wqiT", [128, 2, TPC], BF16)
    wrow = p.sb("wrow", [128, 4], F32)
    wexp = p.sb("wexp", [128, 4], F32)
    wm = p.sb("wm", [128, 1], F32)
    wsum = p.sb("wsum", [128, 1], F32)
    qi_tok = p.sb("qi_tok", [128, 256], F32)
    qi_scl = p.sb("qi_scl", [128, 256], BF16)
    qoffc = p.sb("qoffc", [128, 1], F32)
    rowpos = p.sb("rowpos", [128, 1], F32)
    tilepos = p.sb("tilepos", [128, 1], F32)
    iota512 = p.sb("iota512", [128, 512], F32)
    cmask = p.sb("cmask", [128, 512], F32)

    idx = p.sb("idx", [128, T], F32)
    m8 = p.sb("m8", [128, 8], F32)
    msb = p.sb("msb", [128, 512], BF16)
    mtr = p.sb("mtr", [128, 4, 128], BF16)
    scomb = [p.sb(f"scomb{i}", [128, 512], F32) for i in range(2)]

    kbuf = [p.sb(f"kbuf{i}", [64, HPG, 512], BF16) for i in range(3)]
    vbuf = [p.sb(f"vbuf{i}", [128, 4, 16 * 65], BF16) for i in range(3)]
    mbuf = [p.sb(f"mbuf{i}", [128, 4, 512], BF16) for i in range(3)]
    Ebuf = [p.sb(f"Ebuf{i}", [128, 512], BF16) for i in range(4)]
    Pbuf = [p.sb(f"Pbuf{i}", [128, 512], BF16) for i in range(4)]
    ctxe = p.sb("ctxe", [65, 512], F32)
    yT_sb = wqiT[:, :, :].rearrange("p a q -> p (a q)").rearrange("p (jc j) -> p jc j", j=512)
    attnT = x0T
    zrow = scomb[0]
    zbcs = cmask
    rfin = idx[:, 0:D]
    outt = idx[:, D:2 * D]
    lnsc = dict(
        lnsq=idx[:, 2 * D:3 * D],
        m=p.sb("alnm", [128, 1], F32), s2=p.sb("alns2", [128, 1], F32),
        mu=p.sb("alnmu", [128, 1], F32), var=p.sb("alnvar", [128, 1], F32),
        rstd=p.sb("alnrstd", [128, 1], F32), tmp=p.sb("alntmp", [128, 1], F32))
    ytok = idx[:, 3 * D:4 * D]

    ptr = p.ps("ptr", [128, 1024], BF16)
    psc = [p.ps(f"psc{i}", [128, 512], F32) for i in range(2)]
    pctx = [p.ps(f"pctx{i}", [128, 512], F32) for i in range(4)]
    pz = p.ps("pz", [128, 512], F32)

    emit_setup_ident(s, identf, ident, identF)
    s.add("vector", lambda v: v.memset(onesrowF[:], 1.0), event="onesF")
    s.add("vector", lambda v: v.memset(vint[:, :, 64:65], 1.0), event="vones")
    s.add("gpsimd", lambda g: g.iota(iota512[:], pattern=[[1, 512]], base=0, channel_multiplier=0,
                                     allow_small_or_imprecise_dtypes=True), event="io512")
    s.add("gpsimd", lambda g: g.iota(rowpos[:], pattern=[[0, 1]], base=0, channel_multiplier=1,
                                     allow_small_or_imprecise_dtypes=True), event="iorow")
    s.add("gpsimd", lambda g: g.dma_start(out=qoffc[0:1, :], in_=qoff[:, :]), dma=True, event="ld_qoff")
    s.add("gpsimd", lambda e: e.dma_start(out=wk_sb[:], in_=Wk.rearrange("(dc p) (jc j) -> p dc jc j", p=128, j=128)),
          dma=True, event="ld_wk")
    s.add("gpsimd", lambda e: e.dma_start(out=wv_sb[:], in_=Wv.rearrange("(dc p) (jh j) -> p dc jh j", p=128, j=512)),
          dma=True, event="ld_wv")
    s.add("gpsimd", lambda e: e.dma_start(out=wqi_sb[:], in_=Wqi.rearrange("(dc p) j -> p dc j", p=128)),
          dma=True, event="ld_wqi")
    s.add("gpsimd", lambda e: e.dma_start(out=wki_sb[:], in_=Wki.rearrange("(dc p) j -> p dc j", p=128)),
          dma=True, event="ld_wki")
    s.add("gpsimd", lambda e: e.dma_start(out=ww_sb[:], in_=Ww.rearrange("(dc p) j -> p dc j", p=128)),
          dma=True, event="ld_ww")
    s.add("sync", lambda e: e.dma_start(out=bo_row[:], in_=bo[None, :]), dma=True, event="ld_bor")
    emit_rows_to_cols(s, "bo", bo_row, 1, 8, bo_col, identF, pz, "ld_bor")
    s.add("tensor", lambda t: t.matmul(pz[:, 0:1], onesrowF[:, :], qoffc[0:1, 0:1], start=True, stop=True),
          deps=["ld_qoff", "onesF", "cols_bo"], event="qob_mm")
    s.add("vector", lambda v: v.tensor_copy(qoffc[:], pz[:, 0:1]), deps=["qob_mm"], event="qob")

    last_psc_pre = [None, None]
    last_pctx_pre = [None, None]
    for gp in range(NGRP):
        gb = f"g{gp}"
        shdone = []
        for ti in range(4):
            bt = gp * 4 + ti
            bi = ti % 2
            prevg = f"g{gp-1}"
            s.add("scalar", lambda e, bt=bt, bi=bi: e.dma_start(out=x0t[bi][:], in_=x0b[bt * 128:(bt + 1) * 128, :]),
                  deps=[f"pac{gb}_{ti-1}"] if ti >= 1 else ([f"pac{prevg}_3"] if gp else []),
                  dma=True, event=f"pal{gb}_{ti}")
            cdep = [f"pal{gb}_{ti}"]
            if ti:
                cdep.append(f"patp{gb}_{ti-1}")
            elif gp:
                cdep.append(f"patp{prevg}_3")
            s.add("vector", lambda v, bi=bi: v.tensor_copy(xb16[bi][:], x0t[bi][:]),
                  deps=cdep, event=f"pac{gb}_{ti}")
            tdep = [f"pac{gb}_{ti}", "ident"]
            if ti:
                tdep.append(f"pashTe{gb}_{ti-1}")
            elif gp:
                tdep.append(f"pashTe{prevg}_3")
            for dc in range(8):
                s.add("tensor", lambda tt, bi=bi, dc=dc: tt.transpose(
                    ptr[:, dc * 128:(dc + 1) * 128], xb16[bi][:, dc * 128:(dc + 1) * 128], ident[:]),
                    deps=tdep if dc == 0 else [], event=f"patp{gb}_{ti}" if dc == 7 else None)
            xdep = [f"patp{gb}_{ti}"]
            if gp and ti == 0:
                xdep += [f"kimm{prevg}", f"vmm{prevg}_3_1"]
            s.add("vector", lambda v, ti=ti: v.tensor_copy(
                x0T[:, :, ti * 128:(ti + 1) * 128], ptr[:, :].rearrange("p (dc q) -> p dc q", q=128)),
                deps=xdep, event=f"pashTe{gb}_{ti}")
            shdone.append(f"pashTe{gb}_{ti}")
        for jc in range(8):
            pk = jc % 2
            kdeps = shdone + ["ld_wk"]
            if last_psc_pre[pk]:
                kdeps.append(last_psc_pre[pk])
            for dc in range(8):
                s.add("tensor", lambda tt, jc=jc, dc=dc, pk=pk: tt.matmul(
                    psc[pk][:, :], wk_sb[:, dc, jc, :], x0T[:, dc, :],
                    start=(dc == 0), stop=(dc == 7)),
                    deps=kdeps if dc == 0 else [], event=f"kmm{gb}_{jc}" if dc == 7 else None)
            s.add("vector", lambda v, pk=pk: v.tensor_copy(kevac[:], psc[pk][:, :]),
                  deps=[f"kmm{gb}_{jc}"] + ([f"kst{gb}_{jc-1}"] if jc else ([f"kst{f'g{gp-1}'}_7"] if gp else [])),
                  event=f"kev{gb}_{jc}")
            last_psc_pre[pk] = f"kev{gb}_{jc}"
            s.add("gpsimd", lambda g, gp=gp, jc=jc: g.dma_start(
                out=kT_d.ap()[jc * 128:(jc + 1) * 128, gp * 512:(gp + 1) * 512], in_=kevac[:]),
                deps=[f"kev{gb}_{jc}"], dma=True, event=f"kst{gb}_{jc}")
        for ti in range(4):
            for jh in range(2):
                pk = jh
                vdeps = [f"pashTe{gb}_{ti}", "ld_wv"]
                if last_pctx_pre[pk]:
                    vdeps.append(last_pctx_pre[pk])
                for dc in range(8):
                    s.add("tensor", lambda tt, ti=ti, jh=jh, dc=dc, pk=pk: tt.matmul(
                        pctx[pk][:, :], x0T[:, dc, ti * 128:(ti + 1) * 128], wv_sb[:, dc, jh, :],
                        start=(dc == 0), stop=(dc == 7)),
                        deps=vdeps if dc == 0 else [], event=f"vmm{gb}_{ti}_{jh}" if dc == 7 else None)
                s.add("vector", lambda v, jh=jh, pk=pk: v.tensor_copy(
                    vevac[:, jh * 512:(jh + 1) * 512], pctx[pk][:, :]),
                    deps=[f"vmm{gb}_{ti}_{jh}"] + ([f"vii{gb}_{ti-1}"] if (ti and jh == 0) else []),
                    event=f"vee{gb}_{ti}_{jh}")
                last_pctx_pre[pk] = f"vee{gb}_{ti}_{jh}"
            s.add("vector", lambda v: v.tensor_copy(
                vint[:, :, 0:64], vevac[:, :].rearrange("p (h j) -> p h j", j=64)),
                deps=[f"vee{gb}_{ti}_0", f"vee{gb}_{ti}_1", "vones"] +
                     ([f"vst{gb}_{ti-1}"] if ti else ([f"vst{f'g{gp-1}'}_3"] if gp else [])),
                event=f"vii{gb}_{ti}")
            s.add("gpsimd", lambda g, gp=gp, ti=ti: g.dma_start(
                out=vpad_d.ap()[(gp * 4 + ti) * 128:(gp * 4 + ti + 1) * 128, :],
                in_=vint[:, :, :].rearrange("p h c -> p (h c)")),
                deps=[f"vii{gb}_{ti}"], dma=True, event=f"vst{gb}_{ti}")
        kideps = shdone + ["ld_wki"] + ([f"kiev{f'g{gp-1}'}"] if gp else ["qob"])
        for dc in range(8):
            s.add("tensor", lambda tt, dc=dc: tt.matmul(
                pz[0:64, :], wki_sb[:, dc, :], x0T[:, dc, :],
                start=(dc == 0), stop=(dc == 7)),
                deps=kideps if dc == 0 else [], event=f"kimm{gb}" if dc == 7 else None)
        s.add("vector", lambda v, gp=gp: v.tensor_copy(kiT[:, gp * 512:(gp + 1) * 512], pz[0:64, :]),
              deps=[f"kimm{gb}"], event=f"kiev{gb}")

    s.add("gpsimd", lambda e: e.dma_start(out=wq_sb[:], in_=Wq.rearrange("(dc p) (jc j) -> p dc jc j", p=128, j=128)),
          deps=[f"kmm{f'g{NGRP-1}'}_7"], dma=True, event="ld_wq")
    s.add("gpsimd", lambda e: e.dma_start(
        out=wbig2[:, :, :, :].rearrange("p dc a q -> p dc (a q)"),
        in_=Wo.rearrange("(dc p) j -> p dc j", p=128)),
        deps=[f"vmm{f'g{NGRP-1}'}_3_1"], dma=True, event="ld_wo")
    last_q_psc = [last_psc_pre[0], last_psc_pre[1]]
    for gp in range(TPC // 512):
        gb = f"og{gp}"
        shdone = []
        for ti in range(4):
            ot = gp * 4 + ti
            bi = ti % 2
            prevg = f"og{gp-1}"
            s.add("scalar", lambda e, ot=ot, bi=bi: e.dma_start(out=x0t[bi][:], in_=x0o[ot * 128:(ot + 1) * 128, :]),
                  deps=[f"oac{gb}_{ti-1}"] if ti >= 1 else ([f"oac{prevg}_3"] if gp else [f"pac{f'g{NGRP-1}'}_3"]),
                  dma=True, event=f"oal{gb}_{ti}")
            cdep = [f"oal{gb}_{ti}"]
            if ti:
                cdep.append(f"oatp{gb}_{ti-1}")
            elif gp:
                cdep.append(f"oatp{prevg}_3")
            else:
                cdep.append(f"patp{f'g{NGRP-1}'}_3")
            s.add("vector", lambda v, bi=bi: v.tensor_copy(xb16[bi][:], x0t[bi][:]),
                  deps=cdep, event=f"oac{gb}_{ti}")
            tdep = [f"oac{gb}_{ti}", "ident"]
            if ti:
                tdep.append(f"oashTe{gb}_{ti-1}")
            elif gp:
                tdep.append(f"oashTe{prevg}_3")
                tdep.append(f"qiT{(gp-1)*4+3}")
            else:
                tdep.append(f"pashTe{f'g{NGRP-1}'}_3")
            for dc in range(8):
                s.add("tensor", lambda tt, bi=bi, dc=dc: tt.transpose(
                    ptr[:, dc * 128:(dc + 1) * 128], xb16[bi][:, dc * 128:(dc + 1) * 128], ident[:]),
                    deps=tdep if dc == 0 else [], event=f"oatp{gb}_{ti}" if dc == 7 else None)
            xdep = [f"oatp{gb}_{ti}"]
            if gp and ti == 0:
                xdep += [f"qmm{prevg}_7", f"qimm{prevg}_3_1"]
            elif gp == 0 and ti == 0:
                xdep += [f"kimm{f'g{NGRP-1}'}", f"vmm{f'g{NGRP-1}'}_3_1"]
            s.add("vector", lambda v, ti=ti: v.tensor_copy(
                x0T[:, :, ti * 128:(ti + 1) * 128], ptr[:, :].rearrange("p (dc q) -> p dc q", q=128)),
                deps=xdep, event=f"oashTe{gb}_{ti}")
            shdone.append(f"oashTe{gb}_{ti}")
        for jc in range(8):
            pk = jc % 2
            qdeps = shdone + ["ld_wq"]
            if last_q_psc[pk]:
                qdeps.append(last_q_psc[pk])
            for dc in range(8):
                s.add("tensor", lambda tt, jc=jc, dc=dc, pk=pk: tt.matmul(
                    psc[pk][:, :], wq_sb[:, dc, jc, :], x0T[:, dc, :],
                    start=(dc == 0), stop=(dc == 7)),
                    deps=qdeps if dc == 0 else [], event=f"qmm{gb}_{jc}" if dc == 7 else None)
            s.add("vector", lambda v, pk=pk: v.tensor_copy(kevac[:], psc[pk][:, :]),
                deps=[f"qmm{gb}_{jc}"] + ([f"qsd{gb}_{jc-1}"] if jc else ([f"qsd{f'og{gp-1}'}_7"] if gp else [f"kst{f'g{NGRP-1}'}_7"])),
                event=f"qev{gb}_{jc}")
            s.add("gpsimd", lambda g, gp=gp, jc=jc: g.dma_start(
                out=qT_d.ap()[jc * 128:(jc + 1) * 128, gp * 512:(gp + 1) * 512], in_=kevac[:]),
                deps=[f"qev{gb}_{jc}"], dma=True, event=f"qsd{gb}_{jc}")
            last_q_psc[pk] = f"qev{gb}_{jc}"
        for ti in range(4):
            ot = gp * 4 + ti
            tb = f"oi{ot}"
            wdeps = [f"oashTe{gb}_{ti}", "ld_ww"] + ([f"wex{ot-1}"] if ot else ["qob", f"kiev{f'g{NGRP-1}'}"])
            for dc in range(8):
                s.add("tensor", lambda tt, ti=ti, dc=dc: tt.matmul(
                    pz[:, 0:4], x0T[:, dc, ti * 128:(ti + 1) * 128], ww_sb[:, dc, :],
                    start=(dc == 0), stop=(dc == 7)),
                    deps=wdeps if dc == 0 else [], event=f"wmm{tb}" if dc == 7 else None)
            s.add("vector", lambda v: v.tensor_reduce(wm[:], pz[:, 0:4], axis=AX.X, op=ALU.max),
                  deps=[f"wmm{tb}"], event=f"wmx{tb}")
            s.add("vector", lambda v: v.tensor_scalar(wm[:], wm[:], -1.0, None, ALU.mult),
                  deps=[f"wmx{tb}"], event=f"wng{tb}")
            s.add("scalar", lambda e: e.activation(wexp[:], pz[:, 0:4], AF.Exp, bias=wm[:]),
                  deps=[f"wng{tb}"], event=f"wep{tb}")
            s.add("vector", lambda v: v.tensor_reduce(wsum[:], wexp[:], axis=AX.X, op=ALU.add),
                  deps=[f"wep{tb}"], event=f"wsm{tb}")
            s.add("vector", lambda v: v.reciprocal(wsum[:], wsum[:]), deps=[f"wsm{tb}"], event=f"wrc{tb}")
            s.add("vector", lambda v: v.tensor_scalar(wrow[:], wexp[:], wsum[:], 0.125, ALU.mult, ALU.mult),
                  deps=[f"wrc{tb}"], event=f"wex{ot}")
            qideps = [f"oashTe{gb}_{ti}", "ld_wqi", f"qmm{gb}_7"] + ([f"qisc{ot-1}"] if ot else [])
            for jh in range(2):
                pk = jh
                for dc in range(8):
                    s.add("tensor", lambda tt, ti=ti, jh=jh, dc=dc, pk=pk: tt.matmul(
                        pctx[2 + pk][:, 0:128], x0T[:, dc, ti * 128:(ti + 1) * 128], wqi_sb[:, dc, jh * 128:(jh + 1) * 128],
                        start=(dc == 0), stop=(dc == 7)),
                        deps=qideps if dc == 0 else [], event=f"qimm{gb}_{ti}_{jh}" if dc == 7 else None)
                s.add("vector", lambda v, jh=jh, pk=pk: v.tensor_copy(
                    qi_tok[:, jh * 128:(jh + 1) * 128], pctx[2 + pk][:, 0:128]),
                    deps=[f"qimm{gb}_{ti}_{jh}"] + ([f"qisc{ot-1}"] if (ot and jh == 0) else []),
                    event=f"qie{tb}_{jh}")
            s.add("vector", lambda v: v.tensor_mul(
                qi_scl[:].rearrange("p (h j) -> p h j", j=64),
                qi_tok[:, :].rearrange("p (h j) -> p h j", j=64),
                wrow[:, :, None].to_broadcast([128, 4, 64])),
                deps=[f"qie{tb}_0", f"qie{tb}_1", f"wex{ot}"], event=f"qisc{ot}")
            for jh in range(2):
                s.add("tensor", lambda tt, jh=jh: tt.transpose(
                    ptr[:, jh * 128:(jh + 1) * 128], qi_scl[:, jh * 128:(jh + 1) * 128], ident[:]),
                    deps=([f"qisc{ot}", f"oatp{gb}_{3}"] + ([f"qiT{ot-1}"] if ot else [])) if jh == 0 else [],
                    event=f"qitp{ot}" if jh == 1 else None)
            s.add("vector", lambda v, ot=ot: v.tensor_copy(
                wqiT[:, :, ot * 128:(ot + 1) * 128], ptr[:, 0:256].rearrange("p (jc q) -> p jc q", q=128)),
                deps=[f"qitp{ot}"], event=f"qiT{ot}")

    last_mtr = None
    for ot in range(TILES_PC):
        tb = f"x{ot}"
        s.add("vector", lambda v, ot=ot: v.tensor_scalar(tilepos[:], rowpos[:], float(ot * 128), qoffc[:, 0:1],
                                                         ALU.add, ALU.add),
              deps=["iorow", "qob"] + ([f"mstore{ot-1}"] if ot else [f"qiT{TILES_PC-1}"]), event=f"tp{tb}")
        for hi in range(4):
            s.add("vector", lambda v, hi=hi, ot=ot: v.tensor_copy(
                istage[:, hi, :], wqiT[(hi % 2) * 64:(hi % 2) * 64 + 64, hi // 2, ot * 128:(ot + 1) * 128]),
                deps=([f"tp{tb}", f"qiT{TILES_PC-1}"] + ([f"imm{tb}_{NSMC-1}_3" if False else f"ic3{f'x{ot-1}'}_{NSMC-1}"] if ot else []))
                     if hi == 0 else [],
                event=f"ist{tb}_{hi}")
        for st in range(NSMC):
            sb2 = f"{tb}_{st}"
            pk = st % 2
            rdeps = [f"qiT{TILES_PC-1}", f"kiev{f'g{NGRP-1}'}", f"tp{tb}"]
            if st >= 1:
                rdeps.append(f"ic3{tb}_{st-1}")
            elif ot:
                rdeps.append(f"ic3{f'x{ot-1}'}_{NSMC-1}")
            else:
                rdeps += [f"vee{f'g{NGRP-1}'}_3_0", f"vee{f'g{NGRP-1}'}_3_1",
                          f"qie{f'oi{TILES_PC-1}'}_0", f"qie{f'oi{TILES_PC-1}'}_1"]
            for hi in range(4):
                s.add("tensor", lambda tt, hi=hi, ot=ot, st=st: tt.matmul(
                    pctx[hi][:, :], istage[:, hi, :],
                    kiT[:, st * 512:(st + 1) * 512], start=True, stop=True),
                    deps=(rdeps + [f"ist{tb}_{3}"]) if hi == 0 else [], event=f"imm{sb2}_{hi}")
            s.add("scalar", lambda e, pk=pk: e.activation(scomb[pk][:], pctx[0][:, :], AF.Relu),
                  deps=[f"imm{sb2}_0"] + ([f"icmb{tb}_{st-2}"] if st >= 2 else []), event=f"ic0{sb2}")
            s.add("vector", lambda v, pk=pk: v.scalar_tensor_tensor(
                scomb[pk][:], pctx[1][:, :], 0.0, scomb[pk][:], ALU.max, ALU.add),
                deps=[f"ic0{sb2}", f"imm{sb2}_1"], event=f"ic1{sb2}")
            s.add("vector", lambda v, pk=pk: v.scalar_tensor_tensor(
                scomb[pk][:], pctx[2][:, :], 0.0, scomb[pk][:], ALU.max, ALU.add),
                deps=[f"ic1{sb2}", f"imm{sb2}_2"], event=f"ic2{sb2}")
            s.add("vector", lambda v, pk=pk: v.scalar_tensor_tensor(
                scomb[pk][:], pctx[3][:, :], 0.0, scomb[pk][:], ALU.max, ALU.add),
                deps=[f"ic2{sb2}", f"imm{sb2}_3"], event=f"ic3{sb2}")
            s.add("vector", lambda v, st=st: v.tensor_scalar(cmask[:], iota512[:], float(st * 512), tilepos[:, 0:1],
                                                             ALU.add, ALU.subtract),
                  deps=["io512", f"tp{tb}", f"ic3{sb2}"], event=f"cm0{sb2}")
            s.add("vector", lambda v: v.tensor_scalar(cmask[:], cmask[:], 0.5, None, ALU.is_ge),
                  deps=[f"cm0{sb2}"], event=f"cm1{sb2}")
            s.add("vector", lambda v, ot=ot, st=st, pk=pk: v.scalar_tensor_tensor(
                idx[:, st * 512:(st + 1) * 512], cmask[:], NEGC, scomb[pk][:], ALU.mult, ALU.add),
                deps=[f"cm1{sb2}"] + ([f"ms{tb}_{st}"] if False else []) +
                     ([f"zap{tb}"] if False else ([] if ot == 0 else ([f"mstore{ot-1}"] if st == 0 else []))),
                event=f"icmb{tb}_{st}")
        prev = f"icmb{tb}_{NSMC-1}"
        for r in range(8):
            s.add("vector", lambda v: v.max(out=m8[:], in_=idx[:, :]),
                  deps=[prev] + ([f"icmb{tb}_{st2}" for st2 in range(NSMC)] if r == 0 else []),
                  event=f"mx{tb}_{r}")
            s.add("vector", lambda v: v.match_replace(out=idx[:, :], in_to_replace=m8[:], in_values=idx[:, :],
                                                      imm_value=NEGZ),
                  deps=[f"mx{tb}_{r}"], event=f"mr{tb}_{r}")
            prev = f"mr{tb}_{r}"
        for st in range(NSMC):
            sb2 = f"{tb}_{st}"
            s.add("vector", lambda v, st=st: v.tensor_scalar(cmask[:], iota512[:], float(st * 512), tilepos[:, 0:1],
                                                             ALU.add, ALU.subtract),
                  deps=[prev] + ([f"mtre{tb}_{st-1}"] if st else []), event=f"cb0{sb2}")
            s.add("vector", lambda v: v.tensor_scalar(cmask[:], cmask[:], 0.5, None, ALU.is_lt),
                  deps=[f"cb0{sb2}"], event=f"cb1{sb2}")
            s.add("vector", lambda v, st=st: v.tensor_scalar(scomb[0][:], idx[:, st * 512:(st + 1) * 512],
                                                             NEGZ, None, ALU.is_equal),
                  deps=[f"cb1{sb2}"], event=f"cb2{sb2}")
            s.add("vector", lambda v, st=st: v.tensor_mul(msb[:], scomb[0][:], cmask[:]),
                  deps=[f"cb2{sb2}"], event=f"msc{tb}_{st}")
            tdeps2 = [f"msc{tb}_{st}"]
            if st:
                tdeps2.append(f"mtre{tb}_{st-1}")
            elif ot:
                tdeps2.append(f"mtre{f'x{ot-1}'}_{NSMC-1}")
            else:
                tdeps2.append(f"qiT{TILES_PC-1}")
            for b4 in range(4):
                s.add("tensor", lambda tt, b4=b4: tt.transpose(
                    ptr[:, b4 * 128:(b4 + 1) * 128], msb[:, b4 * 128:(b4 + 1) * 128], ident[:]),
                    deps=tdeps2 if b4 == 0 else [], event=f"mtp{sb2}" if b4 == 3 else None)
            s.add("vector", lambda v: v.tensor_copy(mtr[:, :, :], ptr[:, 0:512].rearrange("p (a q) -> p a q", q=128)),
                  deps=[f"mtp{sb2}"] + ([f"mst{tb}_{st-1}"] if st else ([f"mst{f'x{ot-1}'}_{NSMC-1}"] if ot else [])),
                  event=f"mtre{tb}_{st}")
            s.add("gpsimd", lambda g, ot=ot, st=st: g.dma_start(
                out=mT_d.ap()[st * 512:(st + 1) * 512, ot * 128:(ot + 1) * 128].rearrange("(a p) q -> p a q", p=128),
                in_=mtr[:, :, :]),
                deps=[f"mtre{tb}_{st}"], dma=True, event=f"mst{tb}_{st}")
        last_mtr = f"mst{tb}_{NSMC-1}"
        s.add("vector", lambda v: v.engine_nop(), deps=[f"mtre{tb}_{NSMC-1}"], event=f"mstore{ot}")

    last_psc = [last_q_psc[0], last_q_psc[1], "qob"]
    psc3 = [psc[0], psc[1], pz]
    last_E_reader = [None] * 4
    last_P_reader = [None] * 4
    CTX_LAG = 3
    pend_a = []
    last_cmm_a = [None]

    def _emit_actx(item):
        qb2, smc2, sc2, i2, h2, bi2, pq2 = item
        first = (smc2 == 0 and sc2 == 0)
        last = (smc2 == NSMC - 1 and sc2 == 3)
        cdeps = [f"amul{qb2}_{smc2}_{sc2}_{i2}"]
        if first and last_ctx_evac[i2]:
            cdeps.append(last_ctx_evac[i2])
        s.add("tensor", lambda tt, i=i2, h=h2, sc=sc2, bi=bi2, pq=pq2, first=first, last=last: tt.matmul(
            pctx[i][0:65, :], vbuf[bi][:, sc, h * 65:(h + 1) * 65], Pbuf[pq][:, :],
            start=first, stop=last),
            deps=cdeps, event=f"acmm{qb2}_{smc2}_{sc2}_{i2}")
        last_P_reader[pq2] = f"acmm{qb2}_{smc2}_{sc2}_{i2}"
        last_cmm_a[0] = f"acmm{qb2}_{smc2}_{sc2}_{i2}"
    last_ctx_evac = [None] * HPG
    last_ptr = f"mtre{f'x{TILES_PC-1}'}_{NSMC-1}"
    last_attnT_reader = None
    ev_store = []
    for qt in range(NQT):
        ytiles_done = []
        for hg in range(NHG):
            heads = [hg * HPG + i for i in range(HPG)]
            qb = f"a{qt}h{hg}"
            qsdeps = [f"qsd{f'og{TPC // 512 - 1}'}_7"]
            if (qt, hg) != (0, 0):
                qsdeps.append(f"asmm{prev_qb}_{NSMC-1}_{3}_{HPG-1}")
            for i, h in enumerate(heads):
                s.add("sync", lambda e, h=h, i=i, qt=qt: e.dma_start(
                    out=qstage[:, i, :], in_=qT_d.ap()[h * 64:(h + 1) * 64, qt * 512:(qt + 1) * 512]),
                    deps=qsdeps if i == 0 else [], dma=True, event=f"qst{qb}_{i}")
            for smc in range(NSMC):
                bi = smc % 3
                kdeps = [f"kst{f'g{NGRP-1}'}_7"]
                if smc >= 2:
                    kdeps.append(f"aiter{qb}_{smc-2}")
                elif (qt, hg) != (0, 0):
                    kdeps.append(f"aiter{prev_qb}_{NSMC-2+smc}")
                for i, h in enumerate(heads):
                    s.add("sync", lambda e, h=h, i=i, smc=smc, bi=bi: e.dma_start(
                        out=kbuf[bi][:, i, :], in_=kT_d.ap()[h * 64:(h + 1) * 64, smc * 512:(smc + 1) * 512]),
                        deps=kdeps if i == 0 else [], dma=True, event=f"akl{qb}_{smc}_{i}")
                s.add("sync", lambda e, smc=smc, bi=bi: e.dma_start(
                    out=vbuf[bi][:, :, :],
                    in_=vpad_d.ap()[smc * 512:(smc + 1) * 512, :].rearrange("(a p) c -> p a c", p=128)),
                    deps=[f"vst{f'g{NGRP-1}'}_3"] + kdeps[1:2], dma=True, event=f"avl{qb}_{smc}")
                s.add("sync", lambda e, smc=smc, bi=bi, qt=qt: e.dma_start(
                    out=mbuf[bi][:, :, :],
                    in_=mT_d.ap()[smc * 512:(smc + 1) * 512, qt * 512:(qt + 1) * 512].rearrange("(a p) c -> p a c", p=128)),
                    deps=[last_mtr] + kdeps[1:2], dma=True, event=f"aml{qb}_{smc}")
                iter_last = None
                for sc in range(4):
                    for i, h in enumerate(heads):
                        it = smc * 4 * HPG + sc * HPG + i
                        pp = it % 3
                        pq = it % 4
                        mdeps = [f"akl{qb}_{smc}_{HPG-1}", f"avl{qb}_{smc}", f"aml{qb}_{smc}"]
                        if last_psc[pp]:
                            mdeps.append(last_psc[pp])
                        s.add("tensor", lambda tt, i=i, sc=sc, bi=bi, pp=pp: tt.matmul(
                            psc3[pp][:, :], kbuf[bi][:, i, sc * 128:(sc + 1) * 128],
                            qstage[:, i, :], start=True, stop=True),
                            deps=mdeps + [f"qst{qb}_{3}"], event=f"asmm{qb}_{smc}_{sc}_{i}")
                        edeps = [f"asmm{qb}_{smc}_{sc}_{i}"]
                        if last_E_reader[pq]:
                            edeps.append(last_E_reader[pq])
                        s.add("scalar", lambda e, pp=pp, pq=pq: e.activation(Ebuf[pq][:], psc3[pp][:, :], AF.Exp, scale=0.125),
                              deps=edeps, event=f"aexp{qb}_{smc}_{sc}_{i}")
                        last_psc[pp] = f"aexp{qb}_{smc}_{sc}_{i}"
                        pdeps = [f"aexp{qb}_{smc}_{sc}_{i}"]
                        if last_P_reader[pq]:
                            pdeps.append(last_P_reader[pq])
                        s.add("vector", lambda v, pq=pq, sc=sc, bi=bi: v.tensor_mul(
                            Pbuf[pq][:], Ebuf[pq][:], mbuf[bi][:, sc, :]),
                            deps=pdeps, event=f"amul{qb}_{smc}_{sc}_{i}")
                        last_E_reader[pq] = f"amul{qb}_{smc}_{sc}_{i}"
                        pend_a.append((qb, smc, sc, i, h, bi, pq))
                        if len(pend_a) > CTX_LAG:
                            _emit_actx(pend_a.pop(0))
                if smc == NSMC - 1:
                    while pend_a:
                        _emit_actx(pend_a.pop(0))
                s.add("vector", lambda v: v.engine_nop(),
                      deps=[f"aexp{qb}_{smc}_{3}_{HPG-1}", f"amul{qb}_{smc}_{3}_{HPG-1}", last_cmm_a[0]],
                      event=f"aiter{qb}_{smc}")
            prev_qb = qb
            for i, h in enumerate(heads):
                edeps = [f"acmm{qb}_{NSMC-1}_{3}_{i}"] + ([f"anrm{qb}_{i-1}"] if i else [])
                s.add("vector", lambda v, i=i: v.tensor_copy(ctxe[:, :], pctx[i][0:65, :]),
                      deps=edeps, event=f"ace{qb}_{i}")
                last_ctx_evac[i] = f"ace{qb}_{i}"
                s.add("vector", lambda v: v.reciprocal(zrow[0:1, 0:512], ctxe[64:65, :]),
                      deps=[f"ace{qb}_{i}"], event=f"azr{qb}_{i}")
                s.add("tensor", lambda tt: tt.matmul(psc[0][0:64, :], onesrowF[0:1, 0:64], zrow[0:1, 0:512],
                                                     start=True, stop=True),
                      deps=[f"azr{qb}_{i}", "onesF", last_psc[0]] + ([f"aznrm{qb}_{i-1}"] if i else
                           ([f"aznrm{prev2_qb}_{HPG-1}"] if (qt, hg) != (0, 0) else ["qob_mm"])),
                      event=f"azb{qb}_{i}")
                s.add("vector", lambda v: v.tensor_copy(zbcs[0:64, :], psc[0][0:64, :]),
                      deps=[f"azb{qb}_{i}"], event=f"azbe{qb}_{i}")
                last_psc[0] = f"azbe{qb}_{i}"
                s.add("vector", lambda v, i=i, h=h: v.tensor_mul(
                    attnT[(h % 2) * 64:(h % 2) * 64 + 64, h // 2, :], ctxe[0:64, :], zbcs[0:64, :]),
                    deps=[f"azbe{qb}_{i}"] + ([last_attnT_reader] if (last_attnT_reader and hg == 0 and i == 0) else []),
                    event=f"anrm{qb}_{i}")
                s.add("vector", lambda v: v.engine_nop(), deps=[f"anrm{qb}_{i}"], event=f"aznrm{qb}_{i}")
            prev2_qb = qb
        attn_done = [f"anrm{f'a{qt}h{hg}'}_{i}" for hg in range(NHG) for i in range(HPG)]
        for jc in range(8):
            pk = jc % 2
            odeps = attn_done + ["ld_wo"]
            if last_psc[pk]:
                odeps.append(last_psc[pk])
            for dc in range(8):
                s.add("tensor", lambda tt, jc=jc, dc=dc, pk=pk: tt.matmul(
                    psc[pk][:, :], wo_sb[:, dc, jc, :], attnT[:, dc, :],
                    start=(dc == 0), stop=(dc == 7)),
                    deps=odeps if dc == 0 else [], event=f"ymm{qt}_{jc}" if dc == 7 else None)
            s.add("vector", lambda v, jc=jc, pk=pk: v.tensor_scalar(
                yT_sb[:, jc, :], psc[pk][:, :], bo_col[:, jc, 0:1], None, ALU.add),
                deps=[f"ymm{qt}_{jc}", "cols_bo"], event=f"yev{qt}_{jc}")
            last_psc[pk] = f"yev{qt}_{jc}"
        for ti in range(4):
            t0 = qt * 4 + ti
            for dc in range(8):
                tdeps = [f"yev{qt}_{dc}"] if dc == 0 else []
                if dc == 0:
                    tdeps = [f"yev{qt}_{jc}" for jc in range(8)] + [last_ptr] + ([f"ytoke{t0-1}"] if ti else [])
                s.add("tensor", lambda tt, ti=ti, dc=dc: tt.transpose(
                    ptr[:, dc * 128:(dc + 1) * 128], yT_sb[:, dc, ti * 128:(ti + 1) * 128], ident[:]),
                    deps=tdeps, event=f"ytp{t0}" if dc == 7 else None)
            s.add("vector", lambda v: v.tensor_copy(ytok[:], ptr[:, :]),
                  deps=[f"ytp{t0}"] + ([f"_lns2_a{t0-1}"] if t0 else []), event=f"ytoke{t0}")
            last_ptr = f"ytoke{t0}"
            s.add("scalar", lambda e, t0=t0: e.dma_start(out=x0t[0][:], in_=x0o[t0 * 128:(t0 + 1) * 128, :]),
                  deps=[f"arf{t0-1}"] if t0 else [f"oal{f'og{TPC // 512 - 1}'}_3"], dma=True, event=f"arl{t0}")
            s.add("vector", lambda v: v.tensor_add(rfin, ytok[:], x0t[0][:]),
                  deps=[f"ytoke{t0}", f"arl{t0}"] +
                       ([f"msc{f'x{TILES_PC-1}'}_{NSMC-1}"] if t0 == 0 else []),
                  event=f"arf{t0}")
            emit_layernorm(s, f"a{t0}", rfin, outt, lnsc,
                           deps=[f"arf{t0}"] + ([f"st_a{t0-1}"] if t0 else []),
                           store_fn=lambda e, t0=t0: e.dma_start(out=x1[t0 * 128:(t0 + 1) * 128, :], in_=outt),
                           store_dep_event=f"st_a{t0-1}" if t0 else None)
            ev_store.append(f"st_a{t0}")
        last_attnT_reader = f"ymm{qt}_7"

    if ctx is None:
        s.finish(*ev_store)
        s.emit(block)
        return p
    return ev_store


def _phase_a_in_maps(inputs):
    x0full = np.ascontiguousarray(np.asarray(inputs["L0_attn_x"], np.float32).reshape(NTOK, D))
    maps = []
    for c in range(NCORES):
        b = c // 4
        m = dict(
            x0batch=x0full[b * T:(b + 1) * T],
            x0own=x0full[c * TPC:(c + 1) * TPC],
            qoff=np.full((1, 1), float((c % 4) * TPC), np.float32),
            Wq=inputs["L0_attn_Wq"], Wk=inputs["L0_attn_Wk"], Wv=inputs["L0_attn_Wv"],
            Wo=inputs["L0_attn_Wo"], bo=inputs["L0_attn_bo"], Wqi=inputs["L0_attn_Wqi"],
            Wki=inputs["L0_attn_Wki"], Ww=inputs["L0_attn_Ww"],
        )
        maps.append({k2: np.ascontiguousarray(v, np.float32) for k2, v in m.items()})
    return maps


def run_phase_a(inputs):
    p = build_phase_a()
    maps = _phase_a_in_maps(inputs)
    res = run_bass_kernel_spmd(p.nc, maps, core_ids=list(range(NCORES)))
    return np.concatenate([res.results[c]["x1s"] for c in range(NCORES)], axis=0)



def _phase_d_in_maps(inputs):
    x3full = np.ascontiguousarray(inputs["x3"], dtype=np.float32)
    maps = []
    for c in range(NCORES):
        m = dict(
            x3s=x3full[c * TPC:(c + 1) * TPC],
            gate_W=inputs["L1_ffn_gate_W"],
            W1=inputs["L1_ffn_W1"],
            b1=inputs["L1_ffn_b1"],
            W2=inputs["L1_ffn_W2"],
            b2=inputs["L1_ffn_b2"],
            Ws1=inputs["L1_ffn_Ws1"],
            bs1=inputs["L1_ffn_bs1"],
            Ws2=inputs["L1_ffn_Ws2"],
            bs2=inputs["L1_ffn_bs2"],
            route_scale=np.asarray(inputs["L1_ffn_route_scale"], np.float32).reshape(1, 1),
        )
        maps.append({k: np.ascontiguousarray(v, np.float32) for k, v in m.items()})
    return maps


def run_phase_d(inputs, n_cores=NCORES):
    p = build_phase_d()
    maps = _phase_d_in_maps(inputs)[:n_cores]
    res = run_bass_kernel_spmd(p.nc, maps, core_ids=list(range(n_cores)))
    out = np.concatenate([res.results[c]["x4s"] for c in range(n_cores)], axis=0)
    return out


def build_fused():
    p = Prog()
    nc = p.nc

    x0b = nc.dram_tensor("x0batch", [T, D], F32, kind="ExternalInput").ap()
    x0o = nc.dram_tensor("x0own", [TPC, D], F32, kind="ExternalInput").ap()
    qoff = nc.dram_tensor("qoff", [1, 1], F32, kind="ExternalInput").ap()
    prevmask = nc.dram_tensor("prevmask", [1, 8], F32, kind="ExternalInput").ap()
    WA = {}
    for nm, shp in [("Wq", [D, D]), ("Wk", [D, D]), ("Wv", [D, D]), ("Wo", [D, D]),
                    ("bo", [D]), ("Wqi", [D, 256]), ("Wki", [D, 64]), ("Ww", [D, 4])]:
        WA[nm] = nc.dram_tensor(nm, shp, F32, kind="ExternalInput").ap()
    WB = {}
    for nm, shp in [("Wr", [D, 8]), ("norm_w", [8, D]), ("Wg", [8, D, 2048]),
                    ("Wu", [8, D, 2048]), ("Wd", [8, 2048, D]), ("f_norm", [D]),
                    ("fWg", [D, 512]), ("fWu", [D, 512]), ("fWd", [512, D])]:
        WB[nm] = nc.dram_tensor(nm, shp, F32, kind="ExternalInput").ap()
    WD = {}
    for nm, shp in [("gate_W", [D, 8]), ("W1", [8, D, 512]), ("b1", [8, 512]),
                    ("W2", [8, 512, D]), ("b2", [8, D]), ("Ws1", [D, 512]),
                    ("bs1", [512]), ("Ws2", [512, D]), ("bs2", [D]),
                    ("route_scale", [1, 1])]:
        WD[nm] = nc.dram_tensor(nm, shp, F32, kind="ExternalInput").ap()
    x4 = nc.dram_tensor("x4s", [TPC, D], F32, kind="ExternalOutput").ap()

    x1_d = nc.dram_tensor("x1_d", [TPC, D], F32)
    x2_d = nc.dram_tensor("x2_d", [TPC, D], F32)
    x2b_d = nc.dram_tensor("x2b_d", [T, D], F32)
    x3_d = nc.dram_tensor("x3_d", [TPC, D], F32)

    s = Sched(p)
    block = p.es.enter_context(nc.Block())

    pa = PProg(p, "A_")
    sa = PSched(s, "A_")
    build_phase_a(ctx=dict(p=pa, s=sa, pfx="A_", x0b=x0b, x0o=x0o, qoff=qoff,
                           x1=x1_d.ap(), **WA))
    pa.close()
    fA = s.phase_fence("A")

    pb = PProg(p, "B_")
    sb = PSched(s, "B_", barrier_ev=fA)
    build_phase_b(ctx=dict(p=pb, s=sb, pfx="B_", x1=x1_d.ap(), x2=x2_d.ap(),
                           prevmask=prevmask, **WB))
    pb.close()
    fB = s.phase_fence("B")

    s.add("gpsimd", lambda g: g.collective_compute(
        "AllGather", ALU.bypass, replica_groups=[[0, 1, 2, 3], [4, 5, 6, 7]],
        ins=[x2_d.ap().opt()], outs=[x2b_d.ap().opt()]),
        deps=[fB], event="agx2")

    pc = PProg(p, "C_")
    sc = PSched(s, "C_", barrier_ev="agx2")
    build_phase_c(ctx=dict(p=pc, s=sc, pfx="C_", x2b=x2b_d.ap(), x2o=x2_d.ap(),
                           x3=x3_d.ap()))
    pc.close()
    fC = s.phase_fence("C")

    pd = PProg(p, "D_")
    sd = PSched(s, "D_", barrier_ev=fC)
    build_phase_d(ctx=dict(p=pd, s=sd, pfx="D_", x3=x3_d.ap(), x4=x4, **WD))
    pd.close()
    s.phase_fence("final")
    s.emit(block)
    return p


_FUSED_CACHE = {}


def _fused_in_maps(inputs):
    x0full = np.ascontiguousarray(np.asarray(inputs["L0_attn_x"], np.float32).reshape(NTOK, D))
    maps = []
    for c in range(NCORES):
        b = c // 4
        m = dict(
            x0batch=x0full[b * T:(b + 1) * T],
            x0own=x0full[c * TPC:(c + 1) * TPC],
            qoff=np.full((1, 1), float((c % 4) * TPC), np.float32),
            prevmask=(np.arange(8) < c).astype(np.float32).reshape(1, 8),
            Wq=inputs["L0_attn_Wq"], Wk=inputs["L0_attn_Wk"], Wv=inputs["L0_attn_Wv"],
            Wo=inputs["L0_attn_Wo"], bo=inputs["L0_attn_bo"], Wqi=inputs["L0_attn_Wqi"],
            Wki=inputs["L0_attn_Wki"], Ww=inputs["L0_attn_Ww"],
            Wr=inputs["L0_ffn_Wr"], norm_w=inputs["L0_ffn_norm_w"],
            Wg=inputs["L0_ffn_Wg"], Wu=inputs["L0_ffn_Wu"], Wd=inputs["L0_ffn_Wd"],
            f_norm=inputs["L0_ffn_f_norm"], fWg=inputs["L0_ffn_fWg"],
            fWu=inputs["L0_ffn_fWu"], fWd=inputs["L0_ffn_fWd"],
            gate_W=inputs["L1_ffn_gate_W"], W1=inputs["L1_ffn_W1"], b1=inputs["L1_ffn_b1"],
            W2=inputs["L1_ffn_W2"], b2=inputs["L1_ffn_b2"], Ws1=inputs["L1_ffn_Ws1"],
            bs1=inputs["L1_ffn_bs1"], Ws2=inputs["L1_ffn_Ws2"], bs2=inputs["L1_ffn_bs2"],
            route_scale=np.asarray(inputs["L1_ffn_route_scale"], np.float32).reshape(1, 1),
        )
        maps.append({k2: np.ascontiguousarray(v, np.float32) for k2, v in m.items()})
    return maps


def kernel(**inputs):
    p = build_fused()
    maps = _fused_in_maps(inputs)
    res = run_bass_kernel_spmd(p.nc, maps, core_ids=list(range(NCORES)))
    out = np.concatenate([res.results[c]["x4s"] for c in range(NCORES)], axis=0)
    return out.reshape(B, T, D)


def kernel_unfused(**inputs):
    x1 = run_phase_a(inputs)
    x2 = run_phase_b(dict(x1=x1, **inputs))
    x3 = run_phase_c(dict(x2=x2))
    x4 = run_phase_d(dict(x3=x3, **inputs))
    return x4.reshape(B, T, D)
